# version 07
# baseline (speedup 1.0000x reference)
import contextlib
import os
import numpy as np
import concourse.bass as bass
import concourse.mybir as mybir
from concourse.bass_utils import run_bass_kernel_spmd

F32 = mybir.dt.float32
BF16 = mybir.dt.bfloat16
U8 = mybir.dt.uint8
AF = mybir.ActivationFunctionType
ALU = mybir.AluOpType
AX = mybir.AxisListType

D = 1024
DI = 2048
TL = 4096
HP = 8
EPS = 1e-5
TCA = 1024
NCA = 4
TCB = 2048
NCB = 8
TCC = 1024
NCC = 4
WA = HP + TCA

RG = [[0, 1, 2, 3], [4, 5, 6, 7]]
DBG = bool(int(os.environ.get("MK_DBG", "0")))
NOCACHE = bool(int(os.environ.get("MK_NOCACHE", "0")))


class Sched:
    ENGINES = ("sync", "vector", "scalar", "tensor", "gpsimd")

    def __init__(self):
        self.steps = {e: [] for e in self.ENGINES}
        self.count = {}

    def emit(self, eng, fn, waits=(), inc=None, inc_n=1):
        after = None
        if inc is not None:
            self.count[inc] = self.count.get(inc, 0) + inc_n
            after = self.count[inc]
        self.steps[eng].append((fn, [w for w in waits if w is not None],
                                (inc, inc_n) if inc is not None else None))
        return after

    def val(self, sem):
        return self.count.get(sem, 0)

    def replay(self, eng_name, eng):
        for fn, waits, inc in self.steps[eng_name]:
            for (s_, v_) in waits:
                eng.wait_ge(s_, v_)
            r = fn(eng) if fn is not None else None
            if inc is not None:
                assert r is not None
                r.then_inc(inc[0], inc[1])


class Overlay:
    def __init__(self, nc, stack, name, nbytes):
        self.nc = nc
        self.back = stack.enter_context(nc.sbuf_tensor(name, [128, nbytes], U8))
        self.base = nc.lookup_mloc(self.back).addr

    def view(self, name, shape, dtype, off_bytes):
        h = self.nc.virtual_sbuf_tensor(name, list(shape), dtype)
        self.nc._tensor(h.name, h.shape, h.dtype, type="SB")
        mloc = self.nc.lookup_mloc(h)
        mloc.allocated = True
        mloc.addr = self.base + off_bytes
        return h.ap()


def build_program():
    nc = bass.Bass("TRN2", target_bir_lowering=False)

    def din(name, shape, dt=F32):
        return nc.dram_tensor(name, shape, dt, kind="ExternalInput").ap()

    x_in = din("x_fm", [D, HP + TL])
    w_sg = din("w_sg", [16, 128, 8 * 128])
    w_su = din("w_su", [16, 128, 8 * 128])
    w_sd = din("w_sd", [8, 128, 16 * 128])
    w_gu = din("w_gu", [8, 8, 128, 8 * 128])
    w_dn = din("w_dn", [8, 8, 128, 4 * 128])
    w_r9 = din("w_r9", [8, 128, 9])
    w_in = din("w_in", [16, 128, 8 * 128])
    w_gt = din("w_gt", [16, 128, 8 * 128])
    w_dl = din("w_dl", [16, 128, 16 * 128])
    w_bc = din("w_bc", [16, 128, 32])
    w_ot = din("w_ot", [8, 128, 16 * 128])
    w_cv = din("w_cv", [16, 128, 4 * 128])
    b_in = din("b_in", [128, 16])
    b_gt = din("b_gt", [128, 16])
    b_dl = din("b_dl", [128, 16])
    b_cv = din("b_cv", [128, 16])
    b_ot = din("b_ot", [128, 8])
    a_own = din("a_own", [128, 64])
    d_skip = din("d_skip", [128, 16])
    mask_h = din("mask_h", [128, HP])
    ident = din("ident", [128, 128])
    selbc = din("selbc", [32, 8 * 128])

    out_t = nc.dram_tensor("out_fm", [D, TL], F32, kind="ExternalOutput").ap()

    def dint(name, shape, dt=BF16):
        return nc.dram_tensor(name, shape, dt, kind="Internal").ap()

    c_sg = dint("c_sg", [16, 128, 8 * 128])
    c_su = dint("c_su", [16, 128, 8 * 128])
    c_sd = dint("c_sd", [8, 128, 16 * 128])
    c_gu = dint("c_gu", [8, 8, 128, 8 * 128])
    c_dn = dint("c_dn", [8, 8, 128, 4 * 128])
    c_in = dint("c_in", [16, 128, 8 * 128])
    c_gt = dint("c_gt", [16, 128, 8 * 128])
    c_dl = dint("c_dl", [16, 128, 16 * 128])
    c_ot = dint("c_ot", [8, 128, 16 * 128])
    c_cv = dint("c_cv", [16, 128, 4 * 128])
    x1_sp = dint("x1_sp", [D, TL])
    g_sp = dint("g_sp", [DI, TL])
    act_sp = dint("act_sp", [DI, TL])
    x2_sp = dint("x2_sp", [D, TL])
    cc_di = [dint(f"cc_di{c}", [DI, TCA]) for c in range(NCA)]
    cc_do = [dint(f"cc_do{c}", [4, DI, TCA]) for c in range(NCA)]
    cc_ui = [dint(f"cc_ui{c}", [DI, TCA]) for c in range(NCA)]
    cc_uo = [dint(f"cc_uo{c}", [4, DI, TCA]) for c in range(NCA)]
    cc_bi = [dint(f"cc_bi{c}", [32, TCA]) for c in range(NCA)]
    cc_bo = [dint(f"cc_bo{c}", [4, 32, TCA]) for c in range(NCA)]
    cc_yp = [[dint(f"cc_yp{h}_{q}", [4, 512, TCB], F32) for q in range(4)]
             for h in range(2)]
    cc_yr = [[dint(f"cc_yr{h}_{q}", [512, TCB], F32) for q in range(4)]
             for h in range(2)]

    dbg_out = {}
    if DBG:
        for nm, shp in (("x1", [D, TL]), ("act", [DI, TL]), ("gact", [DI, TL]),
                        ("delta", [DI, TCA]), ("u", [DI, TCA]),
                        ("bc", [32, TCA]), ("y", [DI, TCB]), ("x2", [D, TL]),
                        ("xln", [128, 8 * WA]), ("cwt", [1, 9 * WA]),
                        ("sact", [128, 16 * WA]), ("acts", [128, 32 * WA]),
                        ("yp", [4 * DI, TCB]), ("do0", [4 * DI, TCA]), ("uo0", [4 * DI, TCA]), ("brep", [128, 4 * TCB]), ("crep", [128, 4 * TCB]), ("b0", [128, TCB]), ("h0", [128, TCB]), ("y0", [128, TCB])):
            dbg_out[nm] = nc.dram_tensor("dbg_" + nm, shp, BF16,
                                         kind="ExternalOutput").ap()
        for nm, shp in (("cf", [128, 32]), ("lg", [128, 9]),
                        ("asc", [128, 64]), ("hl", [128, 64]),
                        ("a0f", [128, TCB])):
            dbg_out[nm] = nc.dram_tensor("dbg_" + nm, shp, F32,
                                         kind="ExternalOutput").ap()

    with contextlib.ExitStack() as stk:
        def sball(name, shape, dt):
            return stk.enter_context(nc.sbuf_tensor(name, shape, dt))[:]

        wbufA = sball("wbufA", [128, 16 * 128], BF16)
        wbufB = sball("wbufB", [128, 16 * 128], BF16)
        wf32 = sball("wf32", [128, 16 * 128], F32)
        wf32B = sball("wf32B", [128, 16 * 128], F32)
        wr9f = sball("wr9f", [128, 72], F32)
        wr9 = sball("wr9", [128, 72], BF16)
        wbc_b = sball("wbc_b", [128, 512], BF16)
        sgs = sball("sgs", [128, WA], BF16)
        statb = sball("statb", [1, WA], BF16)
        cw_f = sball("cw_f", [128, 32], F32)
        identb = sball("identb", [128, 128], F32)
        maskb = sball("maskb", [128, HP], F32)
        biasq = sball("biasq", [128, 80], F32)
        dskq = sball("dskq", [128, 16], F32)
        a_sc = sball("a_sc", [128, 64], F32)
        h_last = sball("h_last", [128, 64], F32)
        onescol = sball("onescol", [128, 2], BF16)
        onesrow = sball("onesrow", [1, 128], BF16)
        selb = sball("selb", [32, 8 * 128], BF16)
        bct = sball("bct", [32, TCA], BF16)
        dlt = sball("dlt", [128, TCA], BF16)
        ut = sball("ut", [128, TCA], BF16)

        OV = Overlay(nc, stk, "ov", 167168)
        x_c = OV.view("x_c", [128, 8 * WA], BF16, 0)
        xln = OV.view("xln", [128, 8 * WA], BF16, 16512)
        cwrep = OV.view("cwrep", [128, 9 * WA], BF16, 33024)
        mrep = OV.view("mrep", [128, WA], BF16, 51600)
        rrep = OV.view("rrep", [128, WA], BF16, 53664)
        xsq = OV.view("xsq", [128, WA], BF16, 55728)
        e_sc = OV.view("e_sc", [128, 512], F32, 57792)
        acts = OV.view("acts", [128, 32 * WA], BF16, 63984)
        sact = OV.view("sact", [128, 16 * WA], BF16, 130032)
        xz = OV.view("xz", [128, 16 * WA], BF16, 63984)
        actm = OV.view("actm", [128, 16 * TCA], BF16, 97008)
        stat = OV.view("stat", [1, 4 * WA], F32, 97008)
        xf32 = OV.view("xf32", [128, 8 * 258], F32, 113520)
        cwT = OV.view("cwT", [1, 9 * WA], BF16, 144480)
        of32 = OV.view("of32", [128, 1024], F32, 163056)
        dl_g = OV.view("dl_g", [128, 2 * TCB], BF16, 0)
        u_g = OV.view("u_g", [128, 2 * TCB], BF16, 8192)
        bc_g = OV.view("bc_g", [32, TCB], BF16, 16384)
        bco = OV.view("bco", [8, TCB], BF16, 20480)
        brep = OV.view("brep", [128, 4 * TCB], BF16, 24576)
        crep = OV.view("crep", [128, 4 * TCB], BF16, 40960)
        brepB = OV.view("brepB", [128, 4 * TCB], BF16, 106496)
        crepB = OV.view("crepB", [128, 4 * TCB], BF16, 122880)
        brepb = [brep, brepB]
        crepb = [crep, crepB]
        a_t = OV.view("a_t", [128, 2 * TCB], F32, 57344)
        b_t = OV.view("b_t", [128, 2 * TCB], BF16, 73728)
        h_t = OV.view("h_t", [128, TCB], BF16, 81920)
        p_t = OV.view("p_t", [128, TCB], BF16, 86016)
        ph_t = OV.view("ph_t", [128, TCB], BF16, 90112)
        s1_t = OV.view("s1_t", [128, TCB], BF16, 94208)
        y_t = OV.view("y_t", [128, TCB], F32, 98304)
        yc = OV.view("yc", [128, 16 * TCC], F32, 0)
        actc = OV.view("actc", [128, 16 * TCC], BF16, 65536)
        ycb = OV.view("ycb", [128, 16 * TCC], BF16, 65536)
        gc = OV.view("gc", [128, 16 * TCC], BF16, 98304)
        x1c = OV.view("x1c", [128, 8 * TCC], BF16, 131072)
        x2st = OV.view("x2st", [128, 1024], BF16, 147456)

        sDW = stk.enter_context(nc.semaphore("sDW"))
        sDW2 = stk.enter_context(nc.semaphore("sDW2"))
        sB0 = stk.enter_context(nc.semaphore("sB0"))
        sB1 = stk.enter_context(nc.semaphore("sB1"))
        sBC = stk.enter_context(nc.semaphore("sBC"))
        sP = stk.enter_context(nc.semaphore("sP"))
        sDG = stk.enter_context(nc.semaphore("sDG"))
        sDS = stk.enter_context(nc.semaphore("sDS"))
        sV = stk.enter_context(nc.semaphore("sV"))
        sS = stk.enter_context(nc.semaphore("sS"))
        sT = stk.enter_context(nc.semaphore("sT"))
        sG = stk.enter_context(nc.semaphore("sG"))
        sFin = stk.enter_context(nc.semaphore("sFin"))

        PSB = [stk.enter_context(nc.psum_tensor(f"pb{i}", [128, 512], F32))[:]
               for i in range(8)]

        blk = stk.enter_context(nc.Block())
        S = Sched()

        def mm(psum, lhsT, rhs, start, stop, waits=()):
            return S.emit("tensor",
                          lambda e, a=psum, b=lhsT, c=rhs, s1=start, s2=stop:
                          e.matmul(a, b, c, start=s1, stop=s2),
                          waits=waits, inc=sT)

        bank_free = {i: [] for i in range(8)}
        bank_rr = [0]

        def take_bank():
            i = (bank_rr[0] + 1) % 8
            bank_rr[0] = i
            return i, list(bank_free[i])

        wslot = [0]
        wslot_free = [[], []]

        wf32_free = [[], []]
        cache_bar = [[]]

        def load_wtile(src_f32, src_bf, kt, cache_write):
            sl = wslot[0]
            wslot[0] ^= 1
            buf = wbufA if sl == 0 else wbufB
            waits = list(wslot_free[sl])
            sem = sDW if sl == 0 else sDW2
            eng = "sync" if sl == 0 else "gpsimd"
            if src_f32 is not None:
                stg = wf32 if sl == 0 else wf32B
                v = S.emit(eng, lambda e, s=src_f32, k=kt, g=stg: e.dma_start(
                    out=g[:, 0:k * 128], in_=s),
                    waits=waits + wf32_free[sl], inc=sem, inc_n=16)
                v2 = S.emit("vector", lambda e, d=buf, k=kt, g=stg: e.tensor_copy(
                    d[:, 0:k * 128], g[:, 0:k * 128]),
                    waits=[(sem, v)], inc=sV)
                ready = (sV, v2)
                wf32_free[sl] = [(sV, v2)]
                if cache_write is not None and not NOCACHE:
                    vcw = S.emit("scalar", lambda e, d=cache_write, s=buf, k=kt:
                                 e.dma_start(
                                 out=d, in_=s[:, 0:k * 128]),
                                 waits=[ready], inc=sDS, inc_n=16)
                    wslot_cache[sl] = (sDS, vcw)
            else:
                v = S.emit(eng, lambda e, s=src_bf, d=buf, k=kt: e.dma_start(
                    out=d[:, 0:k * 128], in_=s),
                    waits=waits + cache_bar[0], inc=sem, inc_n=16)
                ready = (sem, v)
            return buf, ready, sl

        wslot_cache = [None, None]

        def free_wslot(sl):
            wslot_free[sl] = [(sT, S.val(sT))]
            if wslot_cache[sl] is not None:
                wslot_free[sl].append(wslot_cache[sl])
                wslot_cache[sl] = None

        def barrier_waits():
            return [(sT, S.val(sT)), (sS, S.val(sS)), (sV, S.val(sV)),
                    (sDG, S.val(sDG)), (sDS, S.val(sDS))]

        S.emit("sync", lambda e: e.dma_start(out=identb, in_=ident), inc=sDW, inc_n=16)
        S.emit("sync", lambda e: e.dma_start(out=maskb, in_=mask_h), inc=sDW, inc_n=16)
        for (c0, c1, src) in ((0, 16, b_in), (16, 32, b_gt), (32, 48, b_dl),
                              (48, 64, b_cv), (64, 72, b_ot)):
            S.emit("sync", lambda e, a=c0, b=c1, s=src: e.dma_start(
                out=biasq[:, a:b], in_=s), inc=sDW, inc_n=16)
        S.emit("sync", lambda e: e.dma_start(out=dskq, in_=d_skip), inc=sDW, inc_n=16)
        S.emit("sync", lambda e: e.dma_start(out=a_sc, in_=a_own), inc=sDW, inc_n=16)
        vself = S.emit("sync", lambda e: e.dma_start(out=wf32[0:32, 0:1024], in_=selbc),
                       inc=sDW, inc_n=16)
        vselc = S.emit("vector", lambda e: e.tensor_copy(selb, wf32[0:32, 0:1024]),
                       waits=[(sDW, vself)], inc=sV)
        vload = S.emit("sync", lambda e: e.dma_start(
            out=wf32[:, 0:512].rearrange("p (k o) -> p k o", k=16),
            in_=w_bc.rearrange("k p o -> p k o")),
            waits=[(sV, vselc)], inc=sDW, inc_n=16)
        S.emit("vector", lambda e: e.memset(onescol, 1.0), inc=sV)
        S.emit("vector", lambda e: e.memset(onesrow, 1.0), inc=sV)
        S.emit("vector", lambda e: e.memset(h_last, 0.0), inc=sV)
        vbc0 = S.emit("vector", lambda e: e.tensor_copy(wbc_b, wf32[:, 0:512]),
                      waits=[(sDW, vload)], inc=sV)
        wf32_free[0] = [(sV, vbc0)]
        wf32_free[1] = [(sV, vbc0)]
        prep_done = [(sDW, S.val(sDW)), (sG, S.val(sG)), (sV, S.val(sV))]

        def emit_ln(W, movs):
            gate = [(sV, S.val(sV)), (sS, S.val(sS)), (sT, S.val(sT))]
            vm = vq = None
            for (mo, mw) in movs:
                bsum, w0 = take_bank()
                last_sum = None
                for it in range(8):
                    ws = (w0 + gate) if it == 0 else []
                    last_sum = mm(PSB[bsum][0:1, 0:mw], onescol[:, 0:1],
                                  x_c[:, it * W + mo: it * W + mo + mw],
                                  start=(it == 0), stop=(it == 7), waits=ws)
                vm = S.emit("vector", lambda e, m_=mo, w_=mw, b=bsum: e.tensor_scalar(
                    stat[0:1, m_:m_ + w_], PSB[b][0:1, 0:w_], 1.0 / D, None,
                    ALU.mult), waits=[(sT, last_sum)], inc=sV)
                bank_free[bsum] = [(sV, vm)]
            for (mo, mw) in movs:
                bsq, w1 = take_bank()
                last_sq = None
                for it in range(8):
                    vs = S.emit("scalar", lambda e, it_=it, W_=W, m_=mo, w_=mw:
                                e.activation(
                                out=xsq[:, 0:w_],
                                in_=x_c[:, it_ * W_ + m_: it_ * W_ + m_ + w_],
                                func=AF.Square), inc=sS, waits=[(sT, S.val(sT))])
                    last_sq = mm(PSB[bsq][0:1, 0:mw], onescol[:, 0:1],
                                 xsq[:, 0:mw], start=(it == 0),
                                 stop=(it == 7),
                                 waits=[(sS, vs)] + (w1 if it == 0 else []))
                vq = S.emit("vector", lambda e, m_=mo, w_=mw, b=bsq, W_=W:
                    e.tensor_scalar(
                    stat[0:1, W_ + m_:W_ + m_ + w_], PSB[b][0:1, 0:w_], 1.0 / D,
                    None, ALU.mult), waits=[(sT, last_sq)], inc=sV)
                bank_free[bsq] = [(sV, vq)]
            S.emit("vector", lambda e, W_=W: e.scalar_tensor_tensor(
                stat[0:1, 2 * W_:3 * W_], stat[0:1, 0:W_], -1.0, stat[0:1, 0:W_],
                ALU.mult, ALU.mult), waits=[(sV, vm)], inc=sV)
            vv2 = S.emit("vector", lambda e, W_=W: e.tensor_add(
                stat[0:1, 2 * W_:3 * W_], stat[0:1, 2 * W_:3 * W_],
                stat[0:1, W_:2 * W_]), waits=[(sV, vq)], inc=sV)
            vv3 = S.emit("vector", lambda e, W_=W: e.tensor_scalar(
                stat[0:1, 2 * W_:3 * W_], stat[0:1, 2 * W_:3 * W_], EPS, None,
                ALU.add), inc=sV)
            vg = S.emit("vector", lambda e: e.engine_nop(),
                        waits=[(sV, vv3)], inc=sV)
            vl = S.emit("scalar", lambda e, W_=W: e.activation(
                out=stat[0:1, 3 * W_:4 * W_], in_=stat[0:1, 2 * W_:3 * W_],
                func=AF.Ln), waits=[(sV, vg)], inc=sS)
            vneg = S.emit("vector", lambda e, W_=W: e.tensor_scalar(
                stat[0:1, 3 * W_:4 * W_], stat[0:1, 3 * W_:4 * W_], -0.5, None,
                ALU.mult), waits=[(sS, vl)], inc=sV)
            vgx = S.emit("vector", lambda e: e.engine_nop(), waits=[(sV, vneg)], inc=sV)
            vr = S.emit("scalar", lambda e, W_=W: e.activation(
                out=stat[0:1, 3 * W_:4 * W_], in_=stat[0:1, 3 * W_:4 * W_],
                func=AF.Exp), waits=[(sV, vgx)], inc=sS)
            evs = []
            for (dst, row, prow) in ((mrep, 0, (sV, vm)), (rrep, 3, (sS, vr))):
                vb = S.emit("vector", lambda e, r_=row, W_=W: e.tensor_copy(
                    statb[0:1, 0:W_], stat[0:1, r_ * W_:r_ * W_ + W_]),
                    waits=[prow, (sT, S.val(sT))] +
                          ([(sS, evs[-1])] if evs else []), inc=sV)
                o2 = 0
                while o2 < W:
                    w2 = min(512, W - o2)
                    bk, wfree = take_bank()
                    vrp = mm(PSB[bk][:, 0:w2], onesrow,
                             statb[0:1, o2:o2 + w2], start=True, stop=True,
                             waits=[(sV, vb)] + wfree)
                    ve = S.emit("scalar", lambda e, d=dst, o_=o2, w_=w2, b=bk:
                                e.copy(d[:, o_:o_ + w_], PSB[b][:, 0:w_]),
                                waits=[(sT, vrp)], inc=sS)
                    bank_free[bk] = [(sS, ve)]
                    evs.append(ve)
                    o2 += w2
            wrep = (sS, evs[-1])
            lastv = None
            for it in range(8):
                S.emit("vector", lambda e, it_=it, W_=W: e.tensor_sub(
                    xln[:, it_ * W_: it_ * W_ + W_],
                    x_c[:, it_ * W_: it_ * W_ + W_], mrep[:, 0:W_]),
                    waits=[wrep, (sT, S.val(sT))] if it == 0 else [], inc=sV)
                lastv = S.emit("vector", lambda e, it_=it, W_=W: e.tensor_mul(
                    xln[:, it_ * W_: it_ * W_ + W_],
                    xln[:, it_ * W_: it_ * W_ + W_], rrep[:, 0:W_]), inc=sV)
            return (sV, lastv)

        def emit_ffn(W, movs, tgroups, cache_mode, final_chunk=None, dbg_dump=False):
            ln_done = emit_ln(W, movs)
            vr9 = S.emit("sync", lambda e: e.dma_start(
                out=wr9f.rearrange("p (k o) -> p k o", k=8),
                in_=w_r9.rearrange("k p o -> p k o")),
                waits=[(sV, S.val(sV))], inc=sDW, inc_n=16)
            vr9c = S.emit("vector", lambda e: e.tensor_copy(wr9, wr9f),
                          waits=[(sDW, vr9)], inc=sV)
            for (go, gw) in tgroups:
                bk, wfree = take_bank()
                lastg = None
                for kt in range(8):
                    lastg = mm(PSB[bk][0:gw, 0:9],
                               xln[:, kt * W + go: kt * W + go + gw],
                               wr9[:, kt * 9:(kt + 1) * 9],
                               start=(kt == 0), stop=(kt == 7),
                               waits=([ln_done, (sV, vr9c)] +
                                      wfree) if kt == 0 else [])
                g = gw
                cf = cw_f

                def vop(fn, waits=()):
                    return S.emit("vector", fn, waits=waits, inc=sV)

                def guard(v):
                    return S.emit("vector", lambda e: e.engine_nop(),
                                  waits=[(sV, v)], inc=sV)

                v1 = vop(lambda e, g=g, bk=bk: e.reduce_max(cf[0:g, 0:1], PSB[bk][0:g, 0:8],
                         axis=AX.X), waits=[(sT, lastg)])
                guard(v1)
                v2 = vop(lambda e, g=g, bk=bk: e.tensor_scalar(cf[0:g, 1:2], cf[0:g, 0:1],
                         -1.0, None, ALU.mult))
                guard(v2)
                v3 = S.emit("scalar", lambda e, b=bk, g=g: e.activation(
                    out=cf[0:g, 2:10], in_=PSB[b][0:g, 0:8], func=AF.Exp,
                    bias=cf[0:g, 1:2]), waits=[(sV, S.val(sV))], inc=sS)
                v4 = vop(lambda e, g=g, bk=bk: e.reduce_sum(cf[0:g, 10:11], cf[0:g, 2:10],
                         axis=AX.X), waits=[(sS, v3)])
                guard(v4)
                v5 = vop(lambda e, g=g, bk=bk: e.reciprocal(cf[0:g, 10:11], cf[0:g, 10:11]))
                guard(v5)
                guard(vop(lambda e, g=g, bk=bk: e.tensor_scalar(cf[0:g, 2:10], cf[0:g, 2:10],
                    cf[0:g, 10:11], None, ALU.mult)))
                v6 = vop(lambda e, g=g, bk=bk: e.reduce_max(cf[0:g, 0:1], cf[0:g, 2:10],
                         axis=AX.X))
                guard(v6)
                guard(vop(lambda e, g=g, bk=bk: e.tensor_scalar(cf[0:g, 30:31], cf[0:g, 0:1],
                    -1.0, None, ALU.mult)))
                guard(vop(lambda e, g=g, bk=bk: e.tensor_scalar(cf[0:g, 12:20], cf[0:g, 2:10],
                    cf[0:g, 30:31], None, ALU.add)))
                guard(vop(lambda e, g=g, bk=bk: e.tensor_scalar(cf[0:g, 12:20], cf[0:g, 12:20],
                    0.0, None, ALU.is_ge)))
                guard(vop(lambda e, g=g, bk=bk: e.tensor_scalar(cf[0:g, 20:28], cf[0:g, 12:20],
                    -1.0, 1.0, ALU.mult, ALU.add)))
                v8 = vop(lambda e, g=g, bk=bk: e.tensor_mul(cf[0:g, 20:28], cf[0:g, 20:28],
                         cf[0:g, 2:10]))
                guard(v8)
                v9 = vop(lambda e, g=g, bk=bk: e.reduce_max(cf[0:g, 1:2], cf[0:g, 20:28],
                         axis=AX.X))
                guard(v9)
                guard(vop(lambda e, g=g, bk=bk: e.tensor_scalar(cf[0:g, 30:31], cf[0:g, 1:2],
                    -1.0, None, ALU.mult)))
                guard(vop(lambda e, g=g, bk=bk: e.tensor_scalar(cf[0:g, 20:28], cf[0:g, 20:28],
                    cf[0:g, 30:31], None, ALU.add)))
                guard(vop(lambda e, g=g, bk=bk: e.tensor_scalar(cf[0:g, 20:28], cf[0:g, 20:28],
                    0.0, None, ALU.is_ge)))
                guard(vop(lambda e, g=g, bk=bk: e.tensor_scalar(cf[0:g, 12:20], cf[0:g, 12:20],
                    cf[0:g, 0:1], None, ALU.mult)))
                guard(vop(lambda e, g=g, bk=bk: e.tensor_scalar(cf[0:g, 20:28], cf[0:g, 20:28],
                    cf[0:g, 1:2], None, ALU.mult)))
                guard(vop(lambda e, g=g, bk=bk: e.tensor_add(cf[0:g, 12:20], cf[0:g, 12:20],
                    cf[0:g, 20:28])))
                va = vop(lambda e, g=g, bk=bk: e.tensor_add(cf[0:g, 0:1], cf[0:g, 0:1],
                         cf[0:g, 1:2]))
                guard(va)
                vb_ = vop(lambda e, g=g, bk=bk: e.reciprocal(cf[0:g, 0:1], cf[0:g, 0:1]))
                guard(vb_)
                vqc = vop(lambda e, g=g, bk=bk: e.tensor_scalar(cf[0:g, 12:20], cf[0:g, 12:20],
                          cf[0:g, 0:1], None, ALU.mult))
                guard(vqc)
                vs1 = S.emit("scalar", lambda e, b=bk, g=g: e.activation(
                    out=cf[0:g, 28:29], in_=PSB[b][0:g, 8:9], func=AF.Exp,
                    scale=-1.0), waits=[(sT, lastg)], inc=sS)
                vs2 = vop(lambda e, g=g, bk=bk: e.tensor_scalar(cf[0:g, 28:29], cf[0:g, 28:29],
                          1.0, None, ALU.add), waits=[(sS, vs1)])
                guard(vs2)
                vs3 = vop(lambda e, g=g, bk=bk: e.reciprocal(cf[0:g, 28:29], cf[0:g, 28:29]))
                guard(vs3)
                vcp = vop(lambda e, g=g, bk=bk: e.tensor_copy(cf[0:g, 20:21], cf[0:g, 28:29]),
                          waits=[(sV, vqc)])
                bank_free[bk] = [(sV, vcp), (sS, vs1)]
                if dbg_dump and DBG and go == 0:
                    S.emit("gpsimd", lambda e, g=g, bk=bk: e.dma_start(
                        out=dbg_out["cf"], in_=cf), waits=[(sV, vcp)],
                        inc=sDG, inc_n=16)
                    S.emit("scalar", lambda e, b=bk, g=g: e.copy(
                        e_sc[:, 0:9], PSB[b][0:128, 0:9]),
                        waits=[(sS, vs1)], inc=sS)
                    S.emit("gpsimd", lambda e, g=g, bk=bk: e.dma_start(
                        out=dbg_out["lg"], in_=e_sc[:, 0:9]),
                        waits=[(sS, S.val(sS))], inc=sDG, inc_n=16)
                for grp0 in range(0, 9, 4):
                    nrow = min(4, 9 - grp0)
                    bk2, wfree2 = take_bank()
                    vtp = None
                    for rr in range(nrow):
                        vtp = S.emit("tensor", lambda e, b=bk2, g_=g, r_=grp0 + rr,
                                     c2=rr * 128: e.transpose(
                                     PSB[b][0:1, c2:c2 + g_],
                                     cf[0:g_, 12 + r_:13 + r_],
                                     identb[0:g_, 0:g_]),
                                     waits=([(sV, vcp)] +
                                            wfree2)
                                     if rr == 0 else [], inc=sT)
                    vev = S.emit("scalar", lambda e, b=bk2, go_=go, g_=g, W_=W,
                                 g0=grp0, nr=nrow: e.copy(
                                 cwT[0:1, g0 * W_: (g0 + nr) * W_].rearrange(
                                     "p (r w) -> p r w", r=nr)[:, :, go_:go_ + g_],
                                 PSB[b][0:1, 0:nr * 128].rearrange(
                                     "p (r w) -> p r w", r=nr)[:, :, 0:g_]),
                                 waits=[(sT, vtp)], inc=sS)
                    bank_free[bk2] = [(sS, vev)]
            ev_cw = (sS, S.val(sS))
            reps = []
            for row in range(9):
                o2 = 0
                while o2 < W:
                    w2 = min(512, W - o2)
                    bk, wfree = take_bank()
                    vrp = mm(PSB[bk][:, 0:w2], onesrow,
                             cwT[0:1, row * W + o2: row * W + o2 + w2],
                             start=True, stop=True,
                             waits=[ev_cw] + wfree)
                    ve = S.emit("scalar", lambda e, r_=row, o_=o2, w_=w2, b=bk:
                                e.copy(cwrep[:, r_ * W + o_: r_ * W + o_ + w_],
                                       PSB[b][:, 0:w_]),
                                waits=[(sT, vrp)], inc=sS)
                    bank_free[bk] = [(sS, ve)]
                    reps.append(ve)
                    o2 += w2
            cw_ready = (sS, reps[-1])
            if dbg_dump and DBG:
                S.emit("gpsimd", lambda e: e.dma_start(out=dbg_out["xln"], in_=xln),
                       waits=[ln_done], inc=sDG, inc_n=16)
                S.emit("gpsimd", lambda e: e.dma_start(out=dbg_out["cwt"], in_=cwT),
                       waits=[ev_cw], inc=sDG, inc_n=16)

            for ot in range(16):
                wb_, wr_, sl = load_wtile(w_sg[ot] if (cache_mode or NOCACHE) else None,
                                          None if (cache_mode or NOCACHE) else c_sg[ot], 8,
                                          c_sg[ot] if (cache_mode or NOCACHE) else None)
                for (mo, mw) in movs:
                    bg, fg = take_bank()
                    lg = None
                    for kt in range(8):
                        lg = mm(PSB[bg][:, 0:mw], wb_[:, kt * 128:(kt + 1) * 128],
                                xln[:, kt * W + mo: kt * W + mo + mw],
                                start=(kt == 0), stop=(kt == 7),
                                waits=([wr_, ln_done] + fg)
                                if kt == 0 else [])
                    vsl = S.emit("scalar", lambda e, b=bg, w_=mw, o_=ot * W + mo:
                                 e.activation(out=sact[:, o_:o_ + w_],
                                              in_=PSB[b][:, 0:w_], func=AF.Silu),
                                 waits=[(sT, lg)] +
                                       ([cw_ready] if ot >= 7 else []), inc=sS)
                    bank_free[bg] = [(sS, vsl)]
                free_wslot(sl)
            for ot in range(16):
                wb_, wr_, sl = load_wtile(w_su[ot] if (cache_mode or NOCACHE) else None,
                                          None if (cache_mode or NOCACHE) else c_su[ot], 8,
                                          c_su[ot] if (cache_mode or NOCACHE) else None)
                for (mo, mw) in movs:
                    bu, fu = take_bank()
                    lu = None
                    for kt in range(8):
                        lu = mm(PSB[bu][:, 0:mw], wb_[:, kt * 128:(kt + 1) * 128],
                                xln[:, kt * W + mo: kt * W + mo + mw],
                                start=(kt == 0), stop=(kt == 7),
                                waits=([wr_] + fu) if kt == 0 else [])
                    v1 = S.emit("vector", lambda e, b=bu, o_=ot * W + mo, w_=mw:
                                e.tensor_mul(sact[:, o_:o_ + w_],
                                             sact[:, o_:o_ + w_], PSB[b][:, 0:w_]),
                                waits=[(sT, lu), (sS, S.val(sS))], inc=sV)
                    if mw < 128:
                        S.emit("vector", lambda e: e.engine_nop(),
                               waits=[(sV, v1)], inc=sV)
                    S.emit("vector", lambda e, o_=ot * W + mo, mo_=mo, w_=mw:
                           e.tensor_mul(sact[:, o_:o_ + w_], sact[:, o_:o_ + w_],
                                        cwrep[:, 8 * W + mo_: 8 * W + mo_ + w_]),
                           waits=[cw_ready], inc=sV)
                    bank_free[bu] = [(sV, v1)]
                free_wslot(sl)

            for ex in range(8):
                for op in range(4):
                    wg_, wgr_, slg = load_wtile(
                        w_gu[ex][op] if (cache_mode or NOCACHE) else None,
                        None if (cache_mode or NOCACHE) else c_gu[ex][op], 8,
                        c_gu[ex][op] if (cache_mode or NOCACHE) else None)
                    wu_, wur_, slu = load_wtile(
                        w_gu[ex][op + 4] if (cache_mode or NOCACHE) else None,
                        None if (cache_mode or NOCACHE) else c_gu[ex][op + 4], 8,
                        c_gu[ex][op + 4] if (cache_mode or NOCACHE) else None)
                    for (mo, mw) in movs:
                        bg, fg = take_bank()
                        bu, fu = take_bank()
                        lg = lu = None
                        for kt in range(8):
                            lg = mm(PSB[bg][:, 0:mw],
                                    wg_[:, kt * 128:(kt + 1) * 128],
                                    xln[:, kt * W + mo: kt * W + mo + mw],
                                    start=(kt == 0), stop=(kt == 7),
                                    waits=([wgr_, ln_done] + fg)
                                    if kt == 0 else [])
                        for kt in range(8):
                            lu = mm(PSB[bu][:, 0:mw],
                                    wu_[:, kt * 128:(kt + 1) * 128],
                                    xln[:, kt * W + mo: kt * W + mo + mw],
                                    start=(kt == 0), stop=(kt == 7),
                                    waits=([wur_] + fu)
                                    if kt == 0 else [])
                        vsl = S.emit("scalar", lambda e, b=bg, w_=mw: e.activation(
                            out=sgs[:, 0:w_], in_=PSB[b][:, 0:w_], func=AF.Silu),
                            waits=[(sT, lg), (sV, S.val(sV))], inc=sS)
                        slot = ex * 4 + op
                        v1 = S.emit("vector", lambda e, b=bu, s_=slot * W + mo, w_=mw:
                                    e.tensor_mul(acts[:, s_:s_ + w_], sgs[:, 0:w_],
                                                 PSB[b][:, 0:w_]),
                                    waits=[(sS, vsl), (sT, lu)], inc=sV)
                        if mw < 128:
                            S.emit("vector", lambda e: e.engine_nop(),
                                   waits=[(sV, v1)], inc=sV)
                        S.emit("vector", lambda e, s_=slot * W + mo, mo_=mo, w_=mw,
                               ex_=ex: e.tensor_mul(
                               acts[:, s_:s_ + w_], acts[:, s_:s_ + w_],
                               cwrep[:, ex_ * W + mo_: ex_ * W + mo_ + w_]),
                               waits=[cw_ready], inc=sV)
                        bank_free[bg] = [(sV, v1)]
                        bank_free[bu] = [(sV, v1)]
                    free_wslot(slg)
                    free_wslot(slu)
            acts_ready = (sV, S.val(sV))
            if dbg_dump and DBG:
                S.emit("gpsimd", lambda e: e.dma_start(out=dbg_out["sact"], in_=sact),
                       waits=[acts_ready], inc=sDG, inc_n=16)
                S.emit("gpsimd", lambda e: e.dma_start(out=dbg_out["acts"], in_=acts),
                       waits=[acts_ready], inc=sDG, inc_n=16)

            for ot in range(8):
                wb_, wr_, sl = load_wtile(w_sd[ot] if (cache_mode or NOCACHE) else None,
                                          None if (cache_mode or NOCACHE) else c_sd[ot], 16,
                                          c_sd[ot] if (cache_mode or NOCACHE) else None)
                banks = []
                for (mo, mw) in movs:
                    bo, fo = take_bank()
                    banks.append((bo, mo, mw))
                    for kt in range(16):
                        mm(PSB[bo][:, 0:mw], wb_[:, kt * 128:(kt + 1) * 128],
                           sact[:, kt * W + mo: kt * W + mo + mw],
                           start=(kt == 0), stop=False,
                           waits=([wr_, acts_ready] + fo)
                           if kt == 0 else [])
                free_wslot(sl)
                lastd = None
                for ex in range(8):
                    wb2, wr2, sl2 = load_wtile(
                        w_dn[ex][ot] if (cache_mode or NOCACHE) else None,
                        None if (cache_mode or NOCACHE) else c_dn[ex][ot], 4,
                        c_dn[ex][ot] if (cache_mode or NOCACHE) else None)
                    for (bo, mo, mw) in banks:
                        for kt in range(4):
                            slot = ex * 4 + kt
                            lastd = mm(PSB[bo][:, 0:mw],
                                       wb2[:, kt * 128:(kt + 1) * 128],
                                       acts[:, slot * W + mo: slot * W + mo + mw],
                                       start=False, stop=(ex == 7 and kt == 3),
                                       waits=[wr2] if kt == 0 else [])
                    free_wslot(sl2)
                for (bo, mo, mw) in banks:
                    if final_chunk is None:
                        vep = S.emit("vector", lambda e, b=bo, o_=ot * W + mo, w_=mw:
                                     e.scalar_tensor_tensor(
                                     x_c[:, o_:o_ + w_], PSB[b][:, 0:w_], 0.5,
                                     x_c[:, o_:o_ + w_], ALU.mult, ALU.add),
                                     waits=[(sT, lastd)], inc=sV)
                        bank_free[bo] = [(sV, vep)]
                    else:
                        vep = S.emit("vector", lambda e, b=bo, o_=ot * W + mo,
                                     mo_=mo, w_=mw: e.scalar_tensor_tensor(
                                     of32[:, mo_:mo_ + w_], PSB[b][:, 0:w_], 0.5,
                                     x_c[:, o_:o_ + w_], ALU.mult, ALU.add),
                                     waits=[(sT, lastd), (sFin, S.val(sFin))],
                                     inc=sV)
                        bank_free[bo] = [(sV, vep)]
                        S.emit("gpsimd", lambda e, o_=ot, mo_=mo, w_=mw, c_=final_chunk:
                               e.dma_start(out=out_t[o_ * 128:(o_ + 1) * 128,
                                                     c_ * TCA + mo_: c_ * TCA + mo_ + w_],
                                           in_=of32[:, mo_:mo_ + w_]),
                               waits=[(sV, vep)], inc=sFin, inc_n=16)

        ag_d, ag_u, ag_b = [], [], []
        for c in range(NCA):
            cache_bar[0] = [(sDS, S.val(sDS))]
            K0, W = c * TCA, WA
            movs = [(0, 512), (512, 512), (1024, 8)]
            tgroups = [(i * 128, 128) for i in range(8)] + [(1024, 8)]
            own0 = HP
            xz_off = 0
            xz_movs = movs
            qw = W // 4
            for q in range(4):
                vld = S.emit("sync", lambda e, K_=K0 + q * qw, w_=qw: e.dma_start(
                    out=xf32[:, 0:8 * w_].rearrange("p (t w) -> p t w", t=8),
                    in_=x_in[:, K_:K_ + w_].rearrange("(t p) w -> p t w", p=128)),
                    waits=(barrier_waits() if q == 0 else [(sV, S.val(sV))]),
                    inc=sDW, inc_n=16)
                S.emit("vector", lambda e, o_=q * qw, w_=qw, W_=W: e.tensor_copy(
                    x_c[:, 0:8 * W_].rearrange("p (t w) -> p t w", t=8)[:, :, o_:o_ + w_],
                    xf32[:, 0:8 * w_].rearrange("p (t w) -> p t w", t=8)),
                    waits=[(sDW, vld)] + (prep_done if c == 0 and q == 0 else []),
                    inc=sV)
            emit_ffn(W, movs, tgroups, cache_mode=(c == 0), final_chunk=None, dbg_dump=(c == 0))
            S.emit("gpsimd", lambda e, c_=c, W_=W, o_=own0: e.dma_start(
                out=x1_sp[:, c_ * TCA:(c_ + 1) * TCA].rearrange(
                    "(t p) w -> p t w", p=128),
                in_=x_c[:, 0:8 * W_].rearrange("p (t w) -> p t w", t=8)[:, :, o_:o_ + TCA]),
                waits=[(sV, S.val(sV))], inc=sDG, inc_n=16)
            ln2 = emit_ln(W, movs)
            xz_gate = (sT, S.val(sT))
            for ot in range(16):
                wb_, wr_, sl = load_wtile(w_in[ot] if (c == 0 or NOCACHE) else None,
                                          None if (c == 0 or NOCACHE) else c_in[ot], 8,
                                          c_in[ot] if (c == 0 or NOCACHE) else None)
                for (mo, mw) in xz_movs:
                    bz, fz = take_bank()
                    lz = None
                    for kt in range(8):
                        lz = mm(PSB[bz][:, 0:mw], wb_[:, kt * 128:(kt + 1) * 128],
                                xln[:, kt * W + mo: kt * W + mo + mw],
                                start=(kt == 0), stop=(kt == 7),
                                waits=([wr_, ln2, xz_gate] + fz)
                                if kt == 0 else [])
                    vz = S.emit("scalar", lambda e, b=bz, d_=ot * WA + xz_off + mo,
                                w_=mw, o_=ot: e.activation(
                                out=xz[:, d_:d_ + w_], in_=PSB[b][:, 0:w_],
                                func=AF.Identity, bias=biasq[:, o_:o_ + 1]),
                                waits=[(sT, lz)], inc=sS)
                    bank_free[bz] = [(sS, vz)]
                free_wslot(sl)
            xz_done = (sS, S.val(sS))
            if c == 0:
                vmk = None
                for ot in range(16):
                    vmk = S.emit("vector", lambda e, o_=ot: e.tensor_mul(
                        xz[:, o_ * WA: o_ * WA + HP], xz[:, o_ * WA: o_ * WA + HP],
                        maskb), waits=[xz_done] if ot == 0 else [], inc=sV)
                xz_done = (sV, vmk)

            for ot in range(16):
                wb_, wr_, sl = load_wtile(w_gt[ot] if (c == 0 or NOCACHE) else None,
                                          None if (c == 0 or NOCACHE) else c_gt[ot], 8,
                                          c_gt[ot] if (c == 0 or NOCACHE) else None)
                for gi in range(2):
                    mo = own0 + gi * 512
                    bz, fz = take_bank()
                    lz = None
                    for kt in range(8):
                        lz = mm(PSB[bz][:, 0:512], wb_[:, kt * 128:(kt + 1) * 128],
                                xln[:, kt * W + mo: kt * W + mo + 512],
                                start=(kt == 0), stop=(kt == 7),
                                waits=([wr_] + fz) if kt == 0 else [])
                    vz = S.emit("scalar", lambda e, b=bz, o_=ot: e.activation(
                        out=sgs[:, 0:512], in_=PSB[b][:, 0:512], func=AF.Silu,
                        bias=biasq[:, 16 + o_:16 + o_ + 1]),
                        waits=[(sT, lz), (sDS, S.val(sDS))], inc=sS)
                    bank_free[bz] = [(sS, vz)]
                    S.emit("scalar", lambda e, c_=c, o_=ot, g_=gi: e.dma_start(
                        out=g_sp[o_ * 128:(o_ + 1) * 128,
                                 c_ * TCA + g_ * 512: c_ * TCA + g_ * 512 + 512],
                        in_=sgs[:, 0:512]), waits=[(sS, vz)], inc=sDS, inc_n=16)
                free_wslot(sl)

            for ot in range(16):
                wcv_, wcvr_, slcv = load_wtile(w_cv[ot] if (c == 0 or NOCACHE) else None,
                                               None if (c == 0 or NOCACHE) else c_cv[ot], 4,
                                               c_cv[ot] if (c == 0 or NOCACHE) else None)
                for gi in range(2):
                    mo = HP + gi * 512
                    bz, fz = take_bank()
                    lz = None
                    for k in range(4):
                        sft = 3 - k
                        lz = mm(PSB[bz][:, 0:512],
                                wcv_[:, k * 128:(k + 1) * 128],
                                xz[:, ot * WA + mo - sft: ot * WA + mo - sft + 512],
                                start=(k == 0), stop=(k == 3),
                                waits=([xz_done, wcvr_] + fz)
                                if k == 0 else [])
                    va = S.emit("scalar", lambda e, b=bz, o_=ot, g_=gi: e.activation(
                        out=actm[:, o_ * TCA + g_ * 512: o_ * TCA + g_ * 512 + 512],
                        in_=PSB[b][:, 0:512], func=AF.Silu,
                        bias=biasq[:, 48 + o_:48 + o_ + 1]),
                        waits=[(sT, lz)], inc=sS)
                    bank_free[bz] = [(sS, va)]
                    S.emit("gpsimd", lambda e, c_=c, o_=ot, g_=gi: e.dma_start(
                        out=act_sp[o_ * 128:(o_ + 1) * 128,
                                   c_ * TCA + g_ * 512: c_ * TCA + g_ * 512 + 512],
                        in_=actm[:, o_ * TCA + g_ * 512: o_ * TCA + g_ * 512 + 512]),
                        waits=[(sS, va)], inc=sDG, inc_n=16)
                free_wslot(slcv)
            act_done = (sS, S.val(sS))

            for ot in range(16):
                wb_, wr_, sl = load_wtile(w_dl[ot] if (c == 0 or NOCACHE) else None,
                                          None if (c == 0 or NOCACHE) else c_dl[ot], 16,
                                          c_dl[ot] if (c == 0 or NOCACHE) else None)
                for gi in range(2):
                    mo = gi * 512
                    bz, fz = take_bank()
                    lz = None
                    for kt in range(16):
                        lz = mm(PSB[bz][:, 0:512], wb_[:, kt * 128:(kt + 1) * 128],
                                actm[:, kt * TCA + mo: kt * TCA + mo + 512],
                                start=(kt == 0), stop=(kt == 15),
                                waits=([wr_, act_done] + fz)
                                if kt == 0 else [])
                    v1 = S.emit("scalar", lambda e, b=bz, o_=ot: e.activation(
                        out=e_sc, in_=PSB[b][:, 0:512], func=AF.Exp,
                        bias=biasq[:, 32 + o_:32 + o_ + 1]),
                        waits=[(sT, lz), (sDG, S.val(sDG)), (sV, S.val(sV))],
                        inc=sS)
                    v2 = S.emit("scalar", lambda e, m_=mo: e.activation(
                        out=dlt[:, m_:m_ + 512], in_=e_sc, func=AF.Ln, bias=1.0),
                        inc=sS)
                    bank_free[bz] = [(sS, v2)]
                S.emit("gpsimd", lambda e, c_=c, o_=ot: e.dma_start(
                    out=cc_di[c_][o_ * 128:(o_ + 1) * 128, :], in_=dlt),
                    waits=[(sS, S.val(sS))], inc=sDG, inc_n=16)
                vu = S.emit("vector", lambda e, o_=ot: e.tensor_mul(
                    ut, dlt, actm[:, o_ * TCA:(o_ + 1) * TCA]),
                    waits=[(sS, S.val(sS)), (sDG, S.val(sDG))], inc=sV)
                S.emit("gpsimd", lambda e, c_=c, o_=ot: e.dma_start(
                    out=cc_ui[c_][o_ * 128:(o_ + 1) * 128, :], in_=ut),
                    waits=[(sV, vu)], inc=sDG, inc_n=16)
                free_wslot(sl)

            for gi in range(2):
                mo = gi * 512
                bz, fz = take_bank()
                lz = None
                for kt in range(16):
                    lz = mm(PSB[bz][0:32, 0:512], wbc_b[:, kt * 32:(kt + 1) * 32],
                            actm[:, kt * TCA + mo: kt * TCA + mo + 512],
                            start=(kt == 0), stop=(kt == 15),
                            waits=([act_done, (sV, vbc0)] +
                                   fz) if kt == 0 else [])
                vz = S.emit("scalar", lambda e, b=bz, m_=mo: e.copy(
                    bct[:, m_:m_ + 512], PSB[b][0:32, 0:512]),
                    waits=[(sT, lz), (sDG, S.val(sDG))], inc=sS)
                bank_free[bz] = [(sS, vz)]
            S.emit("gpsimd", lambda e, c_=c: e.dma_start(
                out=cc_bi[c_], in_=bct), waits=[(sS, S.val(sS))], inc=sDG, inc_n=16)

            ag_d.append(S.emit("gpsimd", lambda e, c_=c: e.collective_compute(
                "AllGather", ALU.bypass, ins=[cc_di[c_]], outs=[cc_do[c_]],
                replica_groups=RG), waits=[(sDG, S.val(sDG))], inc=sG))
            ag_u.append(S.emit("gpsimd", lambda e, c_=c: e.collective_compute(
                "AllGather", ALU.bypass, ins=[cc_ui[c_]], outs=[cc_uo[c_]],
                replica_groups=RG), inc=sG))
            ag_b.append(S.emit("gpsimd", lambda e, c_=c: e.collective_compute(
                "AllGather", ALU.bypass, ins=[cc_bi[c_]], outs=[cc_bo[c_]],
                replica_groups=RG), inc=sG))

        rs_done = [None, None]
        dl_free = [None, None]
        u_free = [None, None]
        a_free = [None, None]
        y_free = None
        rep_guard = [None, None]
        mul_last = None
        free_ph = None
        free_p = None
        for s_ in range(NCB):
            j = s_ // 2
            lc = 2 * (s_ % 2)
            half = s_ % 2
            ag_wait = [(sG, ag_b[lc + 1])]
            if s_ == 0:
                ag_wait += barrier_waits()
            vb1 = S.emit("sync", lambda e, l_=lc, j_=j: e.dma_start(
                out=bc_g[:, 0:TCA], in_=cc_bo[l_][j_]),
                waits=ag_wait + [(sT, S.val(sT))], inc=sBC, inc_n=16)
            vb2 = S.emit("sync", lambda e, l_=lc, j_=j: e.dma_start(
                out=bc_g[:, TCA:TCB], in_=cc_bo[l_ + 1][j_]), inc=sBC, inc_n=16)
            rep_last = None
            first_rep = True
            for nn in range(4):
                for (dst, row) in ((brep, nn), (crep, 4 + nn)):
                    for sp in range(4):
                        bk, fk = take_bank()
                        vrp = mm(PSB[bk][:, 0:512],
                                 selb[:, row * 128:(row + 1) * 128],
                                 bc_g[:, sp * 512:(sp + 1) * 512],
                                 start=True, stop=True,
                                 waits=[(sBC, vb2)] + fk)
                        gw = ([(sV, S.val(sV)), (sP, S.val(sP))]
                              if first_rep else [])
                        first_rep = False
                        rep_last = S.emit("scalar", lambda e, d=dst, n_=nn, s2=sp,
                                          b=bk: e.copy(
                                          d[:, n_ * TCB + s2 * 512:
                                            n_ * TCB + s2 * 512 + 512],
                                          PSB[b][:, 0:512]),
                                          waits=[(sT, vrp)] + gw, inc=sS)
                        bank_free[bk] = [(sS, rep_last)]
            reps_ok = (sS, rep_last)
            if DBG and s_ == 0:
                S.emit("gpsimd", lambda e: e.dma_start(out=dbg_out["brep"], in_=brepb[0]),
                       waits=[reps_ok], inc=sDG, inc_n=16)
                S.emit("gpsimd", lambda e: e.dma_start(out=dbg_out["crep"], in_=crepb[0]),
                       waits=[reps_ok], inc=sDG, inc_n=16)

            for dt in range(16):
                sl = dt % 2
                sB = sB0 if sl == 0 else sB1
                vd1 = S.emit("sync", lambda e, l_=lc, j_=j, d_=dt, s2=sl: e.dma_start(
                    out=dl_g[:, s2 * TCB: s2 * TCB + TCA],
                    in_=cc_do[l_][j_][d_ * 128:(d_ + 1) * 128, :]),
                    waits=[dl_free[sl], (sG, ag_d[lc + 1])], inc=sB, inc_n=16)
                S.emit("sync", lambda e, l_=lc, j_=j, d_=dt, s2=sl: e.dma_start(
                    out=dl_g[:, s2 * TCB + TCA: (s2 + 1) * TCB],
                    in_=cc_do[l_ + 1][j_][d_ * 128:(d_ + 1) * 128, :]),
                    inc=sB, inc_n=16)
                S.emit("sync", lambda e, l_=lc, j_=j, d_=dt, s2=sl: e.dma_start(
                    out=u_g[:, s2 * TCB: s2 * TCB + TCA],
                    in_=cc_uo[l_][j_][d_ * 128:(d_ + 1) * 128, :]),
                    waits=[u_free[sl], (sG, ag_u[lc + 1])], inc=sB, inc_n=16)
                vu2 = S.emit("sync", lambda e, l_=lc, j_=j, d_=dt, s2=sl: e.dma_start(
                    out=u_g[:, s2 * TCB + TCA: (s2 + 1) * TCB],
                    in_=cc_uo[l_ + 1][j_][d_ * 128:(d_ + 1) * 128, :]),
                    inc=sB, inc_n=16)
                dma_ok = (sB, vu2)
                last_a = None
                last_b = None
                for nn in range(4):
                    idx = dt * 4 + nn
                    asl = idx % 2
                    va = S.emit("scalar", lambda e, s2=sl, a2=asl, i_=idx: e.activation(
                        out=a_t[:, a2 * TCB:(a2 + 1) * TCB],
                        in_=dl_g[:, s2 * TCB:(s2 + 1) * TCB], func=AF.Exp,
                        scale=a_sc[:, i_:i_ + 1]),
                        waits=[dma_ok, a_free[asl]], inc=sS)
                    last_a = va
                    vbm = S.emit("vector", lambda e, s2=sl, a2=asl, n_=nn,
                                 h2=half: e.tensor_mul(
                        b_t[:, a2 * TCB:(a2 + 1) * TCB],
                        u_g[:, s2 * TCB:(s2 + 1) * TCB],
                        brep[:, n_ * TCB:(n_ + 1) * TCB]),
                        waits=[dma_ok, reps_ok], inc=sV)
                    last_b = vbm
                    init_ap = 0.0 if s_ == 0 else h_last[:, idx:idx + 1]
                    vsc = S.emit("vector", lambda e, a2=asl, ini=init_ap:
                                 e.tensor_tensor_scan(
                                 out=h_t, data0=a_t[:, a2 * TCB:(a2 + 1) * TCB],
                                 data1=b_t[:, a2 * TCB:(a2 + 1) * TCB],
                                 initial=ini,
                                 op0=ALU.mult, op1=ALU.add),
                                 waits=[(sS, va)] +
                                 ([(sP, mul_last)] if mul_last else []), inc=sV)
                    a_free[asl] = (sV, vsc)
                    if DBG and s_ == 0 and dt == 0 and nn == 0:
                        S.emit("gpsimd", lambda e, a2=asl: e.dma_start(
                            out=dbg_out["a0f"], in_=a_t[:, a2 * TCB:(a2 + 1) * TCB]),
                            waits=[(sV, vsc)], inc=sDG, inc_n=16)
                        S.emit("gpsimd", lambda e, a2=asl: e.dma_start(
                            out=dbg_out["b0"], in_=b_t[:, a2 * TCB:(a2 + 1) * TCB]),
                            waits=[(sV, vsc)], inc=sDG, inc_n=16)
                        vdmp = S.emit("gpsimd", lambda e: e.dma_start(
                            out=dbg_out["h0"], in_=h_t), waits=[(sV, vsc)],
                            inc=sDG, inc_n=16)
                        S.emit("vector", lambda e: e.engine_nop(),
                               waits=[(sDG, vdmp)], inc=sV)
                    S.emit("vector", lambda e: e.engine_nop(),
                           waits=[(sV, vsc)], inc=sV)
                    S.emit("vector", lambda e, i_=idx: e.tensor_copy(
                        h_last[:, i_:i_ + 1], h_t[:, TCB - 1:TCB]), inc=sV)
                    dst = ph_t if nn % 2 == 0 else p_t
                    mw = [(sV, vsc)]
                    if nn == 0:
                        if free_ph is not None:
                            mw.append(free_ph)
                        if dt == 0:
                            mw.append(reps_ok)
                    elif nn == 1:
                        if free_p is not None:
                            mw.append(free_p)
                    else:
                        mw.append((sV, vadd1))
                    mul_last = S.emit("gpsimd", lambda e, d=dst, n_=nn:
                                      e.tensor_mul(
                                      d, h_t, crep[:, n_ * TCB:(n_ + 1) * TCB]),
                                      waits=mw, inc=sP)
                    if nn == 1:
                        vadd1 = S.emit("vector", lambda e: e.tensor_add(
                            s1_t, ph_t, p_t), waits=[(sP, mul_last)], inc=sV)
                    elif nn == 3:
                        vadd3 = S.emit("vector", lambda e: e.tensor_add(
                            p_t, ph_t, p_t), waits=[(sP, mul_last)], inc=sV)
                        free_ph = (sV, vadd3)
                        vy = S.emit("vector", lambda e: e.tensor_add(y_t, s1_t, p_t),
                                    waits=[y_free], inc=sV)
                        free_p = (sV, vy)
                if DBG and s_ == 0 and dt == 0:
                    vdy = S.emit("gpsimd", lambda e: e.dma_start(
                        out=dbg_out["y0"], in_=y_t), waits=[(sV, vy)],
                        inc=sDG, inc_n=16)
                    S.emit("vector", lambda e: e.engine_nop(),
                           waits=[(sDG, vdy)], inc=sV)
                dl_free[sl] = (sS, last_a)
                u_free[sl] = (sV, last_b)
                y_free = (sDS, S.emit("scalar",
                          lambda e, h_=half, j_=j, d_=dt: e.dma_start(
                          out=cc_yp[h_][d_ // 4][j_, (d_ % 4) * 128:
                                                  (d_ % 4 + 1) * 128, :],
                          in_=y_t),
                          waits=[(sV, vy)], inc=sDS, inc_n=16))
                if s_ == 7 and dt % 4 == 3:
                    for hh in range(2):
                        rs_done[hh] = S.emit("gpsimd",
                            lambda e, h2=hh, q2=dt // 4: e.collective_compute(
                            "ReduceScatter", ALU.add, ins=[cc_yp[h2][q2]],
                            outs=[cc_yr[h2][q2]], replica_groups=RG),
                            waits=[(sDG, S.val(sDG)), (sDS, S.val(sDS))], inc=sG)
            rep_guard[half] = (sV, S.val(sV))

        for c in range(NCC):
            cache_bar[0] = [(sDS, S.val(sDS))]
            h_ = c // 2
            off = (c % 2) * TCC
            vy_ = None
            for q in range(4):
                vy_ = S.emit("sync", lambda e, h2=h_, o_=off, q_=q: e.dma_start(
                    out=yc.rearrange("p (t w) -> p t w", t=16)[:, q_ * 4:
                                                              (q_ + 1) * 4, :],
                    in_=cc_yr[h2][q_].rearrange(
                        "(t p) w -> p t w", p=128)[:, :, o_:o_ + TCC]),
                    waits=([(sG, rs_done[h_])] + barrier_waits())
                    if q == 0 else [], inc=sDW, inc_n=16)
            S.emit("sync", lambda e, c_=c: e.dma_start(
                out=actc.rearrange("p (t w) -> p t w", t=16),
                in_=act_sp[:, c_ * TCC:(c_ + 1) * TCC].rearrange(
                    "(t p) w -> p t w", p=128)), inc=sDW, inc_n=16)
            S.emit("sync", lambda e, c_=c: e.dma_start(
                out=gc.rearrange("p (t w) -> p t w", t=16),
                in_=g_sp[:, c_ * TCC:(c_ + 1) * TCC].rearrange(
                    "(t p) w -> p t w", p=128)), inc=sDW, inc_n=16)
            vx_ = S.emit("sync", lambda e, c_=c: e.dma_start(
                out=x1c.rearrange("p (t w) -> p t w", t=8),
                in_=x1_sp[:, c_ * TCC:(c_ + 1) * TCC].rearrange(
                    "(t p) w -> p t w", p=128)), inc=sDW, inc_n=16)
            dok = (sDW, vx_)
            for dt in range(16):
                S.emit("vector", lambda e, d_=dt: e.scalar_tensor_tensor(
                    yc[:, d_ * TCC:(d_ + 1) * TCC], actc[:, d_ * TCC:(d_ + 1) * TCC],
                    dskq[:, d_:d_ + 1], yc[:, d_ * TCC:(d_ + 1) * TCC],
                    ALU.mult, ALU.add), waits=[dok] if dt == 0 else [], inc=sV)
                S.emit("vector", lambda e, d_=dt: e.tensor_mul(
                    ycb[:, d_ * TCC:(d_ + 1) * TCC], yc[:, d_ * TCC:(d_ + 1) * TCC],
                    gc[:, d_ * TCC:(d_ + 1) * TCC]), inc=sV)
            gat = (sV, S.val(sV))
            for ot in range(8):
                wb_, wr_, sl = load_wtile(w_ot[ot] if (c == 0 or NOCACHE) else None,
                                          None if (c == 0 or NOCACHE) else c_ot[ot], 16,
                                          c_ot[ot] if (c == 0 or NOCACHE) else None)
                for gi in range(2):
                    mo = gi * 512
                    bz, fz = take_bank()
                    lz = None
                    for kt in range(16):
                        lz = mm(PSB[bz][:, 0:512], wb_[:, kt * 128:(kt + 1) * 128],
                                ycb[:, kt * TCC + mo: kt * TCC + mo + 512],
                                start=(kt == 0), stop=(kt == 15),
                                waits=([wr_, gat] + fz)
                                if kt == 0 else [])
                    vep = S.emit("vector", lambda e, b=bz, o_=ot, m_=mo:
                                 e.scalar_tensor_tensor(
                                 x2st[:, m_:m_ + 512], PSB[b][:, 0:512],
                                 biasq[:, 64 + o_:64 + o_ + 1],
                                 x1c[:, o_ * TCC + m_: o_ * TCC + m_ + 512],
                                 ALU.add, ALU.add),
                                 waits=[(sT, lz), (sDS, S.val(sDS))], inc=sV)
                    bank_free[bz] = [(sV, vep)]
                    S.emit("scalar", lambda e, c_=c, o_=ot, m_=mo: e.dma_start(
                        out=x2_sp[o_ * 128:(o_ + 1) * 128,
                                  c_ * TCC + m_: c_ * TCC + m_ + 512],
                        in_=x2st[:, m_:m_ + 512]),
                        waits=[(sV, vep)], inc=sDS, inc_n=16)
                free_wslot(sl)
        x2_all = (sDS, S.val(sDS))

        for c in range(NCA):
            cache_bar[0] = [(sDS, S.val(sDS))]
            W = TCA
            movs = [(0, 512), (512, 512)]
            tgroups = [(i * 128, 128) for i in range(8)]
            vld = S.emit("sync", lambda e, c_=c: e.dma_start(
                out=x_c[:, 0:8 * W].rearrange("p (t w) -> p t w", t=8),
                in_=x2_sp[:, c_ * TCA:(c_ + 1) * TCA].rearrange(
                    "(t p) w -> p t w", p=128)),
                waits=[x2_all] + barrier_waits(), inc=sDW, inc_n=16)
            S.emit("vector", lambda e: e.engine_nop(), waits=[(sDW, vld)], inc=sV)
            emit_ffn(W, movs, tgroups, cache_mode=False, final_chunk=c)

        if DBG:
            S.emit("gpsimd", lambda e: e.dma_start(out=dbg_out["asc"], in_=a_sc),
                   waits=[(sFin, S.val(sFin))], inc=sDG, inc_n=16)
            S.emit("gpsimd", lambda e: e.dma_start(out=dbg_out["hl"], in_=h_last),
                   waits=[(sFin, S.val(sFin))], inc=sDG, inc_n=16)
            pairs = [("do0", cc_do[0].rearrange("s d w -> (s d) w")),
                     ("uo0", cc_uo[0].rearrange("s d w -> (s d) w")),
                     ("x1", x1_sp), ("act", act_sp), ("gact", g_sp),
                     ("delta", cc_di[0]), ("u", cc_ui[0]), ("bc", cc_bi[0]),
                     ("x2", x2_sp)]
            for nm, src in pairs:
                S.emit("gpsimd", lambda e, d=dbg_out[nm], s2=src: e.dma_start(
                    out=d, in_=s2), waits=[(sFin, S.val(sFin))], inc=sDG, inc_n=16)

        S.emit("gpsimd", None, waits=[(sFin, S.val(sFin)), (sDG, S.val(sDG)),
                                      (sDS, S.val(sDS)), (sDW, S.val(sDW))])

        @blk.sync
        def _(e):
            S.replay("sync", e)

        @blk.vector
        def _(e):
            S.replay("vector", e)

        @blk.scalar
        def _(e):
            S.replay("scalar", e)

        @blk.tensor
        def _(e):
            S.replay("tensor", e)

        @blk.gpsimd
        def _(e):
            S.replay("gpsimd", e)

    return nc


def _tile_w(w):
    K, O = w.shape
    return np.ascontiguousarray(
        w.reshape(K // 128, 128, O // 128, 128).transpose(2, 1, 0, 3)
        .reshape(O // 128, 128, (K // 128) * 128))


def _bias_cols(b, n):
    return np.ascontiguousarray(b.reshape(n, 128).T)


_NC_CACHE = {}


def kernel(**inp):
    x = inp["L0_attn_x"]
    B, L, _ = x.shape
    if "prog" not in _NC_CACHE:
        _NC_CACHE["prog"] = build_program()
    nc = _NC_CACHE["prog"]

    base = {
        "w_sg": _tile_w(np.ascontiguousarray(inp["L0_ffn_shared_gate_w"].T)),
        "w_su": _tile_w(np.ascontiguousarray(inp["L0_ffn_shared_up_w"].T)),
        "w_sd": _tile_w(np.ascontiguousarray(inp["L0_ffn_shared_down_w"].T)),
        "w_gu": np.stack([_tile_w(np.ascontiguousarray(
            inp["L0_ffn_gate_up_proj"][e].T)) for e in range(8)]),
        "w_dn": np.stack([_tile_w(np.ascontiguousarray(
            inp["L0_ffn_down_proj"][e].T)) for e in range(8)]),
        "w_r9": np.ascontiguousarray(
            np.concatenate([inp["L0_ffn_router_weight"].T,
                            inp["L0_ffn_shared_expert_gate_w"].T],
                           axis=1).reshape(8, 128, 9)),
        "w_in": _tile_w(inp["L0_attn_W_in"]),
        "w_gt": _tile_w(inp["L0_attn_W_gate"]),
        "w_dl": _tile_w(inp["L0_attn_W_delta"]),
        "w_bc": np.ascontiguousarray(
            np.concatenate([inp["L0_attn_W_B"], inp["L0_attn_W_C"]],
                           axis=1).reshape(16, 128, 32)),
        "w_ot": _tile_w(inp["L0_attn_W_out"]),
        "b_in": _bias_cols(inp["L0_attn_b_in"], 16),
        "b_gt": _bias_cols(inp["L0_attn_b_gate"], 16),
        "b_dl": _bias_cols(inp["L0_attn_b_delta"], 16),
        "b_cv": _bias_cols(inp["L0_attn_conv_b"], 16),
        "b_ot": _bias_cols(inp["L0_attn_b_out"], 8),
        "d_skip": _bias_cols(inp["L0_attn_D_skip"], 16),
        "ident": np.eye(128, dtype=np.float32),
    }
    cw = inp["L0_attn_conv_w"][:, 0, :]
    dg = np.zeros((16, 4, 128, 128), np.float32)
    ii = np.arange(128)
    for dt in range(16):
        for k in range(4):
            dg[dt, k, ii, ii] = cw[dt * 128:(dt + 1) * 128, k]
    base["w_cv"] = np.ascontiguousarray(
        dg.transpose(0, 2, 1, 3).reshape(16, 128, 512))
    alog = inp["L0_attn_A_log"]

    in_maps = []
    for core in range(8):
        g, r = core // 4, core % 4
        m = dict(base)
        xt = np.zeros((HP + TL, D), np.float32)
        lo = r * TL
        xt[HP:] = x[g, lo:lo + TL]
        if r > 0:
            xt[HP - 3:HP] = x[g, lo - 3:lo]
        m["x_fm"] = np.ascontiguousarray(xt.T)
        msk = np.ones((128, HP), np.float32)
        if r == 0:
            msk[:] = 0.0
        m["mask_h"] = msk
        a_sl = -np.exp(alog[:, 4 * r:4 * r + 4])
        m["a_own"] = np.ascontiguousarray(
            a_sl.reshape(16, 128, 4).transpose(1, 0, 2).reshape(128, 64))
        sel = np.zeros((32, 8, 128), np.float32)
        for nn in range(4):
            sel[4 * r + nn, nn, :] = 1.0
            sel[16 + 4 * r + nn, 4 + nn, :] = 1.0
        m["selbc"] = sel.reshape(32, 8 * 128)
        in_maps.append(m)

    res = run_bass_kernel_spmd(nc, in_maps, core_ids=list(range(8)))
    out = np.empty((B, L, D), np.float32)
    for core in range(8):
        g, r = core // 4, core % 4
        out[g, r * TL:(r + 1) * TL] = res.results[core]["out_fm"].T
    if DBG:
        kernel.dbg = [dict(res.results[c]) for c in range(8)]
    return out


# version 08
# speedup vs baseline: 1.1761x; 1.1761x over previous
import contextlib
import os
import numpy as np
import concourse.bass as bass
import concourse.mybir as mybir
from concourse.bass_utils import run_bass_kernel_spmd

F32 = mybir.dt.float32
BF16 = mybir.dt.bfloat16
U8 = mybir.dt.uint8
AF = mybir.ActivationFunctionType
ALU = mybir.AluOpType
AX = mybir.AxisListType

D = 1024
DI = 2048
TL = 4096
HP = 8
EPS = 1e-5
TCA = 1024
NCA = 4
TCB = 2048
NCB = 8
TCC = 1024
NCC = 4
WA = HP + TCA

RG = [[0, 1, 2, 3], [4, 5, 6, 7]]
DBG = bool(int(os.environ.get("MK_DBG", "0")))
NOCACHE = bool(int(os.environ.get("MK_NOCACHE", "0")))


class Sched:
    ENGINES = ("sync", "vector", "scalar", "tensor", "gpsimd")

    def __init__(self):
        self.steps = {e: [] for e in self.ENGINES}
        self.count = {}

    def emit(self, eng, fn, waits=(), inc=None, inc_n=1):
        after = None
        if inc is not None:
            self.count[inc] = self.count.get(inc, 0) + inc_n
            after = self.count[inc]
        self.steps[eng].append((fn, [w for w in waits if w is not None],
                                (inc, inc_n) if inc is not None else None))
        return after

    def val(self, sem):
        return self.count.get(sem, 0)

    def replay(self, eng_name, eng):
        for fn, waits, inc in self.steps[eng_name]:
            for (s_, v_) in waits:
                eng.wait_ge(s_, v_)
            r = fn(eng) if fn is not None else None
            if inc is not None:
                assert r is not None
                r.then_inc(inc[0], inc[1])


class Overlay:
    def __init__(self, nc, stack, name, nbytes):
        self.nc = nc
        self.back = stack.enter_context(nc.sbuf_tensor(name, [128, nbytes], U8))
        self.base = nc.lookup_mloc(self.back).addr

    def view(self, name, shape, dtype, off_bytes):
        h = self.nc.virtual_sbuf_tensor(name, list(shape), dtype)
        self.nc._tensor(h.name, h.shape, h.dtype, type="SB")
        mloc = self.nc.lookup_mloc(h)
        mloc.allocated = True
        mloc.addr = self.base + off_bytes
        return h.ap()


def build_program():
    nc = bass.Bass("TRN2", target_bir_lowering=False)

    def din(name, shape, dt=F32):
        return nc.dram_tensor(name, shape, dt, kind="ExternalInput").ap()

    x_in = din("x_fm", [D, HP + TL])
    w_sg = din("w_sg", [16, 128, 8 * 128])
    w_su = din("w_su", [16, 128, 8 * 128])
    w_sd = din("w_sd", [8, 128, 16 * 128])
    w_gu = din("w_gu", [8, 8, 128, 8 * 128])
    w_dn = din("w_dn", [8, 8, 128, 4 * 128])
    w_r9 = din("w_r9", [8, 128, 9])
    w_in = din("w_in", [16, 128, 8 * 128])
    w_gt = din("w_gt", [16, 128, 8 * 128])
    w_dl = din("w_dl", [16, 128, 16 * 128])
    w_bc = din("w_bc", [16, 128, 32])
    w_ot = din("w_ot", [8, 128, 16 * 128])
    w_cv = din("w_cv", [16, 128, 4 * 128])
    b_in = din("b_in", [128, 16])
    b_gt = din("b_gt", [128, 16])
    b_dl = din("b_dl", [128, 16])
    b_cv = din("b_cv", [128, 16])
    b_ot = din("b_ot", [128, 8])
    a_own = din("a_own", [128, 64])
    d_skip = din("d_skip", [128, 16])
    mask_h = din("mask_h", [128, HP])
    ident = din("ident", [128, 128])
    selbc = din("selbc", [32, 8 * 128])

    out_t = nc.dram_tensor("out_fm", [D, TL], F32, kind="ExternalOutput").ap()

    def dint(name, shape, dt=BF16):
        return nc.dram_tensor(name, shape, dt, kind="Internal").ap()

    c_sg = dint("c_sg", [16, 128, 8 * 128])
    c_su = dint("c_su", [16, 128, 8 * 128])
    c_sd = dint("c_sd", [8, 128, 16 * 128])
    c_gu = dint("c_gu", [8, 8, 128, 8 * 128])
    c_dn = dint("c_dn", [8, 8, 128, 4 * 128])
    c_in = dint("c_in", [16, 128, 8 * 128])
    c_gt = dint("c_gt", [16, 128, 8 * 128])
    c_dl = dint("c_dl", [16, 128, 16 * 128])
    c_ot = dint("c_ot", [8, 128, 16 * 128])
    c_cv = dint("c_cv", [16, 128, 4 * 128])
    x1_sp = dint("x1_sp", [D, TL])
    g_sp = dint("g_sp", [DI, TL])
    act_sp = dint("act_sp", [DI, TL])
    x2_sp = dint("x2_sp", [D, TL])
    cc_di = [dint(f"cc_di{c}", [DI, TCA]) for c in range(NCA)]
    cc_do = [dint(f"cc_do{c}", [4, DI, TCA]) for c in range(NCA)]
    cc_ui = [dint(f"cc_ui{c}", [DI, TCA]) for c in range(NCA)]
    cc_uo = [dint(f"cc_uo{c}", [4, DI, TCA]) for c in range(NCA)]
    cc_bi = [dint(f"cc_bi{c}", [32, TCA]) for c in range(NCA)]
    cc_bo = [dint(f"cc_bo{c}", [4, 32, TCA]) for c in range(NCA)]
    cc_yp = [[dint(f"cc_yp{h}_{q}", [4, 512, TCB], F32) for q in range(4)]
             for h in range(2)]
    cc_yr = [[dint(f"cc_yr{h}_{q}", [512, TCB], F32) for q in range(4)]
             for h in range(2)]

    dbg_out = {}
    if DBG:
        for nm, shp in (("x1", [D, TL]), ("act", [DI, TL]), ("gact", [DI, TL]),
                        ("delta", [DI, TCA]), ("u", [DI, TCA]),
                        ("bc", [32, TCA]), ("y", [DI, TCB]), ("x2", [D, TL]),
                        ("xln", [128, 8 * WA]), ("cwt", [1, 9 * WA]),
                        ("sact", [128, 16 * WA]), ("acts", [128, 32 * WA]),
                        ("yp", [4 * DI, TCB]), ("do0", [4 * DI, TCA]), ("uo0", [4 * DI, TCA]), ("brep", [128, 4 * TCB]), ("crep", [128, 4 * TCB]), ("b0", [128, TCB]), ("h0", [128, TCB]), ("y0", [128, TCB])):
            dbg_out[nm] = nc.dram_tensor("dbg_" + nm, shp, BF16,
                                         kind="ExternalOutput").ap()
        for nm, shp in (("cf", [128, 32]), ("lg", [128, 9]),
                        ("asc", [128, 64]), ("hl", [128, 64]),
                        ("a0f", [128, TCB])):
            dbg_out[nm] = nc.dram_tensor("dbg_" + nm, shp, F32,
                                         kind="ExternalOutput").ap()

    with contextlib.ExitStack() as stk:
        def sball(name, shape, dt):
            return stk.enter_context(nc.sbuf_tensor(name, shape, dt))[:]

        wbufA = sball("wbufA", [128, 16 * 128], BF16)
        wbufB = sball("wbufB", [128, 16 * 128], BF16)
        wbufC = sball("wbufC", [128, 16 * 128], BF16)
        wf32 = sball("wf32", [128, 16 * 128], F32)
        wf32B = sball("wf32B", [128, 16 * 128], F32)
        wr9f = sball("wr9f", [128, 72], F32)
        wr9 = sball("wr9", [128, 72], BF16)
        wbc_b = sball("wbc_b", [128, 512], BF16)
        sgs = sball("sgs", [128, WA], BF16)
        statb = sball("statb", [1, WA], BF16)
        cw_f = sball("cw_f", [128, 32], F32)
        identb = sball("identb", [128, 128], F32)
        maskb = sball("maskb", [128, HP], F32)
        biasq = sball("biasq", [128, 80], F32)
        dskq = sball("dskq", [128, 16], F32)
        a_sc = sball("a_sc", [128, 64], F32)
        h_last = sball("h_last", [128, 64], F32)
        onescol = sball("onescol", [128, 2], BF16)
        onesrow = sball("onesrow", [1, 128], BF16)
        selb = sball("selb", [32, 8 * 128], BF16)
        bct = sball("bct", [32, TCA], BF16)
        dlt = sball("dlt", [128, TCA], BF16)
        ut = sball("ut", [128, TCA], BF16)

        OV = Overlay(nc, stk, "ov", 167168)
        x_c = OV.view("x_c", [128, 8 * WA], BF16, 0)
        xln = OV.view("xln", [128, 8 * WA], BF16, 16512)
        cwrep = OV.view("cwrep", [128, 9 * WA], BF16, 33024)
        mrep = OV.view("mrep", [128, WA], BF16, 51600)
        rrep = OV.view("rrep", [128, WA], BF16, 53664)
        xsq = OV.view("xsq", [128, WA], BF16, 55728)
        e_sc = OV.view("e_sc", [128, 512], F32, 57792)
        acts = OV.view("acts", [128, 32 * WA], BF16, 63984)
        sact = OV.view("sact", [128, 16 * WA], BF16, 130032)
        xz = OV.view("xz", [128, 16 * WA], BF16, 63984)
        actm = OV.view("actm", [128, 16 * TCA], BF16, 97008)
        stat = OV.view("stat", [1, 4 * WA], F32, 97008)
        xf32 = OV.view("xf32", [128, 8 * 258], F32, 113520)
        cwT = OV.view("cwT", [1, 9 * WA], BF16, 144480)
        of32 = OV.view("of32", [128, 1024], F32, 163056)
        dl_g = OV.view("dl_g", [128, 2 * TCB], BF16, 0)
        u_g = OV.view("u_g", [128, 2 * TCB], BF16, 8192)
        bc_g = OV.view("bc_g", [32, TCB], BF16, 16384)
        bco = OV.view("bco", [8, TCB], BF16, 20480)
        brep = OV.view("brep", [128, 4 * TCB], BF16, 24576)
        crep = OV.view("crep", [128, 4 * TCB], BF16, 40960)
        brepB = OV.view("brepB", [128, 4 * TCB], BF16, 106496)
        crepB = OV.view("crepB", [128, 4 * TCB], BF16, 122880)
        brepb = [brep, brepB]
        crepb = [crep, crepB]
        a_t = OV.view("a_t", [128, 2 * TCB], F32, 57344)
        b_t = OV.view("b_t", [128, 2 * TCB], BF16, 73728)
        h_t = OV.view("h_t", [128, TCB], BF16, 81920)
        p_t = OV.view("p_t", [128, TCB], BF16, 86016)
        ph_t = OV.view("ph_t", [128, TCB], BF16, 90112)
        s1_t = OV.view("s1_t", [128, TCB], BF16, 94208)
        y_t = OV.view("y_t", [128, TCB], F32, 98304)
        yc = OV.view("yc", [128, 16 * TCC], F32, 0)
        actc = OV.view("actc", [128, 16 * TCC], BF16, 65536)
        ycb = OV.view("ycb", [128, 16 * TCC], BF16, 65536)
        gc = OV.view("gc", [128, 16 * TCC], BF16, 98304)
        x1c = OV.view("x1c", [128, 8 * TCC], BF16, 131072)
        x2st = OV.view("x2st", [128, 1024], BF16, 147456)

        sDW = stk.enter_context(nc.semaphore("sDW"))
        sDW2 = stk.enter_context(nc.semaphore("sDW2"))
        sB0 = stk.enter_context(nc.semaphore("sB0"))
        sB1 = stk.enter_context(nc.semaphore("sB1"))
        sBC = stk.enter_context(nc.semaphore("sBC"))
        sP = stk.enter_context(nc.semaphore("sP"))
        sW3 = stk.enter_context(nc.semaphore("sW3"))
        sW4 = stk.enter_context(nc.semaphore("sW4"))
        sDG = stk.enter_context(nc.semaphore("sDG"))
        sDS = stk.enter_context(nc.semaphore("sDS"))
        sV = stk.enter_context(nc.semaphore("sV"))
        sS = stk.enter_context(nc.semaphore("sS"))
        sT = stk.enter_context(nc.semaphore("sT"))
        sG = stk.enter_context(nc.semaphore("sG"))
        sFin = stk.enter_context(nc.semaphore("sFin"))

        PSB = [stk.enter_context(nc.psum_tensor(f"pb{i}", [128, 512], F32))[:]
               for i in range(8)]

        blk = stk.enter_context(nc.Block())
        S = Sched()

        def mm(psum, lhsT, rhs, start, stop, waits=()):
            return S.emit("tensor",
                          lambda e, a=psum, b=lhsT, c=rhs, s1=start, s2=stop:
                          e.matmul(a, b, c, start=s1, stop=s2),
                          waits=waits, inc=sT)

        bank_free = {i: [] for i in range(8)}
        bank_rr = [0]

        def take_bank():
            i = (bank_rr[0] + 1) % 8
            bank_rr[0] = i
            return i, list(bank_free[i])

        wslot = [0]
        wslot_free = [[], [], []]

        wf32_free = [[], []]
        cache_bar = [[]]
        WBUFS = (wbufA, wbufB, wbufC)
        WSEMS = (sDW, sDW2, sW3)
        WENGS = ("sync", "gpsimd", "sync")

        def load_wtile(src_f32, src_bf, kt, cache_write):
            sl = wslot[0]
            wslot[0] = (wslot[0] + 1) % 3
            buf = WBUFS[sl]
            waits = list(wslot_free[sl])
            sem = WSEMS[sl]
            eng = WENGS[sl]
            if src_f32 is not None:
                stg = wf32 if sl % 2 == 0 else wf32B
                v = S.emit(eng, lambda e, s=src_f32, k=kt, g=stg: e.dma_start(
                    out=g[:, 0:k * 128], in_=s),
                    waits=waits + wf32_free[sl % 2], inc=sem, inc_n=16)
                v2 = S.emit("vector", lambda e, d=buf, k=kt, g=stg: e.tensor_copy(
                    d[:, 0:k * 128], g[:, 0:k * 128]),
                    waits=[(sem, v)], inc=sV)
                ready = (sV, v2)
                wf32_free[sl % 2] = [(sV, v2)]
                if cache_write is not None and not NOCACHE:
                    vcw = S.emit("scalar", lambda e, d=cache_write, s=buf, k=kt:
                                 e.dma_start(
                                 out=d, in_=s[:, 0:k * 128]),
                                 waits=[ready], inc=sDS, inc_n=16)
                    wslot_cache[sl] = (sDS, vcw)
            else:
                v = S.emit(eng, lambda e, s=src_bf, d=buf, k=kt: e.dma_start(
                    out=d[:, 0:k * 128], in_=s),
                    waits=waits + cache_bar[0], inc=sem, inc_n=16)
                ready = (sem, v)
            return buf, ready, sl

        wslot_cache = [None, None, None]

        def free_wslot(sl):
            wslot_free[sl] = [(sT, S.val(sT))]
            if wslot_cache[sl] is not None:
                wslot_free[sl].append(wslot_cache[sl])
                wslot_cache[sl] = None

        def barrier_waits():
            return [(sT, S.val(sT)), (sS, S.val(sS)), (sV, S.val(sV)),
                    (sDG, S.val(sDG)), (sDS, S.val(sDS))]

        S.emit("sync", lambda e: e.dma_start(out=identb, in_=ident), inc=sDW, inc_n=16)
        S.emit("sync", lambda e: e.dma_start(out=maskb, in_=mask_h), inc=sDW, inc_n=16)
        for (c0, c1, src) in ((0, 16, b_in), (16, 32, b_gt), (32, 48, b_dl),
                              (48, 64, b_cv), (64, 72, b_ot)):
            S.emit("sync", lambda e, a=c0, b=c1, s=src: e.dma_start(
                out=biasq[:, a:b], in_=s), inc=sDW, inc_n=16)
        S.emit("sync", lambda e: e.dma_start(out=dskq, in_=d_skip), inc=sDW, inc_n=16)
        S.emit("sync", lambda e: e.dma_start(out=a_sc, in_=a_own), inc=sDW, inc_n=16)
        vself = S.emit("sync", lambda e: e.dma_start(out=wf32[0:32, 0:1024], in_=selbc),
                       inc=sDW, inc_n=16)
        vselc = S.emit("vector", lambda e: e.tensor_copy(selb, wf32[0:32, 0:1024]),
                       waits=[(sDW, vself)], inc=sV)
        vload = S.emit("sync", lambda e: e.dma_start(
            out=wf32[:, 0:512].rearrange("p (k o) -> p k o", k=16),
            in_=w_bc.rearrange("k p o -> p k o")),
            waits=[(sV, vselc)], inc=sDW, inc_n=16)
        S.emit("vector", lambda e: e.memset(onescol, 1.0), inc=sV)
        S.emit("vector", lambda e: e.memset(onesrow, 1.0), inc=sV)
        S.emit("vector", lambda e: e.memset(h_last, 0.0), inc=sV)
        vbc0 = S.emit("vector", lambda e: e.tensor_copy(wbc_b, wf32[:, 0:512]),
                      waits=[(sDW, vload)], inc=sV)
        wf32_free[0] = [(sV, vbc0)]
        wf32_free[1] = [(sV, vbc0)]
        prep_done = [(sDW, S.val(sDW)), (sG, S.val(sG)), (sV, S.val(sV))]

        def emit_ln(W, movs):
            gate = [(sV, S.val(sV)), (sS, S.val(sS)), (sT, S.val(sT))]
            vm = vq = None
            for (mo, mw) in movs:
                bsum, w0 = take_bank()
                last_sum = None
                for it in range(8):
                    ws = (w0 + gate) if it == 0 else []
                    last_sum = mm(PSB[bsum][0:1, 0:mw], onescol[:, 0:1],
                                  x_c[:, it * W + mo: it * W + mo + mw],
                                  start=(it == 0), stop=(it == 7), waits=ws)
                vm = S.emit("vector", lambda e, m_=mo, w_=mw, b=bsum: e.tensor_scalar(
                    stat[0:1, m_:m_ + w_], PSB[b][0:1, 0:w_], 1.0 / D, None,
                    ALU.mult), waits=[(sT, last_sum)], inc=sV)
                bank_free[bsum] = [(sV, vm)]
            for (mo, mw) in movs:
                bsq, w1 = take_bank()
                last_sq = None
                for it in range(8):
                    vs = S.emit("scalar", lambda e, it_=it, W_=W, m_=mo, w_=mw:
                                e.activation(
                                out=xsq[:, 0:w_],
                                in_=x_c[:, it_ * W_ + m_: it_ * W_ + m_ + w_],
                                func=AF.Square), inc=sS, waits=[(sT, S.val(sT))])
                    last_sq = mm(PSB[bsq][0:1, 0:mw], onescol[:, 0:1],
                                 xsq[:, 0:mw], start=(it == 0),
                                 stop=(it == 7),
                                 waits=[(sS, vs)] + (w1 if it == 0 else []))
                vq = S.emit("vector", lambda e, m_=mo, w_=mw, b=bsq, W_=W:
                    e.tensor_scalar(
                    stat[0:1, W_ + m_:W_ + m_ + w_], PSB[b][0:1, 0:w_], 1.0 / D,
                    None, ALU.mult), waits=[(sT, last_sq)], inc=sV)
                bank_free[bsq] = [(sV, vq)]
            S.emit("vector", lambda e, W_=W: e.scalar_tensor_tensor(
                stat[0:1, 2 * W_:3 * W_], stat[0:1, 0:W_], -1.0, stat[0:1, 0:W_],
                ALU.mult, ALU.mult), waits=[(sV, vm)], inc=sV)
            vv2 = S.emit("vector", lambda e, W_=W: e.tensor_add(
                stat[0:1, 2 * W_:3 * W_], stat[0:1, 2 * W_:3 * W_],
                stat[0:1, W_:2 * W_]), waits=[(sV, vq)], inc=sV)
            vv3 = S.emit("vector", lambda e, W_=W: e.tensor_scalar(
                stat[0:1, 2 * W_:3 * W_], stat[0:1, 2 * W_:3 * W_], EPS, None,
                ALU.add), inc=sV)
            vg = S.emit("vector", lambda e: e.engine_nop(),
                        waits=[(sV, vv3)], inc=sV)
            vl = S.emit("scalar", lambda e, W_=W: e.activation(
                out=stat[0:1, 3 * W_:4 * W_], in_=stat[0:1, 2 * W_:3 * W_],
                func=AF.Ln), waits=[(sV, vg)], inc=sS)
            vneg = S.emit("vector", lambda e, W_=W: e.tensor_scalar(
                stat[0:1, 3 * W_:4 * W_], stat[0:1, 3 * W_:4 * W_], -0.5, None,
                ALU.mult), waits=[(sS, vl)], inc=sV)
            vgx = S.emit("vector", lambda e: e.engine_nop(), waits=[(sV, vneg)], inc=sV)
            vr = S.emit("scalar", lambda e, W_=W: e.activation(
                out=stat[0:1, 3 * W_:4 * W_], in_=stat[0:1, 3 * W_:4 * W_],
                func=AF.Exp), waits=[(sV, vgx)], inc=sS)
            evs = []
            for (dst, row, prow) in ((mrep, 0, (sV, vm)), (rrep, 3, (sS, vr))):
                vb = S.emit("vector", lambda e, r_=row, W_=W: e.tensor_copy(
                    statb[0:1, 0:W_], stat[0:1, r_ * W_:r_ * W_ + W_]),
                    waits=[prow, (sT, S.val(sT))] +
                          ([(sS, evs[-1])] if evs else []), inc=sV)
                o2 = 0
                while o2 < W:
                    w2 = min(512, W - o2)
                    bk, wfree = take_bank()
                    vrp = mm(PSB[bk][:, 0:w2], onesrow,
                             statb[0:1, o2:o2 + w2], start=True, stop=True,
                             waits=[(sV, vb)] + wfree)
                    ve = S.emit("scalar", lambda e, d=dst, o_=o2, w_=w2, b=bk:
                                e.copy(d[:, o_:o_ + w_], PSB[b][:, 0:w_]),
                                waits=[(sT, vrp)], inc=sS)
                    bank_free[bk] = [(sS, ve)]
                    evs.append(ve)
                    o2 += w2
            wrep = (sS, evs[-1])
            lastv = None
            for it in range(8):
                S.emit("vector", lambda e, it_=it, W_=W: e.tensor_sub(
                    xln[:, it_ * W_: it_ * W_ + W_],
                    x_c[:, it_ * W_: it_ * W_ + W_], mrep[:, 0:W_]),
                    waits=[wrep, (sT, S.val(sT))] if it == 0 else [], inc=sV)
                lastv = S.emit("vector", lambda e, it_=it, W_=W: e.tensor_mul(
                    xln[:, it_ * W_: it_ * W_ + W_],
                    xln[:, it_ * W_: it_ * W_ + W_], rrep[:, 0:W_]), inc=sV)
            return (sV, lastv)

        def emit_ffn(W, movs, tgroups, cache_mode, final_chunk=None, dbg_dump=False):
            ln_done = emit_ln(W, movs)
            vr9 = S.emit("sync", lambda e: e.dma_start(
                out=wr9f.rearrange("p (k o) -> p k o", k=8),
                in_=w_r9.rearrange("k p o -> p k o")),
                waits=[(sV, S.val(sV))], inc=sDW, inc_n=16)
            vr9c = S.emit("vector", lambda e: e.tensor_copy(wr9, wr9f),
                          waits=[(sDW, vr9)], inc=sV)
            for (go, gw) in tgroups:
                bk, wfree = take_bank()
                lastg = None
                for kt in range(8):
                    lastg = mm(PSB[bk][0:gw, 0:9],
                               xln[:, kt * W + go: kt * W + go + gw],
                               wr9[:, kt * 9:(kt + 1) * 9],
                               start=(kt == 0), stop=(kt == 7),
                               waits=([ln_done, (sV, vr9c)] +
                                      wfree) if kt == 0 else [])
                g = gw
                cf = cw_f

                def vop(fn, waits=()):
                    return S.emit("vector", fn, waits=waits, inc=sV)

                def guard(v):
                    return S.emit("vector", lambda e: e.engine_nop(),
                                  waits=[(sV, v)], inc=sV)

                v1 = vop(lambda e, g=g, bk=bk: e.reduce_max(cf[0:g, 0:1], PSB[bk][0:g, 0:8],
                         axis=AX.X), waits=[(sT, lastg)])
                guard(v1)
                v2 = vop(lambda e, g=g, bk=bk: e.tensor_scalar(cf[0:g, 1:2], cf[0:g, 0:1],
                         -1.0, None, ALU.mult))
                guard(v2)
                v3 = S.emit("scalar", lambda e, b=bk, g=g: e.activation(
                    out=cf[0:g, 2:10], in_=PSB[b][0:g, 0:8], func=AF.Exp,
                    bias=cf[0:g, 1:2]), waits=[(sV, S.val(sV))], inc=sS)
                v4 = vop(lambda e, g=g, bk=bk: e.reduce_sum(cf[0:g, 10:11], cf[0:g, 2:10],
                         axis=AX.X), waits=[(sS, v3)])
                guard(v4)
                v5 = vop(lambda e, g=g, bk=bk: e.reciprocal(cf[0:g, 10:11], cf[0:g, 10:11]))
                guard(v5)
                guard(vop(lambda e, g=g, bk=bk: e.tensor_scalar(cf[0:g, 2:10], cf[0:g, 2:10],
                    cf[0:g, 10:11], None, ALU.mult)))
                v6 = vop(lambda e, g=g, bk=bk: e.reduce_max(cf[0:g, 0:1], cf[0:g, 2:10],
                         axis=AX.X))
                guard(v6)
                guard(vop(lambda e, g=g, bk=bk: e.tensor_scalar(cf[0:g, 30:31], cf[0:g, 0:1],
                    -1.0, None, ALU.mult)))
                guard(vop(lambda e, g=g, bk=bk: e.tensor_scalar(cf[0:g, 12:20], cf[0:g, 2:10],
                    cf[0:g, 30:31], None, ALU.add)))
                guard(vop(lambda e, g=g, bk=bk: e.tensor_scalar(cf[0:g, 12:20], cf[0:g, 12:20],
                    0.0, None, ALU.is_ge)))
                guard(vop(lambda e, g=g, bk=bk: e.tensor_scalar(cf[0:g, 20:28], cf[0:g, 12:20],
                    -1.0, 1.0, ALU.mult, ALU.add)))
                v8 = vop(lambda e, g=g, bk=bk: e.tensor_mul(cf[0:g, 20:28], cf[0:g, 20:28],
                         cf[0:g, 2:10]))
                guard(v8)
                v9 = vop(lambda e, g=g, bk=bk: e.reduce_max(cf[0:g, 1:2], cf[0:g, 20:28],
                         axis=AX.X))
                guard(v9)
                guard(vop(lambda e, g=g, bk=bk: e.tensor_scalar(cf[0:g, 30:31], cf[0:g, 1:2],
                    -1.0, None, ALU.mult)))
                guard(vop(lambda e, g=g, bk=bk: e.tensor_scalar(cf[0:g, 20:28], cf[0:g, 20:28],
                    cf[0:g, 30:31], None, ALU.add)))
                guard(vop(lambda e, g=g, bk=bk: e.tensor_scalar(cf[0:g, 20:28], cf[0:g, 20:28],
                    0.0, None, ALU.is_ge)))
                guard(vop(lambda e, g=g, bk=bk: e.tensor_scalar(cf[0:g, 12:20], cf[0:g, 12:20],
                    cf[0:g, 0:1], None, ALU.mult)))
                guard(vop(lambda e, g=g, bk=bk: e.tensor_scalar(cf[0:g, 20:28], cf[0:g, 20:28],
                    cf[0:g, 1:2], None, ALU.mult)))
                guard(vop(lambda e, g=g, bk=bk: e.tensor_add(cf[0:g, 12:20], cf[0:g, 12:20],
                    cf[0:g, 20:28])))
                va = vop(lambda e, g=g, bk=bk: e.tensor_add(cf[0:g, 0:1], cf[0:g, 0:1],
                         cf[0:g, 1:2]))
                guard(va)
                vb_ = vop(lambda e, g=g, bk=bk: e.reciprocal(cf[0:g, 0:1], cf[0:g, 0:1]))
                guard(vb_)
                vqc = vop(lambda e, g=g, bk=bk: e.tensor_scalar(cf[0:g, 12:20], cf[0:g, 12:20],
                          cf[0:g, 0:1], None, ALU.mult))
                guard(vqc)
                vs1 = S.emit("scalar", lambda e, b=bk, g=g: e.activation(
                    out=cf[0:g, 28:29], in_=PSB[b][0:g, 8:9], func=AF.Exp,
                    scale=-1.0), waits=[(sT, lastg)], inc=sS)
                vs2 = vop(lambda e, g=g, bk=bk: e.tensor_scalar(cf[0:g, 28:29], cf[0:g, 28:29],
                          1.0, None, ALU.add), waits=[(sS, vs1)])
                guard(vs2)
                vs3 = vop(lambda e, g=g, bk=bk: e.reciprocal(cf[0:g, 28:29], cf[0:g, 28:29]))
                guard(vs3)
                vcp = vop(lambda e, g=g, bk=bk: e.tensor_copy(cf[0:g, 20:21], cf[0:g, 28:29]),
                          waits=[(sV, vqc)])
                bank_free[bk] = [(sV, vcp), (sS, vs1)]
                if dbg_dump and DBG and go == 0:
                    S.emit("gpsimd", lambda e, g=g, bk=bk: e.dma_start(
                        out=dbg_out["cf"], in_=cf), waits=[(sV, vcp)],
                        inc=sDG, inc_n=16)
                    S.emit("scalar", lambda e, b=bk, g=g: e.copy(
                        e_sc[:, 0:9], PSB[b][0:128, 0:9]),
                        waits=[(sS, vs1)], inc=sS)
                    S.emit("gpsimd", lambda e, g=g, bk=bk: e.dma_start(
                        out=dbg_out["lg"], in_=e_sc[:, 0:9]),
                        waits=[(sS, S.val(sS))], inc=sDG, inc_n=16)
                for grp0 in range(0, 9, 4):
                    nrow = min(4, 9 - grp0)
                    bk2, wfree2 = take_bank()
                    vtp = None
                    for rr in range(nrow):
                        vtp = S.emit("tensor", lambda e, b=bk2, g_=g, r_=grp0 + rr,
                                     c2=rr * 128: e.transpose(
                                     PSB[b][0:1, c2:c2 + g_],
                                     cf[0:g_, 12 + r_:13 + r_],
                                     identb[0:g_, 0:g_]),
                                     waits=([(sV, vcp)] +
                                            wfree2)
                                     if rr == 0 else [], inc=sT)
                    vev = S.emit("scalar", lambda e, b=bk2, go_=go, g_=g, W_=W,
                                 g0=grp0, nr=nrow: e.copy(
                                 cwT[0:1, g0 * W_: (g0 + nr) * W_].rearrange(
                                     "p (r w) -> p r w", r=nr)[:, :, go_:go_ + g_],
                                 PSB[b][0:1, 0:nr * 128].rearrange(
                                     "p (r w) -> p r w", r=nr)[:, :, 0:g_]),
                                 waits=[(sT, vtp)], inc=sS)
                    bank_free[bk2] = [(sS, vev)]
            ev_cw = (sS, S.val(sS))
            reps = []
            for row in range(9):
                o2 = 0
                while o2 < W:
                    w2 = min(512, W - o2)
                    bk, wfree = take_bank()
                    vrp = mm(PSB[bk][:, 0:w2], onesrow,
                             cwT[0:1, row * W + o2: row * W + o2 + w2],
                             start=True, stop=True,
                             waits=[ev_cw] + wfree)
                    ve = S.emit("scalar", lambda e, r_=row, o_=o2, w_=w2, b=bk:
                                e.copy(cwrep[:, r_ * W + o_: r_ * W + o_ + w_],
                                       PSB[b][:, 0:w_]),
                                waits=[(sT, vrp)], inc=sS)
                    bank_free[bk] = [(sS, ve)]
                    reps.append(ve)
                    o2 += w2
            cw_ready = (sS, reps[-1])
            if dbg_dump and DBG:
                S.emit("gpsimd", lambda e: e.dma_start(out=dbg_out["xln"], in_=xln),
                       waits=[ln_done], inc=sDG, inc_n=16)
                S.emit("gpsimd", lambda e: e.dma_start(out=dbg_out["cwt"], in_=cwT),
                       waits=[ev_cw], inc=sDG, inc_n=16)

            for ot in range(16):
                wb_, wr_, sl = load_wtile(w_sg[ot] if (cache_mode or NOCACHE) else None,
                                          None if (cache_mode or NOCACHE) else c_sg[ot], 8,
                                          c_sg[ot] if (cache_mode or NOCACHE) else None)
                for (mo, mw) in movs:
                    bg, fg = take_bank()
                    lg = None
                    for kt in range(8):
                        lg = mm(PSB[bg][:, 0:mw], wb_[:, kt * 128:(kt + 1) * 128],
                                xln[:, kt * W + mo: kt * W + mo + mw],
                                start=(kt == 0), stop=(kt == 7),
                                waits=([wr_, ln_done] + fg)
                                if kt == 0 else [])
                    vsl = S.emit("scalar", lambda e, b=bg, w_=mw, o_=ot * W + mo:
                                 e.activation(out=sact[:, o_:o_ + w_],
                                              in_=PSB[b][:, 0:w_], func=AF.Silu),
                                 waits=[(sT, lg)] +
                                       ([cw_ready] if ot >= 7 else []), inc=sS)
                    bank_free[bg] = [(sS, vsl)]
                free_wslot(sl)
            for ot in range(16):
                wb_, wr_, sl = load_wtile(w_su[ot] if (cache_mode or NOCACHE) else None,
                                          None if (cache_mode or NOCACHE) else c_su[ot], 8,
                                          c_su[ot] if (cache_mode or NOCACHE) else None)
                for (mo, mw) in movs:
                    bu, fu = take_bank()
                    lu = None
                    for kt in range(8):
                        lu = mm(PSB[bu][:, 0:mw], wb_[:, kt * 128:(kt + 1) * 128],
                                xln[:, kt * W + mo: kt * W + mo + mw],
                                start=(kt == 0), stop=(kt == 7),
                                waits=([wr_] + fu) if kt == 0 else [])
                    v1 = S.emit("vector", lambda e, b=bu, o_=ot * W + mo, w_=mw:
                                e.tensor_mul(sact[:, o_:o_ + w_],
                                             sact[:, o_:o_ + w_], PSB[b][:, 0:w_]),
                                waits=[(sT, lu), (sS, S.val(sS))], inc=sV)
                    if mw < 128:
                        S.emit("vector", lambda e: e.engine_nop(),
                               waits=[(sV, v1)], inc=sV)
                    S.emit("vector", lambda e, o_=ot * W + mo, mo_=mo, w_=mw:
                           e.tensor_mul(sact[:, o_:o_ + w_], sact[:, o_:o_ + w_],
                                        cwrep[:, 8 * W + mo_: 8 * W + mo_ + w_]),
                           waits=[cw_ready], inc=sV)
                    bank_free[bu] = [(sV, v1)]
                free_wslot(sl)

            for ex in range(8):
                for op in range(4):
                    wg_, wgr_, slg = load_wtile(
                        w_gu[ex][op] if (cache_mode or NOCACHE) else None,
                        None if (cache_mode or NOCACHE) else c_gu[ex][op], 8,
                        c_gu[ex][op] if (cache_mode or NOCACHE) else None)
                    wu_, wur_, slu = load_wtile(
                        w_gu[ex][op + 4] if (cache_mode or NOCACHE) else None,
                        None if (cache_mode or NOCACHE) else c_gu[ex][op + 4], 8,
                        c_gu[ex][op + 4] if (cache_mode or NOCACHE) else None)
                    for (mo, mw) in movs:
                        bg, fg = take_bank()
                        bu, fu = take_bank()
                        lg = lu = None
                        for kt in range(8):
                            lg = mm(PSB[bg][:, 0:mw],
                                    wg_[:, kt * 128:(kt + 1) * 128],
                                    xln[:, kt * W + mo: kt * W + mo + mw],
                                    start=(kt == 0), stop=(kt == 7),
                                    waits=([wgr_, ln_done] + fg)
                                    if kt == 0 else [])
                        for kt in range(8):
                            lu = mm(PSB[bu][:, 0:mw],
                                    wu_[:, kt * 128:(kt + 1) * 128],
                                    xln[:, kt * W + mo: kt * W + mo + mw],
                                    start=(kt == 0), stop=(kt == 7),
                                    waits=([wur_] + fu)
                                    if kt == 0 else [])
                        vsl = S.emit("scalar", lambda e, b=bg, w_=mw: e.activation(
                            out=sgs[:, 0:w_], in_=PSB[b][:, 0:w_], func=AF.Silu),
                            waits=[(sT, lg), (sV, S.val(sV))], inc=sS)
                        slot = ex * 4 + op
                        v1 = S.emit("vector", lambda e, b=bu, s_=slot * W + mo, w_=mw:
                                    e.tensor_mul(acts[:, s_:s_ + w_], sgs[:, 0:w_],
                                                 PSB[b][:, 0:w_]),
                                    waits=[(sS, vsl), (sT, lu)], inc=sV)
                        if mw < 128:
                            S.emit("vector", lambda e: e.engine_nop(),
                                   waits=[(sV, v1)], inc=sV)
                        S.emit("vector", lambda e, s_=slot * W + mo, mo_=mo, w_=mw,
                               ex_=ex: e.tensor_mul(
                               acts[:, s_:s_ + w_], acts[:, s_:s_ + w_],
                               cwrep[:, ex_ * W + mo_: ex_ * W + mo_ + w_]),
                               waits=[cw_ready], inc=sV)
                        bank_free[bg] = [(sV, v1)]
                        bank_free[bu] = [(sV, v1)]
                    free_wslot(slg)
                    free_wslot(slu)
            acts_ready = (sV, S.val(sV))
            if dbg_dump and DBG:
                S.emit("gpsimd", lambda e: e.dma_start(out=dbg_out["sact"], in_=sact),
                       waits=[acts_ready], inc=sDG, inc_n=16)
                S.emit("gpsimd", lambda e: e.dma_start(out=dbg_out["acts"], in_=acts),
                       waits=[acts_ready], inc=sDG, inc_n=16)

            for ot in range(8):
                wb_, wr_, sl = load_wtile(w_sd[ot] if (cache_mode or NOCACHE) else None,
                                          None if (cache_mode or NOCACHE) else c_sd[ot], 16,
                                          c_sd[ot] if (cache_mode or NOCACHE) else None)
                banks = []
                for (mo, mw) in movs:
                    bo, fo = take_bank()
                    banks.append((bo, mo, mw))
                    for kt in range(16):
                        mm(PSB[bo][:, 0:mw], wb_[:, kt * 128:(kt + 1) * 128],
                           sact[:, kt * W + mo: kt * W + mo + mw],
                           start=(kt == 0), stop=False,
                           waits=([wr_, acts_ready] + fo)
                           if kt == 0 else [])
                free_wslot(sl)
                lastd = None
                for ex in range(8):
                    wb2, wr2, sl2 = load_wtile(
                        w_dn[ex][ot] if (cache_mode or NOCACHE) else None,
                        None if (cache_mode or NOCACHE) else c_dn[ex][ot], 4,
                        c_dn[ex][ot] if (cache_mode or NOCACHE) else None)
                    for (bo, mo, mw) in banks:
                        for kt in range(4):
                            slot = ex * 4 + kt
                            lastd = mm(PSB[bo][:, 0:mw],
                                       wb2[:, kt * 128:(kt + 1) * 128],
                                       acts[:, slot * W + mo: slot * W + mo + mw],
                                       start=False, stop=(ex == 7 and kt == 3),
                                       waits=[wr2] if kt == 0 else [])
                    free_wslot(sl2)
                for (bo, mo, mw) in banks:
                    if final_chunk is None:
                        vep = S.emit("vector", lambda e, b=bo, o_=ot * W + mo, w_=mw:
                                     e.scalar_tensor_tensor(
                                     x_c[:, o_:o_ + w_], PSB[b][:, 0:w_], 0.5,
                                     x_c[:, o_:o_ + w_], ALU.mult, ALU.add),
                                     waits=[(sT, lastd)], inc=sV)
                        bank_free[bo] = [(sV, vep)]
                    else:
                        vep = S.emit("vector", lambda e, b=bo, o_=ot * W + mo,
                                     mo_=mo, w_=mw: e.scalar_tensor_tensor(
                                     of32[:, mo_:mo_ + w_], PSB[b][:, 0:w_], 0.5,
                                     x_c[:, o_:o_ + w_], ALU.mult, ALU.add),
                                     waits=[(sT, lastd), (sFin, S.val(sFin))],
                                     inc=sV)
                        bank_free[bo] = [(sV, vep)]
                        S.emit("gpsimd", lambda e, o_=ot, mo_=mo, w_=mw, c_=final_chunk:
                               e.dma_start(out=out_t[o_ * 128:(o_ + 1) * 128,
                                                     c_ * TCA + mo_: c_ * TCA + mo_ + w_],
                                           in_=of32[:, mo_:mo_ + w_]),
                               waits=[(sV, vep)], inc=sFin, inc_n=16)

        ag_d, ag_u, ag_b = [], [], []
        for c in range(NCA):
            cache_bar[0] = [(sDS, S.val(sDS))]
            K0, W = c * TCA, WA
            movs = [(0, 512), (512, 512), (1024, 8)]
            tgroups = [(i * 128, 128) for i in range(8)] + [(1024, 8)]
            own0 = HP
            xz_off = 0
            xz_movs = movs
            qw = W // 4
            for q in range(4):
                vld = S.emit("sync", lambda e, K_=K0 + q * qw, w_=qw: e.dma_start(
                    out=xf32[:, 0:8 * w_].rearrange("p (t w) -> p t w", t=8),
                    in_=x_in[:, K_:K_ + w_].rearrange("(t p) w -> p t w", p=128)),
                    waits=(barrier_waits() if q == 0 else [(sV, S.val(sV))]),
                    inc=sDW, inc_n=16)
                S.emit("vector", lambda e, o_=q * qw, w_=qw, W_=W: e.tensor_copy(
                    x_c[:, 0:8 * W_].rearrange("p (t w) -> p t w", t=8)[:, :, o_:o_ + w_],
                    xf32[:, 0:8 * w_].rearrange("p (t w) -> p t w", t=8)),
                    waits=[(sDW, vld)] + (prep_done if c == 0 and q == 0 else []),
                    inc=sV)
            emit_ffn(W, movs, tgroups, cache_mode=(c == 0), final_chunk=None, dbg_dump=(c == 0))
            S.emit("gpsimd", lambda e, c_=c, W_=W, o_=own0: e.dma_start(
                out=x1_sp[:, c_ * TCA:(c_ + 1) * TCA].rearrange(
                    "(t p) w -> p t w", p=128),
                in_=x_c[:, 0:8 * W_].rearrange("p (t w) -> p t w", t=8)[:, :, o_:o_ + TCA]),
                waits=[(sV, S.val(sV))], inc=sDG, inc_n=16)
            ln2 = emit_ln(W, movs)
            xz_gate = (sT, S.val(sT))
            for ot in range(16):
                wb_, wr_, sl = load_wtile(w_in[ot] if (c == 0 or NOCACHE) else None,
                                          None if (c == 0 or NOCACHE) else c_in[ot], 8,
                                          c_in[ot] if (c == 0 or NOCACHE) else None)
                for (mo, mw) in xz_movs:
                    bz, fz = take_bank()
                    lz = None
                    for kt in range(8):
                        lz = mm(PSB[bz][:, 0:mw], wb_[:, kt * 128:(kt + 1) * 128],
                                xln[:, kt * W + mo: kt * W + mo + mw],
                                start=(kt == 0), stop=(kt == 7),
                                waits=([wr_, ln2, xz_gate] + fz)
                                if kt == 0 else [])
                    vz = S.emit("scalar", lambda e, b=bz, d_=ot * WA + xz_off + mo,
                                w_=mw, o_=ot: e.activation(
                                out=xz[:, d_:d_ + w_], in_=PSB[b][:, 0:w_],
                                func=AF.Identity, bias=biasq[:, o_:o_ + 1]),
                                waits=[(sT, lz)], inc=sS)
                    bank_free[bz] = [(sS, vz)]
                free_wslot(sl)
            xz_done = (sS, S.val(sS))
            if c == 0:
                vmk = None
                for ot in range(16):
                    vmk = S.emit("vector", lambda e, o_=ot: e.tensor_mul(
                        xz[:, o_ * WA: o_ * WA + HP], xz[:, o_ * WA: o_ * WA + HP],
                        maskb), waits=[xz_done] if ot == 0 else [], inc=sV)
                xz_done = (sV, vmk)

            for ot in range(16):
                wb_, wr_, sl = load_wtile(w_gt[ot] if (c == 0 or NOCACHE) else None,
                                          None if (c == 0 or NOCACHE) else c_gt[ot], 8,
                                          c_gt[ot] if (c == 0 or NOCACHE) else None)
                for gi in range(2):
                    mo = own0 + gi * 512
                    bz, fz = take_bank()
                    lz = None
                    for kt in range(8):
                        lz = mm(PSB[bz][:, 0:512], wb_[:, kt * 128:(kt + 1) * 128],
                                xln[:, kt * W + mo: kt * W + mo + 512],
                                start=(kt == 0), stop=(kt == 7),
                                waits=([wr_] + fz) if kt == 0 else [])
                    vz = S.emit("scalar", lambda e, b=bz, o_=ot: e.activation(
                        out=sgs[:, 0:512], in_=PSB[b][:, 0:512], func=AF.Silu,
                        bias=biasq[:, 16 + o_:16 + o_ + 1]),
                        waits=[(sT, lz), (sDS, S.val(sDS))], inc=sS)
                    bank_free[bz] = [(sS, vz)]
                    S.emit("scalar", lambda e, c_=c, o_=ot, g_=gi: e.dma_start(
                        out=g_sp[o_ * 128:(o_ + 1) * 128,
                                 c_ * TCA + g_ * 512: c_ * TCA + g_ * 512 + 512],
                        in_=sgs[:, 0:512]), waits=[(sS, vz)], inc=sDS, inc_n=16)
                free_wslot(sl)

            for ot in range(16):
                wcv_, wcvr_, slcv = load_wtile(w_cv[ot] if (c == 0 or NOCACHE) else None,
                                               None if (c == 0 or NOCACHE) else c_cv[ot], 4,
                                               c_cv[ot] if (c == 0 or NOCACHE) else None)
                for gi in range(2):
                    mo = HP + gi * 512
                    bz, fz = take_bank()
                    lz = None
                    for k in range(4):
                        sft = 3 - k
                        lz = mm(PSB[bz][:, 0:512],
                                wcv_[:, k * 128:(k + 1) * 128],
                                xz[:, ot * WA + mo - sft: ot * WA + mo - sft + 512],
                                start=(k == 0), stop=(k == 3),
                                waits=([xz_done, wcvr_] + fz)
                                if k == 0 else [])
                    va = S.emit("scalar", lambda e, b=bz, o_=ot, g_=gi: e.activation(
                        out=actm[:, o_ * TCA + g_ * 512: o_ * TCA + g_ * 512 + 512],
                        in_=PSB[b][:, 0:512], func=AF.Silu,
                        bias=biasq[:, 48 + o_:48 + o_ + 1]),
                        waits=[(sT, lz)], inc=sS)
                    bank_free[bz] = [(sS, va)]
                    S.emit("gpsimd", lambda e, c_=c, o_=ot, g_=gi: e.dma_start(
                        out=act_sp[o_ * 128:(o_ + 1) * 128,
                                   c_ * TCA + g_ * 512: c_ * TCA + g_ * 512 + 512],
                        in_=actm[:, o_ * TCA + g_ * 512: o_ * TCA + g_ * 512 + 512]),
                        waits=[(sS, va)], inc=sDG, inc_n=16)
                free_wslot(slcv)
            act_done = (sS, S.val(sS))

            for ot in range(16):
                wb_, wr_, sl = load_wtile(w_dl[ot] if (c == 0 or NOCACHE) else None,
                                          None if (c == 0 or NOCACHE) else c_dl[ot], 16,
                                          c_dl[ot] if (c == 0 or NOCACHE) else None)
                for gi in range(2):
                    mo = gi * 512
                    bz, fz = take_bank()
                    lz = None
                    for kt in range(16):
                        lz = mm(PSB[bz][:, 0:512], wb_[:, kt * 128:(kt + 1) * 128],
                                actm[:, kt * TCA + mo: kt * TCA + mo + 512],
                                start=(kt == 0), stop=(kt == 15),
                                waits=([wr_, act_done] + fz)
                                if kt == 0 else [])
                    v1 = S.emit("scalar", lambda e, b=bz, o_=ot: e.activation(
                        out=e_sc, in_=PSB[b][:, 0:512], func=AF.Exp,
                        bias=biasq[:, 32 + o_:32 + o_ + 1]),
                        waits=[(sT, lz), (sDG, S.val(sDG)), (sV, S.val(sV))],
                        inc=sS)
                    v2 = S.emit("scalar", lambda e, m_=mo: e.activation(
                        out=dlt[:, m_:m_ + 512], in_=e_sc, func=AF.Ln, bias=1.0),
                        inc=sS)
                    bank_free[bz] = [(sS, v2)]
                S.emit("gpsimd", lambda e, c_=c, o_=ot: e.dma_start(
                    out=cc_di[c_][o_ * 128:(o_ + 1) * 128, :], in_=dlt),
                    waits=[(sS, S.val(sS))], inc=sDG, inc_n=16)
                vu = S.emit("vector", lambda e, o_=ot: e.tensor_mul(
                    ut, dlt, actm[:, o_ * TCA:(o_ + 1) * TCA]),
                    waits=[(sS, S.val(sS)), (sDG, S.val(sDG))], inc=sV)
                S.emit("gpsimd", lambda e, c_=c, o_=ot: e.dma_start(
                    out=cc_ui[c_][o_ * 128:(o_ + 1) * 128, :], in_=ut),
                    waits=[(sV, vu)], inc=sDG, inc_n=16)
                free_wslot(sl)

            for gi in range(2):
                mo = gi * 512
                bz, fz = take_bank()
                lz = None
                for kt in range(16):
                    lz = mm(PSB[bz][0:32, 0:512], wbc_b[:, kt * 32:(kt + 1) * 32],
                            actm[:, kt * TCA + mo: kt * TCA + mo + 512],
                            start=(kt == 0), stop=(kt == 15),
                            waits=([act_done, (sV, vbc0)] +
                                   fz) if kt == 0 else [])
                vz = S.emit("scalar", lambda e, b=bz, m_=mo: e.copy(
                    bct[:, m_:m_ + 512], PSB[b][0:32, 0:512]),
                    waits=[(sT, lz), (sDG, S.val(sDG))], inc=sS)
                bank_free[bz] = [(sS, vz)]
            S.emit("gpsimd", lambda e, c_=c: e.dma_start(
                out=cc_bi[c_], in_=bct), waits=[(sS, S.val(sS))], inc=sDG, inc_n=16)

            ag_d.append(S.emit("gpsimd", lambda e, c_=c: e.collective_compute(
                "AllGather", ALU.bypass, ins=[cc_di[c_]], outs=[cc_do[c_]],
                replica_groups=RG), waits=[(sDG, S.val(sDG))], inc=sG))
            ag_u.append(S.emit("gpsimd", lambda e, c_=c: e.collective_compute(
                "AllGather", ALU.bypass, ins=[cc_ui[c_]], outs=[cc_uo[c_]],
                replica_groups=RG), inc=sG))
            ag_b.append(S.emit("gpsimd", lambda e, c_=c: e.collective_compute(
                "AllGather", ALU.bypass, ins=[cc_bi[c_]], outs=[cc_bo[c_]],
                replica_groups=RG), inc=sG))

        rs_done = [None, None]
        dl_free = [None, None]
        u_free = [None, None]
        a_free = [None, None]
        y_free = None
        rep_guard = [None, None]
        mul_last = None
        free_ph = None
        free_p = None
        for s_ in range(NCB):
            j = s_ // 2
            lc = 2 * (s_ % 2)
            half = s_ % 2
            ag_wait = [(sG, ag_b[lc + 1])]
            if s_ == 0:
                ag_wait += barrier_waits()
            vb1 = S.emit("sync", lambda e, l_=lc, j_=j: e.dma_start(
                out=bc_g[:, 0:TCA], in_=cc_bo[l_][j_]),
                waits=ag_wait + [(sT, S.val(sT))], inc=sBC, inc_n=16)
            vb2 = S.emit("sync", lambda e, l_=lc, j_=j: e.dma_start(
                out=bc_g[:, TCA:TCB], in_=cc_bo[l_ + 1][j_]), inc=sBC, inc_n=16)
            rep_last = None
            first_rep = True
            for nn in range(4):
                for (dst, row) in ((brep, nn), (crep, 4 + nn)):
                    for sp in range(4):
                        bk, fk = take_bank()
                        vrp = mm(PSB[bk][:, 0:512],
                                 selb[:, row * 128:(row + 1) * 128],
                                 bc_g[:, sp * 512:(sp + 1) * 512],
                                 start=True, stop=True,
                                 waits=[(sBC, vb2)] + fk)
                        gw = ([(sV, S.val(sV)), (sP, S.val(sP))]
                              if first_rep else [])
                        first_rep = False
                        rep_last = S.emit("scalar", lambda e, d=dst, n_=nn, s2=sp,
                                          b=bk: e.copy(
                                          d[:, n_ * TCB + s2 * 512:
                                            n_ * TCB + s2 * 512 + 512],
                                          PSB[b][:, 0:512]),
                                          waits=[(sT, vrp)] + gw, inc=sS)
                        bank_free[bk] = [(sS, rep_last)]
            reps_ok = (sS, rep_last)
            if DBG and s_ == 0:
                S.emit("gpsimd", lambda e: e.dma_start(out=dbg_out["brep"], in_=brepb[0]),
                       waits=[reps_ok], inc=sDG, inc_n=16)
                S.emit("gpsimd", lambda e: e.dma_start(out=dbg_out["crep"], in_=crepb[0]),
                       waits=[reps_ok], inc=sDG, inc_n=16)

            for dt in range(16):
                sl = dt % 2
                sB = sB0 if sl == 0 else sB1
                vd1 = S.emit("sync", lambda e, l_=lc, j_=j, d_=dt, s2=sl: e.dma_start(
                    out=dl_g[:, s2 * TCB: s2 * TCB + TCA],
                    in_=cc_do[l_][j_][d_ * 128:(d_ + 1) * 128, :]),
                    waits=[dl_free[sl], (sG, ag_d[lc + 1])], inc=sB, inc_n=16)
                S.emit("sync", lambda e, l_=lc, j_=j, d_=dt, s2=sl: e.dma_start(
                    out=dl_g[:, s2 * TCB + TCA: (s2 + 1) * TCB],
                    in_=cc_do[l_ + 1][j_][d_ * 128:(d_ + 1) * 128, :]),
                    inc=sB, inc_n=16)
                S.emit("sync", lambda e, l_=lc, j_=j, d_=dt, s2=sl: e.dma_start(
                    out=u_g[:, s2 * TCB: s2 * TCB + TCA],
                    in_=cc_uo[l_][j_][d_ * 128:(d_ + 1) * 128, :]),
                    waits=[u_free[sl], (sG, ag_u[lc + 1])], inc=sB, inc_n=16)
                vu2 = S.emit("sync", lambda e, l_=lc, j_=j, d_=dt, s2=sl: e.dma_start(
                    out=u_g[:, s2 * TCB + TCA: (s2 + 1) * TCB],
                    in_=cc_uo[l_ + 1][j_][d_ * 128:(d_ + 1) * 128, :]),
                    inc=sB, inc_n=16)
                dma_ok = (sB, vu2)
                last_a = None
                last_b = None
                for nn in range(4):
                    idx = dt * 4 + nn
                    asl = idx % 2
                    va = S.emit("scalar", lambda e, s2=sl, a2=asl, i_=idx: e.activation(
                        out=a_t[:, a2 * TCB:(a2 + 1) * TCB],
                        in_=dl_g[:, s2 * TCB:(s2 + 1) * TCB], func=AF.Exp,
                        scale=a_sc[:, i_:i_ + 1]),
                        waits=[dma_ok, a_free[asl]], inc=sS)
                    last_a = va
                    vbm = S.emit("vector", lambda e, s2=sl, a2=asl, n_=nn,
                                 h2=half: e.tensor_mul(
                        b_t[:, a2 * TCB:(a2 + 1) * TCB],
                        u_g[:, s2 * TCB:(s2 + 1) * TCB],
                        brep[:, n_ * TCB:(n_ + 1) * TCB]),
                        waits=[dma_ok, reps_ok], inc=sV)
                    last_b = vbm
                    init_ap = 0.0 if s_ == 0 else h_last[:, idx:idx + 1]
                    vsc = S.emit("vector", lambda e, a2=asl, ini=init_ap:
                                 e.tensor_tensor_scan(
                                 out=h_t, data0=a_t[:, a2 * TCB:(a2 + 1) * TCB],
                                 data1=b_t[:, a2 * TCB:(a2 + 1) * TCB],
                                 initial=ini,
                                 op0=ALU.mult, op1=ALU.add),
                                 waits=[(sS, va)], inc=sV)
                    a_free[asl] = (sV, vsc)
                    if DBG and s_ == 0 and dt == 0 and nn == 0:
                        S.emit("gpsimd", lambda e, a2=asl: e.dma_start(
                            out=dbg_out["a0f"], in_=a_t[:, a2 * TCB:(a2 + 1) * TCB]),
                            waits=[(sV, vsc)], inc=sDG, inc_n=16)
                        S.emit("gpsimd", lambda e, a2=asl: e.dma_start(
                            out=dbg_out["b0"], in_=b_t[:, a2 * TCB:(a2 + 1) * TCB]),
                            waits=[(sV, vsc)], inc=sDG, inc_n=16)
                        vdmp = S.emit("gpsimd", lambda e: e.dma_start(
                            out=dbg_out["h0"], in_=h_t), waits=[(sV, vsc)],
                            inc=sDG, inc_n=16)
                        S.emit("vector", lambda e: e.engine_nop(),
                               waits=[(sDG, vdmp)], inc=sV)
                    S.emit("vector", lambda e: e.engine_nop(),
                           waits=[(sV, vsc)], inc=sV)
                    S.emit("vector", lambda e, i_=idx: e.tensor_copy(
                        h_last[:, i_:i_ + 1], h_t[:, TCB - 1:TCB]), inc=sV)
                    dst = ph_t if nn % 2 == 0 else p_t
                    S.emit("vector", lambda e, d=dst, n_=nn: e.tensor_mul(
                        d, h_t, crep[:, n_ * TCB:(n_ + 1) * TCB]), inc=sV)
                    if nn == 1:
                        S.emit("vector", lambda e: e.tensor_add(s1_t, ph_t, p_t),
                               inc=sV)
                    elif nn == 3:
                        S.emit("vector", lambda e: e.tensor_add(p_t, ph_t, p_t),
                               inc=sV)
                        vy = S.emit("vector", lambda e: e.tensor_add(y_t, s1_t, p_t),
                                    waits=[y_free], inc=sV)
                if DBG and s_ == 0 and dt == 0:
                    vdy = S.emit("gpsimd", lambda e: e.dma_start(
                        out=dbg_out["y0"], in_=y_t), waits=[(sV, vy)],
                        inc=sDG, inc_n=16)
                    S.emit("vector", lambda e: e.engine_nop(),
                           waits=[(sDG, vdy)], inc=sV)
                dl_free[sl] = (sS, last_a)
                u_free[sl] = (sV, last_b)
                y_free = (sDS, S.emit("scalar",
                          lambda e, h_=half, j_=j, d_=dt: e.dma_start(
                          out=cc_yp[h_][d_ // 4][j_, (d_ % 4) * 128:
                                                  (d_ % 4 + 1) * 128, :],
                          in_=y_t),
                          waits=[(sV, vy)], inc=sDS, inc_n=16))
                if s_ == 7 and dt % 4 == 3:
                    for hh in range(2):
                        rs_done[hh] = S.emit("gpsimd",
                            lambda e, h2=hh, q2=dt // 4: e.collective_compute(
                            "ReduceScatter", ALU.add, ins=[cc_yp[h2][q2]],
                            outs=[cc_yr[h2][q2]], replica_groups=RG),
                            waits=[(sDG, S.val(sDG)), (sDS, S.val(sDS))], inc=sG)
            rep_guard[half] = (sV, S.val(sV))

        for c in range(NCC):
            cache_bar[0] = [(sDS, S.val(sDS))]
            h_ = c // 2
            off = (c % 2) * TCC
            vy_ = None
            for q in range(4):
                vy_ = S.emit("sync", lambda e, h2=h_, o_=off, q_=q: e.dma_start(
                    out=yc.rearrange("p (t w) -> p t w", t=16)[:, q_ * 4:
                                                              (q_ + 1) * 4, :],
                    in_=cc_yr[h2][q_].rearrange(
                        "(t p) w -> p t w", p=128)[:, :, o_:o_ + TCC]),
                    waits=([(sG, rs_done[h_])] + barrier_waits())
                    if q == 0 else [], inc=sDW, inc_n=16)
            S.emit("sync", lambda e, c_=c: e.dma_start(
                out=actc.rearrange("p (t w) -> p t w", t=16),
                in_=act_sp[:, c_ * TCC:(c_ + 1) * TCC].rearrange(
                    "(t p) w -> p t w", p=128)), inc=sDW, inc_n=16)
            S.emit("sync", lambda e, c_=c: e.dma_start(
                out=gc.rearrange("p (t w) -> p t w", t=16),
                in_=g_sp[:, c_ * TCC:(c_ + 1) * TCC].rearrange(
                    "(t p) w -> p t w", p=128)), inc=sDW, inc_n=16)
            vx_ = S.emit("sync", lambda e, c_=c: e.dma_start(
                out=x1c.rearrange("p (t w) -> p t w", t=8),
                in_=x1_sp[:, c_ * TCC:(c_ + 1) * TCC].rearrange(
                    "(t p) w -> p t w", p=128)), inc=sDW, inc_n=16)
            dok = (sDW, vx_)
            for dt in range(16):
                S.emit("vector", lambda e, d_=dt: e.scalar_tensor_tensor(
                    yc[:, d_ * TCC:(d_ + 1) * TCC], actc[:, d_ * TCC:(d_ + 1) * TCC],
                    dskq[:, d_:d_ + 1], yc[:, d_ * TCC:(d_ + 1) * TCC],
                    ALU.mult, ALU.add), waits=[dok] if dt == 0 else [], inc=sV)
                S.emit("vector", lambda e, d_=dt: e.tensor_mul(
                    ycb[:, d_ * TCC:(d_ + 1) * TCC], yc[:, d_ * TCC:(d_ + 1) * TCC],
                    gc[:, d_ * TCC:(d_ + 1) * TCC]), inc=sV)
            gat = (sV, S.val(sV))
            for ot in range(8):
                wb_, wr_, sl = load_wtile(w_ot[ot] if (c == 0 or NOCACHE) else None,
                                          None if (c == 0 or NOCACHE) else c_ot[ot], 16,
                                          c_ot[ot] if (c == 0 or NOCACHE) else None)
                for gi in range(2):
                    mo = gi * 512
                    bz, fz = take_bank()
                    lz = None
                    for kt in range(16):
                        lz = mm(PSB[bz][:, 0:512], wb_[:, kt * 128:(kt + 1) * 128],
                                ycb[:, kt * TCC + mo: kt * TCC + mo + 512],
                                start=(kt == 0), stop=(kt == 15),
                                waits=([wr_, gat] + fz)
                                if kt == 0 else [])
                    vep = S.emit("vector", lambda e, b=bz, o_=ot, m_=mo:
                                 e.scalar_tensor_tensor(
                                 x2st[:, m_:m_ + 512], PSB[b][:, 0:512],
                                 biasq[:, 64 + o_:64 + o_ + 1],
                                 x1c[:, o_ * TCC + m_: o_ * TCC + m_ + 512],
                                 ALU.add, ALU.add),
                                 waits=[(sT, lz), (sDS, S.val(sDS))], inc=sV)
                    bank_free[bz] = [(sV, vep)]
                    S.emit("scalar", lambda e, c_=c, o_=ot, m_=mo: e.dma_start(
                        out=x2_sp[o_ * 128:(o_ + 1) * 128,
                                  c_ * TCC + m_: c_ * TCC + m_ + 512],
                        in_=x2st[:, m_:m_ + 512]),
                        waits=[(sV, vep)], inc=sDS, inc_n=16)
                free_wslot(sl)
        x2_all = (sDS, S.val(sDS))

        for c in range(NCA):
            cache_bar[0] = [(sDS, S.val(sDS))]
            W = TCA
            movs = [(0, 512), (512, 512)]
            tgroups = [(i * 128, 128) for i in range(8)]
            vld = S.emit("sync", lambda e, c_=c: e.dma_start(
                out=x_c[:, 0:8 * W].rearrange("p (t w) -> p t w", t=8),
                in_=x2_sp[:, c_ * TCA:(c_ + 1) * TCA].rearrange(
                    "(t p) w -> p t w", p=128)),
                waits=[x2_all] + barrier_waits(), inc=sDW, inc_n=16)
            S.emit("vector", lambda e: e.engine_nop(), waits=[(sDW, vld)], inc=sV)
            emit_ffn(W, movs, tgroups, cache_mode=False, final_chunk=c)

        if DBG:
            S.emit("gpsimd", lambda e: e.dma_start(out=dbg_out["asc"], in_=a_sc),
                   waits=[(sFin, S.val(sFin))], inc=sDG, inc_n=16)
            S.emit("gpsimd", lambda e: e.dma_start(out=dbg_out["hl"], in_=h_last),
                   waits=[(sFin, S.val(sFin))], inc=sDG, inc_n=16)
            pairs = [("do0", cc_do[0].rearrange("s d w -> (s d) w")),
                     ("uo0", cc_uo[0].rearrange("s d w -> (s d) w")),
                     ("x1", x1_sp), ("act", act_sp), ("gact", g_sp),
                     ("delta", cc_di[0]), ("u", cc_ui[0]), ("bc", cc_bi[0]),
                     ("x2", x2_sp)]
            for nm, src in pairs:
                S.emit("gpsimd", lambda e, d=dbg_out[nm], s2=src: e.dma_start(
                    out=d, in_=s2), waits=[(sFin, S.val(sFin))], inc=sDG, inc_n=16)

        S.emit("gpsimd", None, waits=[(sFin, S.val(sFin)), (sDG, S.val(sDG)),
                                      (sDS, S.val(sDS)), (sDW, S.val(sDW))])

        @blk.sync
        def _(e):
            S.replay("sync", e)

        @blk.vector
        def _(e):
            S.replay("vector", e)

        @blk.scalar
        def _(e):
            S.replay("scalar", e)

        @blk.tensor
        def _(e):
            S.replay("tensor", e)

        @blk.gpsimd
        def _(e):
            S.replay("gpsimd", e)

    return nc


def _tile_w(w):
    K, O = w.shape
    return np.ascontiguousarray(
        w.reshape(K // 128, 128, O // 128, 128).transpose(2, 1, 0, 3)
        .reshape(O // 128, 128, (K // 128) * 128))


def _bias_cols(b, n):
    return np.ascontiguousarray(b.reshape(n, 128).T)


_NC_CACHE = {}


def kernel(**inp):
    x = inp["L0_attn_x"]
    B, L, _ = x.shape
    if "prog" not in _NC_CACHE:
        _NC_CACHE["prog"] = build_program()
    nc = _NC_CACHE["prog"]

    base = {
        "w_sg": _tile_w(np.ascontiguousarray(inp["L0_ffn_shared_gate_w"].T)),
        "w_su": _tile_w(np.ascontiguousarray(inp["L0_ffn_shared_up_w"].T)),
        "w_sd": _tile_w(np.ascontiguousarray(inp["L0_ffn_shared_down_w"].T)),
        "w_gu": np.stack([_tile_w(np.ascontiguousarray(
            inp["L0_ffn_gate_up_proj"][e].T)) for e in range(8)]),
        "w_dn": np.stack([_tile_w(np.ascontiguousarray(
            inp["L0_ffn_down_proj"][e].T)) for e in range(8)]),
        "w_r9": np.ascontiguousarray(
            np.concatenate([inp["L0_ffn_router_weight"].T,
                            inp["L0_ffn_shared_expert_gate_w"].T],
                           axis=1).reshape(8, 128, 9)),
        "w_in": _tile_w(inp["L0_attn_W_in"]),
        "w_gt": _tile_w(inp["L0_attn_W_gate"]),
        "w_dl": _tile_w(inp["L0_attn_W_delta"]),
        "w_bc": np.ascontiguousarray(
            np.concatenate([inp["L0_attn_W_B"], inp["L0_attn_W_C"]],
                           axis=1).reshape(16, 128, 32)),
        "w_ot": _tile_w(inp["L0_attn_W_out"]),
        "b_in": _bias_cols(inp["L0_attn_b_in"], 16),
        "b_gt": _bias_cols(inp["L0_attn_b_gate"], 16),
        "b_dl": _bias_cols(inp["L0_attn_b_delta"], 16),
        "b_cv": _bias_cols(inp["L0_attn_conv_b"], 16),
        "b_ot": _bias_cols(inp["L0_attn_b_out"], 8),
        "d_skip": _bias_cols(inp["L0_attn_D_skip"], 16),
        "ident": np.eye(128, dtype=np.float32),
    }
    cw = inp["L0_attn_conv_w"][:, 0, :]
    dg = np.zeros((16, 4, 128, 128), np.float32)
    ii = np.arange(128)
    for dt in range(16):
        for k in range(4):
            dg[dt, k, ii, ii] = cw[dt * 128:(dt + 1) * 128, k]
    base["w_cv"] = np.ascontiguousarray(
        dg.transpose(0, 2, 1, 3).reshape(16, 128, 512))
    alog = inp["L0_attn_A_log"]

    in_maps = []
    for core in range(8):
        g, r = core // 4, core % 4
        m = dict(base)
        xt = np.zeros((HP + TL, D), np.float32)
        lo = r * TL
        xt[HP:] = x[g, lo:lo + TL]
        if r > 0:
            xt[HP - 3:HP] = x[g, lo - 3:lo]
        m["x_fm"] = np.ascontiguousarray(xt.T)
        msk = np.ones((128, HP), np.float32)
        if r == 0:
            msk[:] = 0.0
        m["mask_h"] = msk
        a_sl = -np.exp(alog[:, 4 * r:4 * r + 4])
        m["a_own"] = np.ascontiguousarray(
            a_sl.reshape(16, 128, 4).transpose(1, 0, 2).reshape(128, 64))
        sel = np.zeros((32, 8, 128), np.float32)
        for nn in range(4):
            sel[4 * r + nn, nn, :] = 1.0
            sel[16 + 4 * r + nn, 4 + nn, :] = 1.0
        m["selbc"] = sel.reshape(32, 8 * 128)
        in_maps.append(m)

    res = run_bass_kernel_spmd(nc, in_maps, core_ids=list(range(8)))
    out = np.empty((B, L, D), np.float32)
    for core in range(8):
        g, r = core // 4, core % 4
        out[g, r * TL:(r + 1) * TL] = res.results[core]["out_fm"].T
    if DBG:
        kernel.dbg = [dict(res.results[c]) for c in range(8)]
    return out


# version 10
# speedup vs baseline: 1.1814x; 1.0044x over previous
import contextlib
import os
import numpy as np
import concourse.bass as bass
import concourse.mybir as mybir
from concourse.bass_utils import run_bass_kernel_spmd

F32 = mybir.dt.float32
BF16 = mybir.dt.bfloat16
U8 = mybir.dt.uint8
AF = mybir.ActivationFunctionType
ALU = mybir.AluOpType
AX = mybir.AxisListType

D = 1024
DI = 2048
TL = 4096
HP = 8
EPS = 1e-5
TCA = 1024
NCA = 4
TCB = 2048
NCB = 8
TCC = 1024
NCC = 4
WA = HP + TCA

RG = [[0, 1, 2, 3], [4, 5, 6, 7]]
DBG = bool(int(os.environ.get("MK_DBG", "0")))
NOCACHE = bool(int(os.environ.get("MK_NOCACHE", "0")))


class Sched:
    ENGINES = ("sync", "vector", "scalar", "tensor", "gpsimd")

    def __init__(self):
        self.steps = {e: [] for e in self.ENGINES}
        self.count = {}

    def emit(self, eng, fn, waits=(), inc=None, inc_n=1):
        after = None
        if inc is not None:
            self.count[inc] = self.count.get(inc, 0) + inc_n
            after = self.count[inc]
        self.steps[eng].append((fn, [w for w in waits if w is not None],
                                (inc, inc_n) if inc is not None else None))
        return after

    def val(self, sem):
        return self.count.get(sem, 0)

    def replay(self, eng_name, eng):
        for fn, waits, inc in self.steps[eng_name]:
            for (s_, v_) in waits:
                eng.wait_ge(s_, v_)
            r = fn(eng) if fn is not None else None
            if inc is not None:
                assert r is not None
                r.then_inc(inc[0], inc[1])


class Overlay:
    def __init__(self, nc, stack, name, nbytes):
        self.nc = nc
        self.back = stack.enter_context(nc.sbuf_tensor(name, [128, nbytes], U8))
        self.base = nc.lookup_mloc(self.back).addr

    def view(self, name, shape, dtype, off_bytes):
        h = self.nc.virtual_sbuf_tensor(name, list(shape), dtype)
        self.nc._tensor(h.name, h.shape, h.dtype, type="SB")
        mloc = self.nc.lookup_mloc(h)
        mloc.allocated = True
        mloc.addr = self.base + off_bytes
        return h.ap()


def build_program():
    nc = bass.Bass("TRN2", target_bir_lowering=False)

    def din(name, shape, dt=F32):
        return nc.dram_tensor(name, shape, dt, kind="ExternalInput").ap()

    x_in = din("x_fm", [D, HP + TL])
    w_sg = din("w_sg", [16, 128, 8 * 128])
    w_su = din("w_su", [16, 128, 8 * 128])
    w_sd = din("w_sd", [8, 128, 16 * 128])
    w_gu = din("w_gu", [8, 8, 128, 8 * 128])
    w_dn = din("w_dn", [8, 8, 128, 4 * 128])
    w_r9 = din("w_r9", [8, 128, 9])
    w_in = din("w_in", [16, 128, 8 * 128])
    w_gt = din("w_gt", [16, 128, 8 * 128])
    w_dl = din("w_dl", [16, 128, 16 * 128])
    w_bc = din("w_bc", [16, 128, 32])
    w_ot = din("w_ot", [8, 128, 16 * 128])
    w_cv = din("w_cv", [16, 128, 4 * 128])
    b_in = din("b_in", [128, 16])
    b_gt = din("b_gt", [128, 16])
    b_dl = din("b_dl", [128, 16])
    b_cv = din("b_cv", [128, 16])
    b_ot = din("b_ot", [128, 8])
    a_own = din("a_own", [128, 64])
    d_skip = din("d_skip", [128, 16])
    mask_h = din("mask_h", [128, HP])
    ident = din("ident", [128, 128])
    selbc = din("selbc", [32, 8 * 128])

    out_t = nc.dram_tensor("out_fm", [D, TL], F32, kind="ExternalOutput").ap()

    def dint(name, shape, dt=BF16):
        return nc.dram_tensor(name, shape, dt, kind="Internal").ap()

    c_sg = dint("c_sg", [16, 128, 8 * 128])
    c_su = dint("c_su", [16, 128, 8 * 128])
    c_sd = dint("c_sd", [8, 128, 16 * 128])
    c_gu = dint("c_gu", [8, 8, 128, 8 * 128])
    c_dn = dint("c_dn", [8, 8, 128, 4 * 128])
    c_in = dint("c_in", [16, 128, 8 * 128])
    c_gt = dint("c_gt", [16, 128, 8 * 128])
    c_dl = dint("c_dl", [16, 128, 16 * 128])
    c_ot = dint("c_ot", [8, 128, 16 * 128])
    c_cv = dint("c_cv", [16, 128, 4 * 128])
    x1_sp = dint("x1_sp", [D, TL])
    g_sp = dint("g_sp", [DI, TL])
    act_sp = dint("act_sp", [DI, TL])
    x2_sp = dint("x2_sp", [D, TL])
    cc_di = [dint(f"cc_di{c}", [DI, TCA]) for c in range(NCA)]
    cc_do = [dint(f"cc_do{c}", [4, DI, TCA]) for c in range(NCA)]
    cc_ui = [dint(f"cc_ui{c}", [DI, TCA]) for c in range(NCA)]
    cc_uo = [dint(f"cc_uo{c}", [4, DI, TCA]) for c in range(NCA)]
    cc_bi = [dint(f"cc_bi{c}", [32, TCA]) for c in range(NCA)]
    cc_bo = [dint(f"cc_bo{c}", [4, 32, TCA]) for c in range(NCA)]
    cc_yp = [[dint(f"cc_yp{h}_{q}", [4, 512, TCB], F32) for q in range(4)]
             for h in range(2)]
    cc_yr = [[dint(f"cc_yr{h}_{q}", [512, TCB], F32) for q in range(4)]
             for h in range(2)]

    dbg_out = {}
    if DBG:
        for nm, shp in (("x1", [D, TL]), ("act", [DI, TL]), ("gact", [DI, TL]),
                        ("delta", [DI, TCA]), ("u", [DI, TCA]),
                        ("bc", [32, TCA]), ("y", [DI, TCB]), ("x2", [D, TL]),
                        ("xln", [128, 8 * WA]), ("cwt", [1, 9 * WA]),
                        ("sact", [128, 16 * WA]), ("acts", [128, 32 * WA]),
                        ("yp", [4 * DI, TCB]), ("do0", [4 * DI, TCA]), ("uo0", [4 * DI, TCA]), ("brep", [128, 4 * TCB]), ("crep", [128, 4 * TCB]), ("b0", [128, TCB]), ("h0", [128, TCB]), ("y0", [128, TCB])):
            dbg_out[nm] = nc.dram_tensor("dbg_" + nm, shp, BF16,
                                         kind="ExternalOutput").ap()
        for nm, shp in (("cf", [128, 32]), ("lg", [128, 9]),
                        ("asc", [128, 64]), ("hl", [128, 64]),
                        ("a0f", [128, TCB])):
            dbg_out[nm] = nc.dram_tensor("dbg_" + nm, shp, F32,
                                         kind="ExternalOutput").ap()

    with contextlib.ExitStack() as stk:
        def sball(name, shape, dt):
            return stk.enter_context(nc.sbuf_tensor(name, shape, dt))[:]

        wbufA = sball("wbufA", [128, 16 * 128], BF16)
        wbufB = sball("wbufB", [128, 16 * 128], BF16)
        wbufC = sball("wbufC", [128, 16 * 128], BF16)
        wf32 = sball("wf32", [128, 16 * 128], F32)
        wf32B = sball("wf32B", [128, 16 * 128], F32)
        wr9f = sball("wr9f", [128, 72], F32)
        wr9 = sball("wr9", [128, 72], BF16)
        wbc_b = sball("wbc_b", [128, 512], BF16)
        sgs = sball("sgs", [128, WA], BF16)
        statb = sball("statb", [1, WA], BF16)
        cw_f = sball("cw_f", [128, 32], F32)
        identb = sball("identb", [128, 128], F32)
        maskb = sball("maskb", [128, HP], F32)
        biasq = sball("biasq", [128, 80], F32)
        dskq = sball("dskq", [128, 16], F32)
        a_sc = sball("a_sc", [128, 64], F32)
        h_last = sball("h_last", [128, 64], F32)
        onescol = sball("onescol", [128, 2], BF16)
        onesrow = sball("onesrow", [1, 128], BF16)
        selb = sball("selb", [32, 8 * 128], BF16)
        bct = sball("bct", [32, TCA], BF16)
        dlt = sball("dlt", [128, TCA], BF16)
        ut = sball("ut", [128, TCA], BF16)

        OV = Overlay(nc, stk, "ov", 167168)
        x_c = OV.view("x_c", [128, 8 * WA], BF16, 0)
        xln = OV.view("xln", [128, 8 * WA], BF16, 16512)
        cwrep = OV.view("cwrep", [128, 9 * WA], BF16, 33024)
        mrep = OV.view("mrep", [128, WA], BF16, 51600)
        rrep = OV.view("rrep", [128, WA], BF16, 53664)
        xsq = OV.view("xsq", [128, WA], BF16, 55728)
        e_sc = OV.view("e_sc", [128, 512], F32, 57792)
        acts = OV.view("acts", [128, 32 * WA], BF16, 63984)
        sact = OV.view("sact", [128, 16 * WA], BF16, 130032)
        xz = OV.view("xz", [128, 16 * WA], BF16, 63984)
        actm = OV.view("actm", [128, 16 * TCA], BF16, 97008)
        stat = OV.view("stat", [1, 4 * WA], F32, 97008)
        xf32 = OV.view("xf32", [128, 8 * 258], F32, 113520)
        cwT = OV.view("cwT", [1, 9 * WA], BF16, 144480)
        of32 = OV.view("of32", [128, 1024], F32, 163056)
        dl_g = OV.view("dl_g", [128, 2 * TCB], BF16, 0)
        u_g = OV.view("u_g", [128, 2 * TCB], BF16, 8192)
        bc_g = OV.view("bc_g", [32, TCB], BF16, 16384)
        bco = OV.view("bco", [8, TCB], BF16, 20480)
        brep = OV.view("brep", [128, 4 * TCB], BF16, 24576)
        crep = OV.view("crep", [128, 4 * TCB], BF16, 40960)
        brepB = OV.view("brepB", [128, 4 * TCB], BF16, 106496)
        crepB = OV.view("crepB", [128, 4 * TCB], BF16, 122880)
        brepb = [brep, brepB]
        crepb = [crep, crepB]
        a_t = OV.view("a_t", [128, 2 * TCB], BF16, 57344)
        b_t = OV.view("b_t", [128, 2 * TCB], BF16, 73728)
        h_t = OV.view("h_t", [128, TCB], BF16, 81920)
        p_t = OV.view("p_t", [128, TCB], BF16, 86016)
        ph_t = OV.view("ph_t", [128, TCB], BF16, 90112)
        s1_t = OV.view("s1_t", [128, TCB], BF16, 94208)
        y_t = OV.view("y_t", [128, TCB], F32, 98304)
        yc = OV.view("yc", [128, 16 * TCC], F32, 0)
        actc = OV.view("actc", [128, 16 * TCC], BF16, 65536)
        ycb = OV.view("ycb", [128, 16 * TCC], BF16, 65536)
        gc = OV.view("gc", [128, 16 * TCC], BF16, 98304)
        x1c = OV.view("x1c", [128, 8 * TCC], BF16, 131072)
        x2st = OV.view("x2st", [128, 1024], BF16, 147456)

        sDW = stk.enter_context(nc.semaphore("sDW"))
        sDW2 = stk.enter_context(nc.semaphore("sDW2"))
        sB0 = stk.enter_context(nc.semaphore("sB0"))
        sB1 = stk.enter_context(nc.semaphore("sB1"))
        sBC = stk.enter_context(nc.semaphore("sBC"))
        sP = stk.enter_context(nc.semaphore("sP"))
        sW3 = stk.enter_context(nc.semaphore("sW3"))
        sW4 = stk.enter_context(nc.semaphore("sW4"))
        sDG = stk.enter_context(nc.semaphore("sDG"))
        sDS = stk.enter_context(nc.semaphore("sDS"))
        sV = stk.enter_context(nc.semaphore("sV"))
        sS = stk.enter_context(nc.semaphore("sS"))
        sT = stk.enter_context(nc.semaphore("sT"))
        sG = stk.enter_context(nc.semaphore("sG"))
        sFin = stk.enter_context(nc.semaphore("sFin"))

        PSB = [stk.enter_context(nc.psum_tensor(f"pb{i}", [128, 512], F32))[:]
               for i in range(8)]

        blk = stk.enter_context(nc.Block())
        S = Sched()

        def mm(psum, lhsT, rhs, start, stop, waits=()):
            return S.emit("tensor",
                          lambda e, a=psum, b=lhsT, c=rhs, s1=start, s2=stop:
                          e.matmul(a, b, c, start=s1, stop=s2),
                          waits=waits, inc=sT)

        bank_free = {i: [] for i in range(8)}
        bank_rr = [0]

        def take_bank():
            i = (bank_rr[0] + 1) % 8
            bank_rr[0] = i
            return i, list(bank_free[i])

        wslot = [0]
        wslot_free = [[], [], []]

        wf32_free = [[], []]
        cache_bar = [[]]
        WBUFS = (wbufA, wbufB, wbufC)
        WSEMS = (sDW, sDW2, sW3)
        WENGS = ("sync", "gpsimd", "sync")

        def load_wtile(src_f32, src_bf, kt, cache_write):
            sl = wslot[0]
            wslot[0] = (wslot[0] + 1) % 3
            buf = WBUFS[sl]
            waits = list(wslot_free[sl])
            sem = WSEMS[sl]
            eng = WENGS[sl]
            if src_f32 is not None:
                stg = wf32 if sl % 2 == 0 else wf32B
                v = S.emit(eng, lambda e, s=src_f32, k=kt, g=stg: e.dma_start(
                    out=g[:, 0:k * 128], in_=s),
                    waits=waits + wf32_free[sl % 2], inc=sem, inc_n=16)
                v2 = S.emit("vector", lambda e, d=buf, k=kt, g=stg: e.tensor_copy(
                    d[:, 0:k * 128], g[:, 0:k * 128]),
                    waits=[(sem, v)], inc=sV)
                ready = (sV, v2)
                wf32_free[sl % 2] = [(sV, v2)]
                if cache_write is not None and not NOCACHE:
                    vcw = S.emit("scalar", lambda e, d=cache_write, s=buf, k=kt:
                                 e.dma_start(
                                 out=d, in_=s[:, 0:k * 128]),
                                 waits=[ready], inc=sDS, inc_n=16)
                    wslot_cache[sl] = (sDS, vcw)
            else:
                v = S.emit(eng, lambda e, s=src_bf, d=buf, k=kt: e.dma_start(
                    out=d[:, 0:k * 128], in_=s),
                    waits=waits + cache_bar[0], inc=sem, inc_n=16)
                ready = (sem, v)
            return buf, ready, sl

        wslot_cache = [None, None, None]

        def free_wslot(sl):
            wslot_free[sl] = [(sT, S.val(sT))]
            if wslot_cache[sl] is not None:
                wslot_free[sl].append(wslot_cache[sl])
                wslot_cache[sl] = None

        def barrier_waits():
            return [(sT, S.val(sT)), (sS, S.val(sS)), (sV, S.val(sV)),
                    (sDG, S.val(sDG)), (sDS, S.val(sDS))]

        S.emit("sync", lambda e: e.dma_start(out=identb, in_=ident), inc=sDW, inc_n=16)
        S.emit("sync", lambda e: e.dma_start(out=maskb, in_=mask_h), inc=sDW, inc_n=16)
        for (c0, c1, src) in ((0, 16, b_in), (16, 32, b_gt), (32, 48, b_dl),
                              (48, 64, b_cv), (64, 72, b_ot)):
            S.emit("sync", lambda e, a=c0, b=c1, s=src: e.dma_start(
                out=biasq[:, a:b], in_=s), inc=sDW, inc_n=16)
        S.emit("sync", lambda e: e.dma_start(out=dskq, in_=d_skip), inc=sDW, inc_n=16)
        S.emit("sync", lambda e: e.dma_start(out=a_sc, in_=a_own), inc=sDW, inc_n=16)
        vself = S.emit("sync", lambda e: e.dma_start(out=wf32[0:32, 0:1024], in_=selbc),
                       inc=sDW, inc_n=16)
        vselc = S.emit("vector", lambda e: e.tensor_copy(selb, wf32[0:32, 0:1024]),
                       waits=[(sDW, vself)], inc=sV)
        vload = S.emit("sync", lambda e: e.dma_start(
            out=wf32[:, 0:512].rearrange("p (k o) -> p k o", k=16),
            in_=w_bc.rearrange("k p o -> p k o")),
            waits=[(sV, vselc)], inc=sDW, inc_n=16)
        S.emit("vector", lambda e: e.memset(onescol, 1.0), inc=sV)
        S.emit("vector", lambda e: e.memset(onesrow, 1.0), inc=sV)
        S.emit("vector", lambda e: e.memset(h_last, 0.0), inc=sV)
        vbc0 = S.emit("vector", lambda e: e.tensor_copy(wbc_b, wf32[:, 0:512]),
                      waits=[(sDW, vload)], inc=sV)
        wf32_free[0] = [(sV, vbc0)]
        wf32_free[1] = [(sV, vbc0)]
        prep_done = [(sDW, S.val(sDW)), (sG, S.val(sG)), (sV, S.val(sV))]

        def emit_ln(W, movs):
            gate = [(sV, S.val(sV)), (sS, S.val(sS)), (sT, S.val(sT))]
            vm = vq = None
            for (mo, mw) in movs:
                bsum, w0 = take_bank()
                last_sum = None
                for it in range(8):
                    ws = (w0 + gate) if it == 0 else []
                    last_sum = mm(PSB[bsum][0:1, 0:mw], onescol[:, 0:1],
                                  x_c[:, it * W + mo: it * W + mo + mw],
                                  start=(it == 0), stop=(it == 7), waits=ws)
                vm = S.emit("vector", lambda e, m_=mo, w_=mw, b=bsum: e.tensor_scalar(
                    stat[0:1, m_:m_ + w_], PSB[b][0:1, 0:w_], 1.0 / D, None,
                    ALU.mult), waits=[(sT, last_sum)], inc=sV)
                bank_free[bsum] = [(sV, vm)]
            for (mo, mw) in movs:
                bsq, w1 = take_bank()
                last_sq = None
                for it in range(8):
                    vs = S.emit("scalar", lambda e, it_=it, W_=W, m_=mo, w_=mw:
                                e.activation(
                                out=xsq[:, 0:w_],
                                in_=x_c[:, it_ * W_ + m_: it_ * W_ + m_ + w_],
                                func=AF.Square), inc=sS, waits=[(sT, S.val(sT))])
                    last_sq = mm(PSB[bsq][0:1, 0:mw], onescol[:, 0:1],
                                 xsq[:, 0:mw], start=(it == 0),
                                 stop=(it == 7),
                                 waits=[(sS, vs)] + (w1 if it == 0 else []))
                vq = S.emit("vector", lambda e, m_=mo, w_=mw, b=bsq, W_=W:
                    e.tensor_scalar(
                    stat[0:1, W_ + m_:W_ + m_ + w_], PSB[b][0:1, 0:w_], 1.0 / D,
                    None, ALU.mult), waits=[(sT, last_sq)], inc=sV)
                bank_free[bsq] = [(sV, vq)]
            S.emit("vector", lambda e, W_=W: e.scalar_tensor_tensor(
                stat[0:1, 2 * W_:3 * W_], stat[0:1, 0:W_], -1.0, stat[0:1, 0:W_],
                ALU.mult, ALU.mult), waits=[(sV, vm)], inc=sV)
            vv2 = S.emit("vector", lambda e, W_=W: e.tensor_add(
                stat[0:1, 2 * W_:3 * W_], stat[0:1, 2 * W_:3 * W_],
                stat[0:1, W_:2 * W_]), waits=[(sV, vq)], inc=sV)
            vv3 = S.emit("vector", lambda e, W_=W: e.tensor_scalar(
                stat[0:1, 2 * W_:3 * W_], stat[0:1, 2 * W_:3 * W_], EPS, None,
                ALU.add), inc=sV)
            vg = S.emit("vector", lambda e: e.engine_nop(),
                        waits=[(sV, vv3)], inc=sV)
            vl = S.emit("scalar", lambda e, W_=W: e.activation(
                out=stat[0:1, 3 * W_:4 * W_], in_=stat[0:1, 2 * W_:3 * W_],
                func=AF.Ln), waits=[(sV, vg)], inc=sS)
            vneg = S.emit("vector", lambda e, W_=W: e.tensor_scalar(
                stat[0:1, 3 * W_:4 * W_], stat[0:1, 3 * W_:4 * W_], -0.5, None,
                ALU.mult), waits=[(sS, vl)], inc=sV)
            vgx = S.emit("vector", lambda e: e.engine_nop(), waits=[(sV, vneg)], inc=sV)
            vr = S.emit("scalar", lambda e, W_=W: e.activation(
                out=stat[0:1, 3 * W_:4 * W_], in_=stat[0:1, 3 * W_:4 * W_],
                func=AF.Exp), waits=[(sV, vgx)], inc=sS)
            evs = []
            for (dst, row, prow) in ((mrep, 0, (sV, vm)), (rrep, 3, (sS, vr))):
                vb = S.emit("vector", lambda e, r_=row, W_=W: e.tensor_copy(
                    statb[0:1, 0:W_], stat[0:1, r_ * W_:r_ * W_ + W_]),
                    waits=[prow, (sT, S.val(sT))] +
                          ([(sS, evs[-1])] if evs else []), inc=sV)
                o2 = 0
                while o2 < W:
                    w2 = min(512, W - o2)
                    bk, wfree = take_bank()
                    vrp = mm(PSB[bk][:, 0:w2], onesrow,
                             statb[0:1, o2:o2 + w2], start=True, stop=True,
                             waits=[(sV, vb)] + wfree)
                    ve = S.emit("scalar", lambda e, d=dst, o_=o2, w_=w2, b=bk:
                                e.copy(d[:, o_:o_ + w_], PSB[b][:, 0:w_]),
                                waits=[(sT, vrp)], inc=sS)
                    bank_free[bk] = [(sS, ve)]
                    evs.append(ve)
                    o2 += w2
            wrep = (sS, evs[-1])
            lastv = None
            for it in range(8):
                S.emit("vector", lambda e, it_=it, W_=W: e.tensor_sub(
                    xln[:, it_ * W_: it_ * W_ + W_],
                    x_c[:, it_ * W_: it_ * W_ + W_], mrep[:, 0:W_]),
                    waits=[wrep, (sT, S.val(sT))] if it == 0 else [], inc=sV)
                lastv = S.emit("vector", lambda e, it_=it, W_=W: e.tensor_mul(
                    xln[:, it_ * W_: it_ * W_ + W_],
                    xln[:, it_ * W_: it_ * W_ + W_], rrep[:, 0:W_]), inc=sV)
            return (sV, lastv)

        def emit_ffn(W, movs, tgroups, cache_mode, final_chunk=None, dbg_dump=False):
            ln_done = emit_ln(W, movs)
            vr9 = S.emit("sync", lambda e: e.dma_start(
                out=wr9f.rearrange("p (k o) -> p k o", k=8),
                in_=w_r9.rearrange("k p o -> p k o")),
                waits=[(sV, S.val(sV))], inc=sDW, inc_n=16)
            vr9c = S.emit("vector", lambda e: e.tensor_copy(wr9, wr9f),
                          waits=[(sDW, vr9)], inc=sV)
            for (go, gw) in tgroups:
                bk, wfree = take_bank()
                lastg = None
                for kt in range(8):
                    lastg = mm(PSB[bk][0:gw, 0:9],
                               xln[:, kt * W + go: kt * W + go + gw],
                               wr9[:, kt * 9:(kt + 1) * 9],
                               start=(kt == 0), stop=(kt == 7),
                               waits=([ln_done, (sV, vr9c)] +
                                      wfree) if kt == 0 else [])
                g = gw
                cf = cw_f

                def vop(fn, waits=()):
                    return S.emit("vector", fn, waits=waits, inc=sV)

                def guard(v):
                    return S.emit("vector", lambda e: e.engine_nop(),
                                  waits=[(sV, v)], inc=sV)

                v1 = vop(lambda e, g=g, bk=bk: e.reduce_max(cf[0:g, 0:1], PSB[bk][0:g, 0:8],
                         axis=AX.X), waits=[(sT, lastg)])
                guard(v1)
                v2 = vop(lambda e, g=g, bk=bk: e.tensor_scalar(cf[0:g, 1:2], cf[0:g, 0:1],
                         -1.0, None, ALU.mult))
                guard(v2)
                v3 = S.emit("scalar", lambda e, b=bk, g=g: e.activation(
                    out=cf[0:g, 2:10], in_=PSB[b][0:g, 0:8], func=AF.Exp,
                    bias=cf[0:g, 1:2]), waits=[(sV, S.val(sV))], inc=sS)
                v4 = vop(lambda e, g=g, bk=bk: e.reduce_sum(cf[0:g, 10:11], cf[0:g, 2:10],
                         axis=AX.X), waits=[(sS, v3)])
                guard(v4)
                v5 = vop(lambda e, g=g, bk=bk: e.reciprocal(cf[0:g, 10:11], cf[0:g, 10:11]))
                guard(v5)
                guard(vop(lambda e, g=g, bk=bk: e.tensor_scalar(cf[0:g, 2:10], cf[0:g, 2:10],
                    cf[0:g, 10:11], None, ALU.mult)))
                v6 = vop(lambda e, g=g, bk=bk: e.reduce_max(cf[0:g, 0:1], cf[0:g, 2:10],
                         axis=AX.X))
                guard(v6)
                guard(vop(lambda e, g=g, bk=bk: e.tensor_scalar(cf[0:g, 30:31], cf[0:g, 0:1],
                    -1.0, None, ALU.mult)))
                guard(vop(lambda e, g=g, bk=bk: e.tensor_scalar(cf[0:g, 12:20], cf[0:g, 2:10],
                    cf[0:g, 30:31], None, ALU.add)))
                guard(vop(lambda e, g=g, bk=bk: e.tensor_scalar(cf[0:g, 12:20], cf[0:g, 12:20],
                    0.0, None, ALU.is_ge)))
                guard(vop(lambda e, g=g, bk=bk: e.tensor_scalar(cf[0:g, 20:28], cf[0:g, 12:20],
                    -1.0, 1.0, ALU.mult, ALU.add)))
                v8 = vop(lambda e, g=g, bk=bk: e.tensor_mul(cf[0:g, 20:28], cf[0:g, 20:28],
                         cf[0:g, 2:10]))
                guard(v8)
                v9 = vop(lambda e, g=g, bk=bk: e.reduce_max(cf[0:g, 1:2], cf[0:g, 20:28],
                         axis=AX.X))
                guard(v9)
                guard(vop(lambda e, g=g, bk=bk: e.tensor_scalar(cf[0:g, 30:31], cf[0:g, 1:2],
                    -1.0, None, ALU.mult)))
                guard(vop(lambda e, g=g, bk=bk: e.tensor_scalar(cf[0:g, 20:28], cf[0:g, 20:28],
                    cf[0:g, 30:31], None, ALU.add)))
                guard(vop(lambda e, g=g, bk=bk: e.tensor_scalar(cf[0:g, 20:28], cf[0:g, 20:28],
                    0.0, None, ALU.is_ge)))
                guard(vop(lambda e, g=g, bk=bk: e.tensor_scalar(cf[0:g, 12:20], cf[0:g, 12:20],
                    cf[0:g, 0:1], None, ALU.mult)))
                guard(vop(lambda e, g=g, bk=bk: e.tensor_scalar(cf[0:g, 20:28], cf[0:g, 20:28],
                    cf[0:g, 1:2], None, ALU.mult)))
                guard(vop(lambda e, g=g, bk=bk: e.tensor_add(cf[0:g, 12:20], cf[0:g, 12:20],
                    cf[0:g, 20:28])))
                va = vop(lambda e, g=g, bk=bk: e.tensor_add(cf[0:g, 0:1], cf[0:g, 0:1],
                         cf[0:g, 1:2]))
                guard(va)
                vb_ = vop(lambda e, g=g, bk=bk: e.reciprocal(cf[0:g, 0:1], cf[0:g, 0:1]))
                guard(vb_)
                vqc = vop(lambda e, g=g, bk=bk: e.tensor_scalar(cf[0:g, 12:20], cf[0:g, 12:20],
                          cf[0:g, 0:1], None, ALU.mult))
                guard(vqc)
                vs1 = S.emit("scalar", lambda e, b=bk, g=g: e.activation(
                    out=cf[0:g, 28:29], in_=PSB[b][0:g, 8:9], func=AF.Exp,
                    scale=-1.0), waits=[(sT, lastg)], inc=sS)
                vs2 = vop(lambda e, g=g, bk=bk: e.tensor_scalar(cf[0:g, 28:29], cf[0:g, 28:29],
                          1.0, None, ALU.add), waits=[(sS, vs1)])
                guard(vs2)
                vs3 = vop(lambda e, g=g, bk=bk: e.reciprocal(cf[0:g, 28:29], cf[0:g, 28:29]))
                guard(vs3)
                vcp = vop(lambda e, g=g, bk=bk: e.tensor_copy(cf[0:g, 20:21], cf[0:g, 28:29]),
                          waits=[(sV, vqc)])
                bank_free[bk] = [(sV, vcp), (sS, vs1)]
                if dbg_dump and DBG and go == 0:
                    S.emit("gpsimd", lambda e, g=g, bk=bk: e.dma_start(
                        out=dbg_out["cf"], in_=cf), waits=[(sV, vcp)],
                        inc=sDG, inc_n=16)
                    S.emit("scalar", lambda e, b=bk, g=g: e.copy(
                        e_sc[:, 0:9], PSB[b][0:128, 0:9]),
                        waits=[(sS, vs1)], inc=sS)
                    S.emit("gpsimd", lambda e, g=g, bk=bk: e.dma_start(
                        out=dbg_out["lg"], in_=e_sc[:, 0:9]),
                        waits=[(sS, S.val(sS))], inc=sDG, inc_n=16)
                for grp0 in range(0, 9, 4):
                    nrow = min(4, 9 - grp0)
                    bk2, wfree2 = take_bank()
                    vtp = None
                    for rr in range(nrow):
                        vtp = S.emit("tensor", lambda e, b=bk2, g_=g, r_=grp0 + rr,
                                     c2=rr * 128: e.transpose(
                                     PSB[b][0:1, c2:c2 + g_],
                                     cf[0:g_, 12 + r_:13 + r_],
                                     identb[0:g_, 0:g_]),
                                     waits=([(sV, vcp)] +
                                            wfree2)
                                     if rr == 0 else [], inc=sT)
                    vev = S.emit("scalar", lambda e, b=bk2, go_=go, g_=g, W_=W,
                                 g0=grp0, nr=nrow: e.copy(
                                 cwT[0:1, g0 * W_: (g0 + nr) * W_].rearrange(
                                     "p (r w) -> p r w", r=nr)[:, :, go_:go_ + g_],
                                 PSB[b][0:1, 0:nr * 128].rearrange(
                                     "p (r w) -> p r w", r=nr)[:, :, 0:g_]),
                                 waits=[(sT, vtp)], inc=sS)
                    bank_free[bk2] = [(sS, vev)]
            ev_cw = (sS, S.val(sS))
            reps = []
            for row in range(9):
                o2 = 0
                while o2 < W:
                    w2 = min(512, W - o2)
                    bk, wfree = take_bank()
                    vrp = mm(PSB[bk][:, 0:w2], onesrow,
                             cwT[0:1, row * W + o2: row * W + o2 + w2],
                             start=True, stop=True,
                             waits=[ev_cw] + wfree)
                    ve = S.emit("scalar", lambda e, r_=row, o_=o2, w_=w2, b=bk:
                                e.copy(cwrep[:, r_ * W + o_: r_ * W + o_ + w_],
                                       PSB[b][:, 0:w_]),
                                waits=[(sT, vrp)], inc=sS)
                    bank_free[bk] = [(sS, ve)]
                    reps.append(ve)
                    o2 += w2
            cw_ready = (sS, reps[-1])
            if dbg_dump and DBG:
                S.emit("gpsimd", lambda e: e.dma_start(out=dbg_out["xln"], in_=xln),
                       waits=[ln_done], inc=sDG, inc_n=16)
                S.emit("gpsimd", lambda e: e.dma_start(out=dbg_out["cwt"], in_=cwT),
                       waits=[ev_cw], inc=sDG, inc_n=16)

            for ot in range(16):
                wb_, wr_, sl = load_wtile(w_sg[ot] if (cache_mode or NOCACHE) else None,
                                          None if (cache_mode or NOCACHE) else c_sg[ot], 8,
                                          c_sg[ot] if (cache_mode or NOCACHE) else None)
                for (mo, mw) in movs:
                    bg, fg = take_bank()
                    lg = None
                    for kt in range(8):
                        lg = mm(PSB[bg][:, 0:mw], wb_[:, kt * 128:(kt + 1) * 128],
                                xln[:, kt * W + mo: kt * W + mo + mw],
                                start=(kt == 0), stop=(kt == 7),
                                waits=([wr_, ln_done] + fg)
                                if kt == 0 else [])
                    vsl = S.emit("scalar", lambda e, b=bg, w_=mw, o_=ot * W + mo:
                                 e.activation(out=sact[:, o_:o_ + w_],
                                              in_=PSB[b][:, 0:w_], func=AF.Silu),
                                 waits=[(sT, lg)] +
                                       ([cw_ready] if ot >= 7 else []), inc=sS)
                    bank_free[bg] = [(sS, vsl)]
                free_wslot(sl)
            for ot in range(16):
                wb_, wr_, sl = load_wtile(w_su[ot] if (cache_mode or NOCACHE) else None,
                                          None if (cache_mode or NOCACHE) else c_su[ot], 8,
                                          c_su[ot] if (cache_mode or NOCACHE) else None)
                for (mo, mw) in movs:
                    bu, fu = take_bank()
                    lu = None
                    for kt in range(8):
                        lu = mm(PSB[bu][:, 0:mw], wb_[:, kt * 128:(kt + 1) * 128],
                                xln[:, kt * W + mo: kt * W + mo + mw],
                                start=(kt == 0), stop=(kt == 7),
                                waits=([wr_] + fu) if kt == 0 else [])
                    v1 = S.emit("vector", lambda e, b=bu, o_=ot * W + mo, w_=mw:
                                e.tensor_mul(sact[:, o_:o_ + w_],
                                             sact[:, o_:o_ + w_], PSB[b][:, 0:w_]),
                                waits=[(sT, lu), (sS, S.val(sS))], inc=sV)
                    if mw < 128:
                        S.emit("vector", lambda e: e.engine_nop(),
                               waits=[(sV, v1)], inc=sV)
                    S.emit("vector", lambda e, o_=ot * W + mo, mo_=mo, w_=mw:
                           e.tensor_mul(sact[:, o_:o_ + w_], sact[:, o_:o_ + w_],
                                        cwrep[:, 8 * W + mo_: 8 * W + mo_ + w_]),
                           waits=[cw_ready], inc=sV)
                    bank_free[bu] = [(sV, v1)]
                free_wslot(sl)

            for ex in range(8):
                for op in range(4):
                    wg_, wgr_, slg = load_wtile(
                        w_gu[ex][op] if (cache_mode or NOCACHE) else None,
                        None if (cache_mode or NOCACHE) else c_gu[ex][op], 8,
                        c_gu[ex][op] if (cache_mode or NOCACHE) else None)
                    wu_, wur_, slu = load_wtile(
                        w_gu[ex][op + 4] if (cache_mode or NOCACHE) else None,
                        None if (cache_mode or NOCACHE) else c_gu[ex][op + 4], 8,
                        c_gu[ex][op + 4] if (cache_mode or NOCACHE) else None)
                    for (mo, mw) in movs:
                        bg, fg = take_bank()
                        bu, fu = take_bank()
                        lg = lu = None
                        for kt in range(8):
                            lg = mm(PSB[bg][:, 0:mw],
                                    wg_[:, kt * 128:(kt + 1) * 128],
                                    xln[:, kt * W + mo: kt * W + mo + mw],
                                    start=(kt == 0), stop=(kt == 7),
                                    waits=([wgr_, ln_done] + fg)
                                    if kt == 0 else [])
                        for kt in range(8):
                            lu = mm(PSB[bu][:, 0:mw],
                                    wu_[:, kt * 128:(kt + 1) * 128],
                                    xln[:, kt * W + mo: kt * W + mo + mw],
                                    start=(kt == 0), stop=(kt == 7),
                                    waits=([wur_] + fu)
                                    if kt == 0 else [])
                        vsl = S.emit("scalar", lambda e, b=bg, w_=mw: e.activation(
                            out=sgs[:, 0:w_], in_=PSB[b][:, 0:w_], func=AF.Silu),
                            waits=[(sT, lg), (sV, S.val(sV))], inc=sS)
                        slot = ex * 4 + op
                        v1 = S.emit("vector", lambda e, b=bu, s_=slot * W + mo, w_=mw:
                                    e.tensor_mul(acts[:, s_:s_ + w_], sgs[:, 0:w_],
                                                 PSB[b][:, 0:w_]),
                                    waits=[(sS, vsl), (sT, lu)], inc=sV)
                        if mw < 128:
                            S.emit("vector", lambda e: e.engine_nop(),
                                   waits=[(sV, v1)], inc=sV)
                        S.emit("vector", lambda e, s_=slot * W + mo, mo_=mo, w_=mw,
                               ex_=ex: e.tensor_mul(
                               acts[:, s_:s_ + w_], acts[:, s_:s_ + w_],
                               cwrep[:, ex_ * W + mo_: ex_ * W + mo_ + w_]),
                               waits=[cw_ready], inc=sV)
                        bank_free[bg] = [(sV, v1)]
                        bank_free[bu] = [(sV, v1)]
                    free_wslot(slg)
                    free_wslot(slu)
            acts_ready = (sV, S.val(sV))
            if dbg_dump and DBG:
                S.emit("gpsimd", lambda e: e.dma_start(out=dbg_out["sact"], in_=sact),
                       waits=[acts_ready], inc=sDG, inc_n=16)
                S.emit("gpsimd", lambda e: e.dma_start(out=dbg_out["acts"], in_=acts),
                       waits=[acts_ready], inc=sDG, inc_n=16)

            for ot in range(8):
                wb_, wr_, sl = load_wtile(w_sd[ot] if (cache_mode or NOCACHE) else None,
                                          None if (cache_mode or NOCACHE) else c_sd[ot], 16,
                                          c_sd[ot] if (cache_mode or NOCACHE) else None)
                banks = []
                for (mo, mw) in movs:
                    bo, fo = take_bank()
                    banks.append((bo, mo, mw))
                    for kt in range(16):
                        mm(PSB[bo][:, 0:mw], wb_[:, kt * 128:(kt + 1) * 128],
                           sact[:, kt * W + mo: kt * W + mo + mw],
                           start=(kt == 0), stop=False,
                           waits=([wr_, acts_ready] + fo)
                           if kt == 0 else [])
                free_wslot(sl)
                lastd = None
                for ex in range(8):
                    wb2, wr2, sl2 = load_wtile(
                        w_dn[ex][ot] if (cache_mode or NOCACHE) else None,
                        None if (cache_mode or NOCACHE) else c_dn[ex][ot], 4,
                        c_dn[ex][ot] if (cache_mode or NOCACHE) else None)
                    for (bo, mo, mw) in banks:
                        for kt in range(4):
                            slot = ex * 4 + kt
                            lastd = mm(PSB[bo][:, 0:mw],
                                       wb2[:, kt * 128:(kt + 1) * 128],
                                       acts[:, slot * W + mo: slot * W + mo + mw],
                                       start=False, stop=(ex == 7 and kt == 3),
                                       waits=[wr2] if kt == 0 else [])
                    free_wslot(sl2)
                for (bo, mo, mw) in banks:
                    if final_chunk is None:
                        vep = S.emit("vector", lambda e, b=bo, o_=ot * W + mo, w_=mw:
                                     e.scalar_tensor_tensor(
                                     x_c[:, o_:o_ + w_], PSB[b][:, 0:w_], 0.5,
                                     x_c[:, o_:o_ + w_], ALU.mult, ALU.add),
                                     waits=[(sT, lastd)], inc=sV)
                        bank_free[bo] = [(sV, vep)]
                    else:
                        vep = S.emit("vector", lambda e, b=bo, o_=ot * W + mo,
                                     mo_=mo, w_=mw: e.scalar_tensor_tensor(
                                     of32[:, mo_:mo_ + w_], PSB[b][:, 0:w_], 0.5,
                                     x_c[:, o_:o_ + w_], ALU.mult, ALU.add),
                                     waits=[(sT, lastd), (sFin, S.val(sFin))],
                                     inc=sV)
                        bank_free[bo] = [(sV, vep)]
                        S.emit("gpsimd", lambda e, o_=ot, mo_=mo, w_=mw, c_=final_chunk:
                               e.dma_start(out=out_t[o_ * 128:(o_ + 1) * 128,
                                                     c_ * TCA + mo_: c_ * TCA + mo_ + w_],
                                           in_=of32[:, mo_:mo_ + w_]),
                               waits=[(sV, vep)], inc=sFin, inc_n=16)

        ag_d, ag_u, ag_b = [], [], []
        for c in range(NCA):
            cache_bar[0] = [(sDS, S.val(sDS))]
            K0, W = c * TCA, WA
            movs = [(0, 512), (512, 512), (1024, 8)]
            tgroups = [(i * 128, 128) for i in range(8)] + [(1024, 8)]
            own0 = HP
            xz_off = 0
            xz_movs = movs
            qw = W // 4
            for q in range(4):
                vld = S.emit("sync", lambda e, K_=K0 + q * qw, w_=qw: e.dma_start(
                    out=xf32[:, 0:8 * w_].rearrange("p (t w) -> p t w", t=8),
                    in_=x_in[:, K_:K_ + w_].rearrange("(t p) w -> p t w", p=128)),
                    waits=(barrier_waits() if q == 0 else [(sV, S.val(sV))]),
                    inc=sDW, inc_n=16)
                S.emit("vector", lambda e, o_=q * qw, w_=qw, W_=W: e.tensor_copy(
                    x_c[:, 0:8 * W_].rearrange("p (t w) -> p t w", t=8)[:, :, o_:o_ + w_],
                    xf32[:, 0:8 * w_].rearrange("p (t w) -> p t w", t=8)),
                    waits=[(sDW, vld)] + (prep_done if c == 0 and q == 0 else []),
                    inc=sV)
            emit_ffn(W, movs, tgroups, cache_mode=(c == 0), final_chunk=None, dbg_dump=(c == 0))
            S.emit("gpsimd", lambda e, c_=c, W_=W, o_=own0: e.dma_start(
                out=x1_sp[:, c_ * TCA:(c_ + 1) * TCA].rearrange(
                    "(t p) w -> p t w", p=128),
                in_=x_c[:, 0:8 * W_].rearrange("p (t w) -> p t w", t=8)[:, :, o_:o_ + TCA]),
                waits=[(sV, S.val(sV))], inc=sDG, inc_n=16)
            ln2 = emit_ln(W, movs)
            xz_gate = (sT, S.val(sT))
            for ot in range(16):
                wb_, wr_, sl = load_wtile(w_in[ot] if (c == 0 or NOCACHE) else None,
                                          None if (c == 0 or NOCACHE) else c_in[ot], 8,
                                          c_in[ot] if (c == 0 or NOCACHE) else None)
                for (mo, mw) in xz_movs:
                    bz, fz = take_bank()
                    lz = None
                    for kt in range(8):
                        lz = mm(PSB[bz][:, 0:mw], wb_[:, kt * 128:(kt + 1) * 128],
                                xln[:, kt * W + mo: kt * W + mo + mw],
                                start=(kt == 0), stop=(kt == 7),
                                waits=([wr_, ln2, xz_gate] + fz)
                                if kt == 0 else [])
                    vz = S.emit("scalar", lambda e, b=bz, d_=ot * WA + xz_off + mo,
                                w_=mw, o_=ot: e.activation(
                                out=xz[:, d_:d_ + w_], in_=PSB[b][:, 0:w_],
                                func=AF.Identity, bias=biasq[:, o_:o_ + 1]),
                                waits=[(sT, lz)], inc=sS)
                    bank_free[bz] = [(sS, vz)]
                free_wslot(sl)
            xz_done = (sS, S.val(sS))
            if c == 0:
                vmk = None
                for ot in range(16):
                    vmk = S.emit("vector", lambda e, o_=ot: e.tensor_mul(
                        xz[:, o_ * WA: o_ * WA + HP], xz[:, o_ * WA: o_ * WA + HP],
                        maskb), waits=[xz_done] if ot == 0 else [], inc=sV)
                xz_done = (sV, vmk)

            for ot in range(16):
                wb_, wr_, sl = load_wtile(w_gt[ot] if (c == 0 or NOCACHE) else None,
                                          None if (c == 0 or NOCACHE) else c_gt[ot], 8,
                                          c_gt[ot] if (c == 0 or NOCACHE) else None)
                for gi in range(2):
                    mo = own0 + gi * 512
                    bz, fz = take_bank()
                    lz = None
                    for kt in range(8):
                        lz = mm(PSB[bz][:, 0:512], wb_[:, kt * 128:(kt + 1) * 128],
                                xln[:, kt * W + mo: kt * W + mo + 512],
                                start=(kt == 0), stop=(kt == 7),
                                waits=([wr_] + fz) if kt == 0 else [])
                    vz = S.emit("scalar", lambda e, b=bz, o_=ot: e.activation(
                        out=sgs[:, 0:512], in_=PSB[b][:, 0:512], func=AF.Silu,
                        bias=biasq[:, 16 + o_:16 + o_ + 1]),
                        waits=[(sT, lz), (sDS, S.val(sDS))], inc=sS)
                    bank_free[bz] = [(sS, vz)]
                    S.emit("scalar", lambda e, c_=c, o_=ot, g_=gi: e.dma_start(
                        out=g_sp[o_ * 128:(o_ + 1) * 128,
                                 c_ * TCA + g_ * 512: c_ * TCA + g_ * 512 + 512],
                        in_=sgs[:, 0:512]), waits=[(sS, vz)], inc=sDS, inc_n=16)
                free_wslot(sl)

            for ot in range(16):
                wcv_, wcvr_, slcv = load_wtile(w_cv[ot] if (c == 0 or NOCACHE) else None,
                                               None if (c == 0 or NOCACHE) else c_cv[ot], 4,
                                               c_cv[ot] if (c == 0 or NOCACHE) else None)
                for gi in range(2):
                    mo = HP + gi * 512
                    bz, fz = take_bank()
                    lz = None
                    for k in range(4):
                        sft = 3 - k
                        lz = mm(PSB[bz][:, 0:512],
                                wcv_[:, k * 128:(k + 1) * 128],
                                xz[:, ot * WA + mo - sft: ot * WA + mo - sft + 512],
                                start=(k == 0), stop=(k == 3),
                                waits=([xz_done, wcvr_] + fz)
                                if k == 0 else [])
                    va = S.emit("scalar", lambda e, b=bz, o_=ot, g_=gi: e.activation(
                        out=actm[:, o_ * TCA + g_ * 512: o_ * TCA + g_ * 512 + 512],
                        in_=PSB[b][:, 0:512], func=AF.Silu,
                        bias=biasq[:, 48 + o_:48 + o_ + 1]),
                        waits=[(sT, lz)], inc=sS)
                    bank_free[bz] = [(sS, va)]
                    S.emit("gpsimd", lambda e, c_=c, o_=ot, g_=gi: e.dma_start(
                        out=act_sp[o_ * 128:(o_ + 1) * 128,
                                   c_ * TCA + g_ * 512: c_ * TCA + g_ * 512 + 512],
                        in_=actm[:, o_ * TCA + g_ * 512: o_ * TCA + g_ * 512 + 512]),
                        waits=[(sS, va)], inc=sDG, inc_n=16)
                free_wslot(slcv)
            act_done = (sS, S.val(sS))

            for ot in range(16):
                wb_, wr_, sl = load_wtile(w_dl[ot] if (c == 0 or NOCACHE) else None,
                                          None if (c == 0 or NOCACHE) else c_dl[ot], 16,
                                          c_dl[ot] if (c == 0 or NOCACHE) else None)
                for gi in range(2):
                    mo = gi * 512
                    bz, fz = take_bank()
                    lz = None
                    for kt in range(16):
                        lz = mm(PSB[bz][:, 0:512], wb_[:, kt * 128:(kt + 1) * 128],
                                actm[:, kt * TCA + mo: kt * TCA + mo + 512],
                                start=(kt == 0), stop=(kt == 15),
                                waits=([wr_, act_done] + fz)
                                if kt == 0 else [])
                    v1 = S.emit("scalar", lambda e, b=bz, o_=ot: e.activation(
                        out=e_sc, in_=PSB[b][:, 0:512], func=AF.Exp,
                        bias=biasq[:, 32 + o_:32 + o_ + 1]),
                        waits=[(sT, lz), (sDG, S.val(sDG)), (sV, S.val(sV))],
                        inc=sS)
                    v2 = S.emit("scalar", lambda e, m_=mo: e.activation(
                        out=dlt[:, m_:m_ + 512], in_=e_sc, func=AF.Ln, bias=1.0),
                        inc=sS)
                    bank_free[bz] = [(sS, v2)]
                S.emit("gpsimd", lambda e, c_=c, o_=ot: e.dma_start(
                    out=cc_di[c_][o_ * 128:(o_ + 1) * 128, :], in_=dlt),
                    waits=[(sS, S.val(sS))], inc=sDG, inc_n=16)
                vu = S.emit("vector", lambda e, o_=ot: e.tensor_mul(
                    ut, dlt, actm[:, o_ * TCA:(o_ + 1) * TCA]),
                    waits=[(sS, S.val(sS)), (sDG, S.val(sDG))], inc=sV)
                S.emit("gpsimd", lambda e, c_=c, o_=ot: e.dma_start(
                    out=cc_ui[c_][o_ * 128:(o_ + 1) * 128, :], in_=ut),
                    waits=[(sV, vu)], inc=sDG, inc_n=16)
                free_wslot(sl)

            for gi in range(2):
                mo = gi * 512
                bz, fz = take_bank()
                lz = None
                for kt in range(16):
                    lz = mm(PSB[bz][0:32, 0:512], wbc_b[:, kt * 32:(kt + 1) * 32],
                            actm[:, kt * TCA + mo: kt * TCA + mo + 512],
                            start=(kt == 0), stop=(kt == 15),
                            waits=([act_done, (sV, vbc0)] +
                                   fz) if kt == 0 else [])
                vz = S.emit("scalar", lambda e, b=bz, m_=mo: e.copy(
                    bct[:, m_:m_ + 512], PSB[b][0:32, 0:512]),
                    waits=[(sT, lz), (sDG, S.val(sDG))], inc=sS)
                bank_free[bz] = [(sS, vz)]
            S.emit("gpsimd", lambda e, c_=c: e.dma_start(
                out=cc_bi[c_], in_=bct), waits=[(sS, S.val(sS))], inc=sDG, inc_n=16)

            ag_d.append(S.emit("gpsimd", lambda e, c_=c: e.collective_compute(
                "AllGather", ALU.bypass, ins=[cc_di[c_]], outs=[cc_do[c_]],
                replica_groups=RG), waits=[(sDG, S.val(sDG))], inc=sG))
            ag_u.append(S.emit("gpsimd", lambda e, c_=c: e.collective_compute(
                "AllGather", ALU.bypass, ins=[cc_ui[c_]], outs=[cc_uo[c_]],
                replica_groups=RG), inc=sG))
            ag_b.append(S.emit("gpsimd", lambda e, c_=c: e.collective_compute(
                "AllGather", ALU.bypass, ins=[cc_bi[c_]], outs=[cc_bo[c_]],
                replica_groups=RG), inc=sG))

        rs_done = [None, None]
        dl_free = [None, None]
        u_free = [None, None]
        a_free = [None, None]
        y_free = None
        rep_guard = [None, None]
        mul_last = None
        free_ph = None
        free_p = None
        for s_ in range(NCB):
            j = s_ // 2
            lc = 2 * (s_ % 2)
            half = s_ % 2
            ag_wait = [(sG, ag_b[lc + 1])]
            if s_ == 0:
                ag_wait += barrier_waits()
            vb1 = S.emit("sync", lambda e, l_=lc, j_=j: e.dma_start(
                out=bc_g[:, 0:TCA], in_=cc_bo[l_][j_]),
                waits=ag_wait + [(sT, S.val(sT))], inc=sBC, inc_n=16)
            vb2 = S.emit("sync", lambda e, l_=lc, j_=j: e.dma_start(
                out=bc_g[:, TCA:TCB], in_=cc_bo[l_ + 1][j_]), inc=sBC, inc_n=16)
            rep_last = None
            first_rep = True
            for nn in range(4):
                for (dst, row) in ((brep, nn), (crep, 4 + nn)):
                    for sp in range(4):
                        bk, fk = take_bank()
                        vrp = mm(PSB[bk][:, 0:512],
                                 selb[:, row * 128:(row + 1) * 128],
                                 bc_g[:, sp * 512:(sp + 1) * 512],
                                 start=True, stop=True,
                                 waits=[(sBC, vb2)] + fk)
                        gw = ([(sV, S.val(sV)), (sP, S.val(sP))]
                              if first_rep else [])
                        first_rep = False
                        rep_last = S.emit("scalar", lambda e, d=dst, n_=nn, s2=sp,
                                          b=bk: e.copy(
                                          d[:, n_ * TCB + s2 * 512:
                                            n_ * TCB + s2 * 512 + 512],
                                          PSB[b][:, 0:512]),
                                          waits=[(sT, vrp)] + gw, inc=sS)
                        bank_free[bk] = [(sS, rep_last)]
            reps_ok = (sS, rep_last)
            if DBG and s_ == 0:
                S.emit("gpsimd", lambda e: e.dma_start(out=dbg_out["brep"], in_=brepb[0]),
                       waits=[reps_ok], inc=sDG, inc_n=16)
                S.emit("gpsimd", lambda e: e.dma_start(out=dbg_out["crep"], in_=crepb[0]),
                       waits=[reps_ok], inc=sDG, inc_n=16)

            for dt in range(16):
                sl = dt % 2
                sB = sB0 if sl == 0 else sB1
                qe = "sync" if sl == 0 else "gpsimd"
                vd1 = S.emit(qe, lambda e, l_=lc, j_=j, d_=dt, s2=sl: e.dma_start(
                    out=dl_g[:, s2 * TCB: s2 * TCB + TCA],
                    in_=cc_do[l_][j_][d_ * 128:(d_ + 1) * 128, :]),
                    waits=[dl_free[sl], (sG, ag_d[lc + 1])], inc=sB, inc_n=16)
                S.emit(qe, lambda e, l_=lc, j_=j, d_=dt, s2=sl: e.dma_start(
                    out=dl_g[:, s2 * TCB + TCA: (s2 + 1) * TCB],
                    in_=cc_do[l_ + 1][j_][d_ * 128:(d_ + 1) * 128, :]),
                    inc=sB, inc_n=16)
                S.emit(qe, lambda e, l_=lc, j_=j, d_=dt, s2=sl: e.dma_start(
                    out=u_g[:, s2 * TCB: s2 * TCB + TCA],
                    in_=cc_uo[l_][j_][d_ * 128:(d_ + 1) * 128, :]),
                    waits=[u_free[sl], (sG, ag_u[lc + 1])], inc=sB, inc_n=16)
                vu2 = S.emit(qe, lambda e, l_=lc, j_=j, d_=dt, s2=sl: e.dma_start(
                    out=u_g[:, s2 * TCB + TCA: (s2 + 1) * TCB],
                    in_=cc_uo[l_ + 1][j_][d_ * 128:(d_ + 1) * 128, :]),
                    inc=sB, inc_n=16)
                dma_ok = (sB, vu2)
                last_a = None
                last_b = None
                for nn in range(4):
                    idx = dt * 4 + nn
                    asl = idx % 2
                    va = S.emit("scalar", lambda e, s2=sl, a2=asl, i_=idx: e.activation(
                        out=a_t[:, a2 * TCB:(a2 + 1) * TCB],
                        in_=dl_g[:, s2 * TCB:(s2 + 1) * TCB], func=AF.Exp,
                        scale=a_sc[:, i_:i_ + 1]),
                        waits=[dma_ok, a_free[asl]], inc=sS)
                    last_a = va
                    vbm = S.emit("vector", lambda e, s2=sl, a2=asl, n_=nn,
                                 h2=half: e.tensor_mul(
                        b_t[:, a2 * TCB:(a2 + 1) * TCB],
                        u_g[:, s2 * TCB:(s2 + 1) * TCB],
                        brep[:, n_ * TCB:(n_ + 1) * TCB]),
                        waits=[dma_ok, reps_ok], inc=sV)
                    last_b = vbm
                    init_ap = 0.0 if s_ == 0 else h_last[:, idx:idx + 1]
                    vsc = S.emit("vector", lambda e, a2=asl, ini=init_ap:
                                 e.tensor_tensor_scan(
                                 out=h_t, data0=a_t[:, a2 * TCB:(a2 + 1) * TCB],
                                 data1=b_t[:, a2 * TCB:(a2 + 1) * TCB],
                                 initial=ini,
                                 op0=ALU.mult, op1=ALU.add),
                                 waits=[(sS, va)], inc=sV)
                    a_free[asl] = (sV, vsc)
                    if DBG and s_ == 0 and dt == 0 and nn == 0:
                        S.emit("gpsimd", lambda e, a2=asl: e.dma_start(
                            out=dbg_out["a0f"], in_=a_t[:, a2 * TCB:(a2 + 1) * TCB]),
                            waits=[(sV, vsc)], inc=sDG, inc_n=16)
                        S.emit("gpsimd", lambda e, a2=asl: e.dma_start(
                            out=dbg_out["b0"], in_=b_t[:, a2 * TCB:(a2 + 1) * TCB]),
                            waits=[(sV, vsc)], inc=sDG, inc_n=16)
                        vdmp = S.emit("gpsimd", lambda e: e.dma_start(
                            out=dbg_out["h0"], in_=h_t), waits=[(sV, vsc)],
                            inc=sDG, inc_n=16)
                        S.emit("vector", lambda e: e.engine_nop(),
                               waits=[(sDG, vdmp)], inc=sV)
                    S.emit("vector", lambda e: e.engine_nop(),
                           waits=[(sV, vsc)], inc=sV)
                    S.emit("vector", lambda e, i_=idx: e.tensor_copy(
                        h_last[:, i_:i_ + 1], h_t[:, TCB - 1:TCB]), inc=sV)
                    dst = ph_t if nn % 2 == 0 else p_t
                    S.emit("vector", lambda e, d=dst, n_=nn: e.tensor_mul(
                        d, h_t, crep[:, n_ * TCB:(n_ + 1) * TCB]), inc=sV)
                    if nn == 1:
                        S.emit("vector", lambda e: e.tensor_add(s1_t, ph_t, p_t),
                               inc=sV)
                    elif nn == 3:
                        S.emit("vector", lambda e: e.tensor_add(p_t, ph_t, p_t),
                               inc=sV)
                        vy = S.emit("vector", lambda e: e.tensor_add(y_t, s1_t, p_t),
                                    waits=[y_free], inc=sV)
                if DBG and s_ == 0 and dt == 0:
                    vdy = S.emit("gpsimd", lambda e: e.dma_start(
                        out=dbg_out["y0"], in_=y_t), waits=[(sV, vy)],
                        inc=sDG, inc_n=16)
                    S.emit("vector", lambda e: e.engine_nop(),
                           waits=[(sDG, vdy)], inc=sV)
                dl_free[sl] = (sS, last_a)
                u_free[sl] = (sV, last_b)
                y_free = (sDS, S.emit("scalar",
                          lambda e, h_=half, j_=j, d_=dt: e.dma_start(
                          out=cc_yp[h_][d_ // 4][j_, (d_ % 4) * 128:
                                                  (d_ % 4 + 1) * 128, :],
                          in_=y_t),
                          waits=[(sV, vy)], inc=sDS, inc_n=16))
                if s_ == 7 and dt % 4 == 3:
                    for hh in range(2):
                        rs_done[hh] = S.emit("gpsimd",
                            lambda e, h2=hh, q2=dt // 4: e.collective_compute(
                            "ReduceScatter", ALU.add, ins=[cc_yp[h2][q2]],
                            outs=[cc_yr[h2][q2]], replica_groups=RG),
                            waits=[(sDG, S.val(sDG)), (sDS, S.val(sDS))], inc=sG)
            rep_guard[half] = (sV, S.val(sV))

        for c in range(NCC):
            cache_bar[0] = [(sDS, S.val(sDS))]
            h_ = c // 2
            off = (c % 2) * TCC
            S.emit("sync", lambda e, c_=c: e.dma_start(
                out=actc.rearrange("p (t w) -> p t w", t=16),
                in_=act_sp[:, c_ * TCC:(c_ + 1) * TCC].rearrange(
                    "(t p) w -> p t w", p=128)),
                waits=barrier_waits(), inc=sDW, inc_n=16)
            S.emit("sync", lambda e, c_=c: e.dma_start(
                out=gc.rearrange("p (t w) -> p t w", t=16),
                in_=g_sp[:, c_ * TCC:(c_ + 1) * TCC].rearrange(
                    "(t p) w -> p t w", p=128)), inc=sDW, inc_n=16)
            vy_ = None
            for q in range(4):
                vy_ = S.emit("sync", lambda e, h2=h_, o_=off, q_=q: e.dma_start(
                    out=yc.rearrange("p (t w) -> p t w", t=16)[:, q_ * 4:
                                                              (q_ + 1) * 4, :],
                    in_=cc_yr[h2][q_].rearrange(
                        "(t p) w -> p t w", p=128)[:, :, o_:o_ + TCC]),
                    waits=[(sG, rs_done[h_])] if q == 0 else [],
                    inc=sDW, inc_n=16)
            vx_ = S.emit("sync", lambda e, c_=c: e.dma_start(
                out=x1c.rearrange("p (t w) -> p t w", t=8),
                in_=x1_sp[:, c_ * TCC:(c_ + 1) * TCC].rearrange(
                    "(t p) w -> p t w", p=128)), inc=sDW, inc_n=16)
            dok = (sDW, vx_)
            for dt in range(16):
                S.emit("vector", lambda e, d_=dt: e.scalar_tensor_tensor(
                    yc[:, d_ * TCC:(d_ + 1) * TCC], actc[:, d_ * TCC:(d_ + 1) * TCC],
                    dskq[:, d_:d_ + 1], yc[:, d_ * TCC:(d_ + 1) * TCC],
                    ALU.mult, ALU.add), waits=[dok] if dt == 0 else [], inc=sV)
                S.emit("vector", lambda e, d_=dt: e.tensor_mul(
                    ycb[:, d_ * TCC:(d_ + 1) * TCC], yc[:, d_ * TCC:(d_ + 1) * TCC],
                    gc[:, d_ * TCC:(d_ + 1) * TCC]), inc=sV)
            gat = (sV, S.val(sV))
            for ot in range(8):
                wb_, wr_, sl = load_wtile(w_ot[ot] if (c == 0 or NOCACHE) else None,
                                          None if (c == 0 or NOCACHE) else c_ot[ot], 16,
                                          c_ot[ot] if (c == 0 or NOCACHE) else None)
                for gi in range(2):
                    mo = gi * 512
                    bz, fz = take_bank()
                    lz = None
                    for kt in range(16):
                        lz = mm(PSB[bz][:, 0:512], wb_[:, kt * 128:(kt + 1) * 128],
                                ycb[:, kt * TCC + mo: kt * TCC + mo + 512],
                                start=(kt == 0), stop=(kt == 15),
                                waits=([wr_, gat] + fz)
                                if kt == 0 else [])
                    vep = S.emit("vector", lambda e, b=bz, o_=ot, m_=mo:
                                 e.scalar_tensor_tensor(
                                 x2st[:, m_:m_ + 512], PSB[b][:, 0:512],
                                 biasq[:, 64 + o_:64 + o_ + 1],
                                 x1c[:, o_ * TCC + m_: o_ * TCC + m_ + 512],
                                 ALU.add, ALU.add),
                                 waits=[(sT, lz), (sDS, S.val(sDS))], inc=sV)
                    bank_free[bz] = [(sV, vep)]
                    S.emit("scalar", lambda e, c_=c, o_=ot, m_=mo: e.dma_start(
                        out=x2_sp[o_ * 128:(o_ + 1) * 128,
                                  c_ * TCC + m_: c_ * TCC + m_ + 512],
                        in_=x2st[:, m_:m_ + 512]),
                        waits=[(sV, vep)], inc=sDS, inc_n=16)
                free_wslot(sl)
        x2_all = (sDS, S.val(sDS))

        for c in range(NCA):
            cache_bar[0] = [(sDS, S.val(sDS))]
            W = TCA
            movs = [(0, 512), (512, 512)]
            tgroups = [(i * 128, 128) for i in range(8)]
            vld = S.emit("sync", lambda e, c_=c: e.dma_start(
                out=x_c[:, 0:8 * W].rearrange("p (t w) -> p t w", t=8),
                in_=x2_sp[:, c_ * TCA:(c_ + 1) * TCA].rearrange(
                    "(t p) w -> p t w", p=128)),
                waits=[x2_all] + barrier_waits(), inc=sDW, inc_n=16)
            S.emit("vector", lambda e: e.engine_nop(), waits=[(sDW, vld)], inc=sV)
            emit_ffn(W, movs, tgroups, cache_mode=False, final_chunk=c)

        if DBG:
            S.emit("gpsimd", lambda e: e.dma_start(out=dbg_out["asc"], in_=a_sc),
                   waits=[(sFin, S.val(sFin))], inc=sDG, inc_n=16)
            S.emit("gpsimd", lambda e: e.dma_start(out=dbg_out["hl"], in_=h_last),
                   waits=[(sFin, S.val(sFin))], inc=sDG, inc_n=16)
            pairs = [("do0", cc_do[0].rearrange("s d w -> (s d) w")),
                     ("uo0", cc_uo[0].rearrange("s d w -> (s d) w")),
                     ("x1", x1_sp), ("act", act_sp), ("gact", g_sp),
                     ("delta", cc_di[0]), ("u", cc_ui[0]), ("bc", cc_bi[0]),
                     ("x2", x2_sp)]
            for nm, src in pairs:
                S.emit("gpsimd", lambda e, d=dbg_out[nm], s2=src: e.dma_start(
                    out=d, in_=s2), waits=[(sFin, S.val(sFin))], inc=sDG, inc_n=16)

        S.emit("gpsimd", None, waits=[(sFin, S.val(sFin)), (sDG, S.val(sDG)),
                                      (sDS, S.val(sDS)), (sDW, S.val(sDW))])

        @blk.sync
        def _(e):
            S.replay("sync", e)

        @blk.vector
        def _(e):
            S.replay("vector", e)

        @blk.scalar
        def _(e):
            S.replay("scalar", e)

        @blk.tensor
        def _(e):
            S.replay("tensor", e)

        @blk.gpsimd
        def _(e):
            S.replay("gpsimd", e)

    return nc


def _tile_w(w):
    K, O = w.shape
    return np.ascontiguousarray(
        w.reshape(K // 128, 128, O // 128, 128).transpose(2, 1, 0, 3)
        .reshape(O // 128, 128, (K // 128) * 128))


def _bias_cols(b, n):
    return np.ascontiguousarray(b.reshape(n, 128).T)


_NC_CACHE = {}


def kernel(**inp):
    x = inp["L0_attn_x"]
    B, L, _ = x.shape
    if "prog" not in _NC_CACHE:
        _NC_CACHE["prog"] = build_program()
    nc = _NC_CACHE["prog"]

    base = {
        "w_sg": _tile_w(np.ascontiguousarray(inp["L0_ffn_shared_gate_w"].T)),
        "w_su": _tile_w(np.ascontiguousarray(inp["L0_ffn_shared_up_w"].T)),
        "w_sd": _tile_w(np.ascontiguousarray(inp["L0_ffn_shared_down_w"].T)),
        "w_gu": np.stack([_tile_w(np.ascontiguousarray(
            inp["L0_ffn_gate_up_proj"][e].T)) for e in range(8)]),
        "w_dn": np.stack([_tile_w(np.ascontiguousarray(
            inp["L0_ffn_down_proj"][e].T)) for e in range(8)]),
        "w_r9": np.ascontiguousarray(
            np.concatenate([inp["L0_ffn_router_weight"].T,
                            inp["L0_ffn_shared_expert_gate_w"].T],
                           axis=1).reshape(8, 128, 9)),
        "w_in": _tile_w(inp["L0_attn_W_in"]),
        "w_gt": _tile_w(inp["L0_attn_W_gate"]),
        "w_dl": _tile_w(inp["L0_attn_W_delta"]),
        "w_bc": np.ascontiguousarray(
            np.concatenate([inp["L0_attn_W_B"], inp["L0_attn_W_C"]],
                           axis=1).reshape(16, 128, 32)),
        "w_ot": _tile_w(inp["L0_attn_W_out"]),
        "b_in": _bias_cols(inp["L0_attn_b_in"], 16),
        "b_gt": _bias_cols(inp["L0_attn_b_gate"], 16),
        "b_dl": _bias_cols(inp["L0_attn_b_delta"], 16),
        "b_cv": _bias_cols(inp["L0_attn_conv_b"], 16),
        "b_ot": _bias_cols(inp["L0_attn_b_out"], 8),
        "d_skip": _bias_cols(inp["L0_attn_D_skip"], 16),
        "ident": np.eye(128, dtype=np.float32),
    }
    cw = inp["L0_attn_conv_w"][:, 0, :]
    dg = np.zeros((16, 4, 128, 128), np.float32)
    ii = np.arange(128)
    for dt in range(16):
        for k in range(4):
            dg[dt, k, ii, ii] = cw[dt * 128:(dt + 1) * 128, k]
    base["w_cv"] = np.ascontiguousarray(
        dg.transpose(0, 2, 1, 3).reshape(16, 128, 512))
    alog = inp["L0_attn_A_log"]

    in_maps = []
    for core in range(8):
        g, r = core // 4, core % 4
        m = dict(base)
        xt = np.zeros((HP + TL, D), np.float32)
        lo = r * TL
        xt[HP:] = x[g, lo:lo + TL]
        if r > 0:
            xt[HP - 3:HP] = x[g, lo - 3:lo]
        m["x_fm"] = np.ascontiguousarray(xt.T)
        msk = np.ones((128, HP), np.float32)
        if r == 0:
            msk[:] = 0.0
        m["mask_h"] = msk
        a_sl = -np.exp(alog[:, 4 * r:4 * r + 4])
        m["a_own"] = np.ascontiguousarray(
            a_sl.reshape(16, 128, 4).transpose(1, 0, 2).reshape(128, 64))
        sel = np.zeros((32, 8, 128), np.float32)
        for nn in range(4):
            sel[4 * r + nn, nn, :] = 1.0
            sel[16 + 4 * r + nn, 4 + nn, :] = 1.0
        m["selbc"] = sel.reshape(32, 8 * 128)
        in_maps.append(m)

    res = run_bass_kernel_spmd(nc, in_maps, core_ids=list(range(8)))
    out = np.empty((B, L, D), np.float32)
    for core in range(8):
        g, r = core // 4, core % 4
        out[g, r * TL:(r + 1) * TL] = res.results[core]["out_fm"].T
    if DBG:
        kernel.dbg = [dict(res.results[c]) for c in range(8)]
    return out


# version 11
# speedup vs baseline: 1.1847x; 1.0028x over previous
import contextlib
import os
import numpy as np
import concourse.bass as bass
import concourse.mybir as mybir
from concourse.bass_utils import run_bass_kernel_spmd

F32 = mybir.dt.float32
BF16 = mybir.dt.bfloat16
U8 = mybir.dt.uint8
AF = mybir.ActivationFunctionType
ALU = mybir.AluOpType
AX = mybir.AxisListType

D = 1024
DI = 2048
TL = 4096
HP = 8
EPS = 1e-5
TCA = 1024
NCA = 4
TCB = 2048
NCB = 8
TCC = 1024
NCC = 4
WA = HP + TCA

RG = [[0, 1, 2, 3], [4, 5, 6, 7]]
DBG = bool(int(os.environ.get("MK_DBG", "0")))
NOCACHE = bool(int(os.environ.get("MK_NOCACHE", "0")))


class Sched:
    ENGINES = ("sync", "vector", "scalar", "tensor", "gpsimd")

    def __init__(self):
        self.steps = {e: [] for e in self.ENGINES}
        self.count = {}

    def emit(self, eng, fn, waits=(), inc=None, inc_n=1):
        after = None
        if inc is not None:
            self.count[inc] = self.count.get(inc, 0) + inc_n
            after = self.count[inc]
        self.steps[eng].append((fn, [w for w in waits if w is not None],
                                (inc, inc_n) if inc is not None else None))
        return after

    def val(self, sem):
        return self.count.get(sem, 0)

    def replay(self, eng_name, eng):
        for fn, waits, inc in self.steps[eng_name]:
            for (s_, v_) in waits:
                eng.wait_ge(s_, v_)
            r = fn(eng) if fn is not None else None
            if inc is not None:
                assert r is not None
                r.then_inc(inc[0], inc[1])


class Overlay:
    def __init__(self, nc, stack, name, nbytes):
        self.nc = nc
        self.back = stack.enter_context(nc.sbuf_tensor(name, [128, nbytes], U8))
        self.base = nc.lookup_mloc(self.back).addr

    def view(self, name, shape, dtype, off_bytes):
        h = self.nc.virtual_sbuf_tensor(name, list(shape), dtype)
        self.nc._tensor(h.name, h.shape, h.dtype, type="SB")
        mloc = self.nc.lookup_mloc(h)
        mloc.allocated = True
        mloc.addr = self.base + off_bytes
        return h.ap()


def build_program():
    nc = bass.Bass("TRN2", target_bir_lowering=False)

    def din(name, shape, dt=F32):
        return nc.dram_tensor(name, shape, dt, kind="ExternalInput").ap()

    x_in = din("x_fm", [D, HP + TL])
    w_sg = din("w_sg", [16, 128, 8 * 128])
    w_su = din("w_su", [16, 128, 8 * 128])
    w_sd = din("w_sd", [8, 128, 16 * 128])
    w_gu = din("w_gu", [8, 8, 128, 8 * 128])
    w_dn = din("w_dn", [8, 8, 128, 4 * 128])
    w_r9 = din("w_r9", [8, 128, 9])
    w_in = din("w_in", [16, 128, 8 * 128])
    w_gt = din("w_gt", [16, 128, 8 * 128])
    w_dl = din("w_dl", [16, 128, 16 * 128])
    w_bc = din("w_bc", [16, 128, 32])
    w_ot = din("w_ot", [8, 128, 16 * 128])
    w_cv = din("w_cv", [16, 128, 4 * 128])
    b_in = din("b_in", [128, 16])
    b_gt = din("b_gt", [128, 16])
    b_dl = din("b_dl", [128, 16])
    b_cv = din("b_cv", [128, 16])
    b_ot = din("b_ot", [128, 8])
    a_own = din("a_own", [128, 64])
    d_skip = din("d_skip", [128, 16])
    mask_h = din("mask_h", [128, HP])
    ident = din("ident", [128, 128])
    selbc = din("selbc", [32, 8 * 128])

    out_t = nc.dram_tensor("out_fm", [D, TL], F32, kind="ExternalOutput").ap()

    def dint(name, shape, dt=BF16):
        return nc.dram_tensor(name, shape, dt, kind="Internal").ap()

    c_sg = dint("c_sg", [16, 128, 8 * 128])
    c_su = dint("c_su", [16, 128, 8 * 128])
    c_sd = dint("c_sd", [8, 128, 16 * 128])
    c_gu = dint("c_gu", [8, 8, 128, 8 * 128])
    c_dn = dint("c_dn", [8, 8, 128, 4 * 128])
    c_in = dint("c_in", [16, 128, 8 * 128])
    c_gt = dint("c_gt", [16, 128, 8 * 128])
    c_dl = dint("c_dl", [16, 128, 16 * 128])
    c_ot = dint("c_ot", [8, 128, 16 * 128])
    c_cv = dint("c_cv", [16, 128, 4 * 128])
    x1_sp = dint("x1_sp", [D, TL])
    g_sp = dint("g_sp", [DI, TL])
    act_sp = dint("act_sp", [DI, TL])
    x2_sp = dint("x2_sp", [D, TL])
    cc_di = [dint(f"cc_di{c}", [DI, TCA]) for c in range(NCA)]
    cc_do = [dint(f"cc_do{c}", [4, DI, TCA]) for c in range(NCA)]
    cc_ui = [dint(f"cc_ui{c}", [DI, TCA]) for c in range(NCA)]
    cc_uo = [dint(f"cc_uo{c}", [4, DI, TCA]) for c in range(NCA)]
    cc_bi = [dint(f"cc_bi{c}", [32, TCA]) for c in range(NCA)]
    cc_bo = [dint(f"cc_bo{c}", [4, 32, TCA]) for c in range(NCA)]
    cc_yp = [[dint(f"cc_yp{h}_{q}", [4, 512, TCB], F32) for q in range(4)]
             for h in range(2)]
    cc_yr = [[dint(f"cc_yr{h}_{q}", [512, TCB], F32) for q in range(4)]
             for h in range(2)]

    dbg_out = {}
    if DBG:
        for nm, shp in (("x1", [D, TL]), ("act", [DI, TL]), ("gact", [DI, TL]),
                        ("delta", [DI, TCA]), ("u", [DI, TCA]),
                        ("bc", [32, TCA]), ("y", [DI, TCB]), ("x2", [D, TL]),
                        ("xln", [128, 8 * WA]), ("cwt", [1, 9 * WA]),
                        ("sact", [128, 16 * WA]), ("acts", [128, 32 * WA]),
                        ("yp", [4 * DI, TCB]), ("do0", [4 * DI, TCA]), ("uo0", [4 * DI, TCA]), ("brep", [128, 4 * TCB]), ("crep", [128, 4 * TCB]), ("b0", [128, TCB]), ("h0", [128, TCB]), ("y0", [128, TCB])):
            dbg_out[nm] = nc.dram_tensor("dbg_" + nm, shp, BF16,
                                         kind="ExternalOutput").ap()
        for nm, shp in (("cf", [128, 32]), ("lg", [128, 9]),
                        ("asc", [128, 64]), ("hl", [128, 64]),
                        ("a0f", [128, TCB])):
            dbg_out[nm] = nc.dram_tensor("dbg_" + nm, shp, F32,
                                         kind="ExternalOutput").ap()

    with contextlib.ExitStack() as stk:
        def sball(name, shape, dt):
            return stk.enter_context(nc.sbuf_tensor(name, shape, dt))[:]

        wbufA = sball("wbufA", [128, 16 * 128], BF16)
        wbufB = sball("wbufB", [128, 16 * 128], BF16)
        wbufC = sball("wbufC", [128, 16 * 128], BF16)
        wf32 = sball("wf32", [128, 16 * 128], F32)
        wf32B = sball("wf32B", [128, 16 * 128], F32)
        wr9f = sball("wr9f", [128, 72], F32)
        wr9 = sball("wr9", [128, 72], BF16)
        wbc_b = sball("wbc_b", [128, 512], BF16)
        sgs = sball("sgs", [128, WA], BF16)
        statb = sball("statb", [1, WA], BF16)
        cw_f = sball("cw_f", [128, 32], F32)
        identb = sball("identb", [128, 128], F32)
        maskb = sball("maskb", [128, HP], F32)
        biasq = sball("biasq", [128, 80], F32)
        dskq = sball("dskq", [128, 16], F32)
        a_sc = sball("a_sc", [128, 64], F32)
        h_last = sball("h_last", [128, 64], F32)
        onescol = sball("onescol", [128, 2], BF16)
        onesrow = sball("onesrow", [1, 128], BF16)
        selb = sball("selb", [32, 8 * 128], BF16)
        bct = sball("bct", [32, TCA], BF16)
        dlt = sball("dlt", [128, TCA], BF16)
        ut = sball("ut", [128, TCA], BF16)

        OV = Overlay(nc, stk, "ov", 167168)
        x_c = OV.view("x_c", [128, 8 * WA], BF16, 0)
        xln = OV.view("xln", [128, 8 * WA], BF16, 16512)
        cwrep = OV.view("cwrep", [128, 9 * WA], BF16, 33024)
        mrep = OV.view("mrep", [128, WA], BF16, 51600)
        rrep = OV.view("rrep", [128, WA], BF16, 53664)
        xsq = OV.view("xsq", [128, WA], BF16, 55728)
        e_sc = OV.view("e_sc", [128, 512], F32, 57792)
        acts = OV.view("acts", [128, 32 * WA], BF16, 63984)
        sact = OV.view("sact", [128, 16 * WA], BF16, 130032)
        xz = OV.view("xz", [128, 16 * WA], BF16, 63984)
        actm = OV.view("actm", [128, 16 * TCA], BF16, 97008)
        stat = OV.view("stat", [1, 4 * WA], F32, 97008)
        xf32 = OV.view("xf32", [128, 8 * 258], F32, 113520)
        cwT = OV.view("cwT", [1, 9 * WA], BF16, 144480)
        of32 = OV.view("of32", [128, 1024], F32, 163056)
        dl_g = OV.view("dl_g", [128, 2 * TCB], BF16, 0)
        u_g = OV.view("u_g", [128, 2 * TCB], BF16, 8192)
        bc_g = OV.view("bc_g", [32, TCB], BF16, 16384)
        bco = OV.view("bco", [8, TCB], BF16, 20480)
        brep = OV.view("brep", [128, 4 * TCB], BF16, 24576)
        crep = OV.view("crep", [128, 4 * TCB], BF16, 40960)
        brepB = OV.view("brepB", [128, 4 * TCB], BF16, 106496)
        crepB = OV.view("crepB", [128, 4 * TCB], BF16, 122880)
        brepb = [brep, brepB]
        crepb = [crep, crepB]
        a_t = OV.view("a_t", [128, 2 * TCB], BF16, 57344)
        b_t = OV.view("b_t", [128, 2 * TCB], BF16, 73728)
        h_t = OV.view("h_t", [128, TCB], BF16, 81920)
        p_t = OV.view("p_t", [128, TCB], BF16, 86016)
        ph_t = OV.view("ph_t", [128, TCB], BF16, 90112)
        s1_t = OV.view("s1_t", [128, TCB], BF16, 94208)
        y_t = OV.view("y_t", [128, TCB], F32, 98304)
        yc = OV.view("yc", [128, 16 * TCC], F32, 0)
        actc = OV.view("actc", [128, 16 * TCC], BF16, 65536)
        ycb = OV.view("ycb", [128, 16 * TCC], BF16, 65536)
        gc = OV.view("gc", [128, 16 * TCC], BF16, 98304)
        x1c = OV.view("x1c", [128, 8 * TCC], BF16, 131072)
        x2st = OV.view("x2st", [128, 1024], BF16, 147456)

        sDW = stk.enter_context(nc.semaphore("sDW"))
        sDW2 = stk.enter_context(nc.semaphore("sDW2"))
        sB0 = stk.enter_context(nc.semaphore("sB0"))
        sB1 = stk.enter_context(nc.semaphore("sB1"))
        sBC = stk.enter_context(nc.semaphore("sBC"))
        sP = stk.enter_context(nc.semaphore("sP"))
        sW3 = stk.enter_context(nc.semaphore("sW3"))
        sW4 = stk.enter_context(nc.semaphore("sW4"))
        sDG = stk.enter_context(nc.semaphore("sDG"))
        sDS = stk.enter_context(nc.semaphore("sDS"))
        sV = stk.enter_context(nc.semaphore("sV"))
        sS = stk.enter_context(nc.semaphore("sS"))
        sT = stk.enter_context(nc.semaphore("sT"))
        sG = stk.enter_context(nc.semaphore("sG"))
        sFin = stk.enter_context(nc.semaphore("sFin"))

        PSB = [stk.enter_context(nc.psum_tensor(f"pb{i}", [128, 512], F32))[:]
               for i in range(8)]

        blk = stk.enter_context(nc.Block())
        S = Sched()

        def mm(psum, lhsT, rhs, start, stop, waits=()):
            return S.emit("tensor",
                          lambda e, a=psum, b=lhsT, c=rhs, s1=start, s2=stop:
                          e.matmul(a, b, c, start=s1, stop=s2),
                          waits=waits, inc=sT)

        bank_free = {i: [] for i in range(8)}
        bank_rr = [0]

        def take_bank():
            i = (bank_rr[0] + 1) % 8
            bank_rr[0] = i
            return i, list(bank_free[i])

        wslot = [0]
        wslot_free = [[], [], []]

        wf32_free = [[], []]
        cache_bar = [[]]
        WBUFS = (wbufA, wbufB, wbufC)
        WSEMS = (sDW, sDW2, sW3)
        WENGS = ("sync", "gpsimd", "sync")

        def load_wtile(src_f32, src_bf, kt, cache_write):
            sl = wslot[0]
            wslot[0] = (wslot[0] + 1) % 3
            buf = WBUFS[sl]
            waits = list(wslot_free[sl])
            sem = WSEMS[sl]
            eng = WENGS[sl]
            if src_f32 is not None:
                stg = wf32 if sl % 2 == 0 else wf32B
                v = S.emit(eng, lambda e, s=src_f32, k=kt, g=stg: e.dma_start(
                    out=g[:, 0:k * 128], in_=s),
                    waits=waits + wf32_free[sl % 2], inc=sem, inc_n=16)
                v2 = S.emit("vector", lambda e, d=buf, k=kt, g=stg: e.tensor_copy(
                    d[:, 0:k * 128], g[:, 0:k * 128]),
                    waits=[(sem, v)], inc=sV)
                ready = (sV, v2)
                wf32_free[sl % 2] = [(sV, v2)]
                if cache_write is not None and not NOCACHE:
                    vcw = S.emit("scalar", lambda e, d=cache_write, s=buf, k=kt:
                                 e.dma_start(
                                 out=d, in_=s[:, 0:k * 128]),
                                 waits=[ready], inc=sDS, inc_n=16)
                    wslot_cache[sl] = (sDS, vcw)
            else:
                v = S.emit(eng, lambda e, s=src_bf, d=buf, k=kt: e.dma_start(
                    out=d[:, 0:k * 128], in_=s),
                    waits=waits + cache_bar[0], inc=sem, inc_n=16)
                ready = (sem, v)
            return buf, ready, sl

        wslot_cache = [None, None, None]

        def free_wslot(sl):
            wslot_free[sl] = [(sT, S.val(sT))]
            if wslot_cache[sl] is not None:
                wslot_free[sl].append(wslot_cache[sl])
                wslot_cache[sl] = None

        def barrier_waits():
            return [(sT, S.val(sT)), (sS, S.val(sS)), (sV, S.val(sV)),
                    (sDG, S.val(sDG)), (sDS, S.val(sDS))]

        S.emit("sync", lambda e: e.dma_start(out=identb, in_=ident), inc=sDW, inc_n=16)
        S.emit("sync", lambda e: e.dma_start(out=maskb, in_=mask_h), inc=sDW, inc_n=16)
        for (c0, c1, src) in ((0, 16, b_in), (16, 32, b_gt), (32, 48, b_dl),
                              (48, 64, b_cv), (64, 72, b_ot)):
            S.emit("sync", lambda e, a=c0, b=c1, s=src: e.dma_start(
                out=biasq[:, a:b], in_=s), inc=sDW, inc_n=16)
        S.emit("sync", lambda e: e.dma_start(out=dskq, in_=d_skip), inc=sDW, inc_n=16)
        S.emit("sync", lambda e: e.dma_start(out=a_sc, in_=a_own), inc=sDW, inc_n=16)
        vself = S.emit("sync", lambda e: e.dma_start(out=wf32[0:32, 0:1024], in_=selbc),
                       inc=sDW, inc_n=16)
        vselc = S.emit("vector", lambda e: e.tensor_copy(selb, wf32[0:32, 0:1024]),
                       waits=[(sDW, vself)], inc=sV)
        vload = S.emit("sync", lambda e: e.dma_start(
            out=wf32[:, 0:512].rearrange("p (k o) -> p k o", k=16),
            in_=w_bc.rearrange("k p o -> p k o")),
            waits=[(sV, vselc)], inc=sDW, inc_n=16)
        S.emit("vector", lambda e: e.memset(onescol, 1.0), inc=sV)
        S.emit("vector", lambda e: e.memset(onesrow, 1.0), inc=sV)
        S.emit("vector", lambda e: e.memset(h_last, 0.0), inc=sV)
        vbc0 = S.emit("vector", lambda e: e.tensor_copy(wbc_b, wf32[:, 0:512]),
                      waits=[(sDW, vload)], inc=sV)
        wf32_free[0] = [(sV, vbc0)]
        wf32_free[1] = [(sV, vbc0)]
        prep_done = [(sDW, S.val(sDW)), (sG, S.val(sG)), (sV, S.val(sV))]

        def emit_ln(W, movs):
            gate = [(sV, S.val(sV)), (sS, S.val(sS)), (sT, S.val(sT))]
            vm = vq = None
            for (mo, mw) in movs:
                bsum, w0 = take_bank()
                last_sum = None
                for it in range(8):
                    ws = (w0 + gate) if it == 0 else []
                    last_sum = mm(PSB[bsum][0:1, 0:mw], onescol[:, 0:1],
                                  x_c[:, it * W + mo: it * W + mo + mw],
                                  start=(it == 0), stop=(it == 7), waits=ws)
                vm = S.emit("vector", lambda e, m_=mo, w_=mw, b=bsum: e.tensor_scalar(
                    stat[0:1, m_:m_ + w_], PSB[b][0:1, 0:w_], 1.0 / D, None,
                    ALU.mult), waits=[(sT, last_sum)], inc=sV)
                bank_free[bsum] = [(sV, vm)]
            for (mo, mw) in movs:
                bsq, w1 = take_bank()
                last_sq = None
                for it in range(8):
                    vs = S.emit("scalar", lambda e, it_=it, W_=W, m_=mo, w_=mw:
                                e.activation(
                                out=xsq[:, 0:w_],
                                in_=x_c[:, it_ * W_ + m_: it_ * W_ + m_ + w_],
                                func=AF.Square), inc=sS, waits=[(sT, S.val(sT))])
                    last_sq = mm(PSB[bsq][0:1, 0:mw], onescol[:, 0:1],
                                 xsq[:, 0:mw], start=(it == 0),
                                 stop=(it == 7),
                                 waits=[(sS, vs)] + (w1 if it == 0 else []))
                vq = S.emit("vector", lambda e, m_=mo, w_=mw, b=bsq, W_=W:
                    e.tensor_scalar(
                    stat[0:1, W_ + m_:W_ + m_ + w_], PSB[b][0:1, 0:w_], 1.0 / D,
                    None, ALU.mult), waits=[(sT, last_sq)], inc=sV)
                bank_free[bsq] = [(sV, vq)]
            S.emit("vector", lambda e, W_=W: e.scalar_tensor_tensor(
                stat[0:1, 2 * W_:3 * W_], stat[0:1, 0:W_], -1.0, stat[0:1, 0:W_],
                ALU.mult, ALU.mult), waits=[(sV, vm)], inc=sV)
            vv2 = S.emit("vector", lambda e, W_=W: e.tensor_add(
                stat[0:1, 2 * W_:3 * W_], stat[0:1, 2 * W_:3 * W_],
                stat[0:1, W_:2 * W_]), waits=[(sV, vq)], inc=sV)
            vv3 = S.emit("vector", lambda e, W_=W: e.tensor_scalar(
                stat[0:1, 2 * W_:3 * W_], stat[0:1, 2 * W_:3 * W_], EPS, None,
                ALU.add), inc=sV)
            vg = S.emit("vector", lambda e: e.engine_nop(),
                        waits=[(sV, vv3)], inc=sV)
            vl = S.emit("scalar", lambda e, W_=W: e.activation(
                out=stat[0:1, 3 * W_:4 * W_], in_=stat[0:1, 2 * W_:3 * W_],
                func=AF.Ln), waits=[(sV, vg)], inc=sS)
            vneg = S.emit("vector", lambda e, W_=W: e.tensor_scalar(
                stat[0:1, 3 * W_:4 * W_], stat[0:1, 3 * W_:4 * W_], -0.5, None,
                ALU.mult), waits=[(sS, vl)], inc=sV)
            vgx = S.emit("vector", lambda e: e.engine_nop(), waits=[(sV, vneg)], inc=sV)
            vr = S.emit("scalar", lambda e, W_=W: e.activation(
                out=stat[0:1, 3 * W_:4 * W_], in_=stat[0:1, 3 * W_:4 * W_],
                func=AF.Exp), waits=[(sV, vgx)], inc=sS)
            evs = []
            for (dst, row, prow) in ((mrep, 0, (sV, vm)), (rrep, 3, (sS, vr))):
                vb = S.emit("vector", lambda e, r_=row, W_=W: e.tensor_copy(
                    statb[0:1, 0:W_], stat[0:1, r_ * W_:r_ * W_ + W_]),
                    waits=[prow, (sT, S.val(sT))] +
                          ([(sS, evs[-1])] if evs else []), inc=sV)
                o2 = 0
                while o2 < W:
                    w2 = min(512, W - o2)
                    bk, wfree = take_bank()
                    vrp = mm(PSB[bk][:, 0:w2], onesrow,
                             statb[0:1, o2:o2 + w2], start=True, stop=True,
                             waits=[(sV, vb)] + wfree)
                    ve = S.emit("scalar", lambda e, d=dst, o_=o2, w_=w2, b=bk:
                                e.copy(d[:, o_:o_ + w_], PSB[b][:, 0:w_]),
                                waits=[(sT, vrp)], inc=sS)
                    bank_free[bk] = [(sS, ve)]
                    evs.append(ve)
                    o2 += w2
            wrep = (sS, evs[-1])
            lastv = None
            for it in range(8):
                S.emit("vector", lambda e, it_=it, W_=W: e.tensor_sub(
                    xln[:, it_ * W_: it_ * W_ + W_],
                    x_c[:, it_ * W_: it_ * W_ + W_], mrep[:, 0:W_]),
                    waits=[wrep, (sT, S.val(sT))] if it == 0 else [], inc=sV)
                lastv = S.emit("vector", lambda e, it_=it, W_=W: e.tensor_mul(
                    xln[:, it_ * W_: it_ * W_ + W_],
                    xln[:, it_ * W_: it_ * W_ + W_], rrep[:, 0:W_]), inc=sV)
            return (sV, lastv)

        def emit_ffn(W, movs, tgroups, cache_mode, final_chunk=None, dbg_dump=False):
            ln_done = emit_ln(W, movs)
            vr9 = S.emit("sync", lambda e: e.dma_start(
                out=wr9f.rearrange("p (k o) -> p k o", k=8),
                in_=w_r9.rearrange("k p o -> p k o")),
                waits=[(sV, S.val(sV))], inc=sDW, inc_n=16)
            vr9c = S.emit("vector", lambda e: e.tensor_copy(wr9, wr9f),
                          waits=[(sDW, vr9)], inc=sV)
            for (go, gw) in tgroups:
                bk, wfree = take_bank()
                lastg = None
                for kt in range(8):
                    lastg = mm(PSB[bk][0:gw, 0:9],
                               xln[:, kt * W + go: kt * W + go + gw],
                               wr9[:, kt * 9:(kt + 1) * 9],
                               start=(kt == 0), stop=(kt == 7),
                               waits=([ln_done, (sV, vr9c)] +
                                      wfree) if kt == 0 else [])
                g = gw
                cf = cw_f

                def vop(fn, waits=()):
                    return S.emit("vector", fn, waits=waits, inc=sV)

                def guard(v):
                    return S.emit("vector", lambda e: e.engine_nop(),
                                  waits=[(sV, v)], inc=sV)

                v1 = vop(lambda e, g=g, bk=bk: e.reduce_max(cf[0:g, 0:1], PSB[bk][0:g, 0:8],
                         axis=AX.X), waits=[(sT, lastg)])
                guard(v1)
                v2 = vop(lambda e, g=g, bk=bk: e.tensor_scalar(cf[0:g, 1:2], cf[0:g, 0:1],
                         -1.0, None, ALU.mult))
                guard(v2)
                v3 = S.emit("scalar", lambda e, b=bk, g=g: e.activation(
                    out=cf[0:g, 2:10], in_=PSB[b][0:g, 0:8], func=AF.Exp,
                    bias=cf[0:g, 1:2]), waits=[(sV, S.val(sV))], inc=sS)
                v4 = vop(lambda e, g=g, bk=bk: e.reduce_sum(cf[0:g, 10:11], cf[0:g, 2:10],
                         axis=AX.X), waits=[(sS, v3)])
                guard(v4)
                v5 = vop(lambda e, g=g, bk=bk: e.reciprocal(cf[0:g, 10:11], cf[0:g, 10:11]))
                guard(v5)
                guard(vop(lambda e, g=g, bk=bk: e.tensor_scalar(cf[0:g, 2:10], cf[0:g, 2:10],
                    cf[0:g, 10:11], None, ALU.mult)))
                v6 = vop(lambda e, g=g, bk=bk: e.reduce_max(cf[0:g, 0:1], cf[0:g, 2:10],
                         axis=AX.X))
                guard(v6)
                guard(vop(lambda e, g=g, bk=bk: e.tensor_scalar(cf[0:g, 30:31], cf[0:g, 0:1],
                    -1.0, None, ALU.mult)))
                guard(vop(lambda e, g=g, bk=bk: e.tensor_scalar(cf[0:g, 12:20], cf[0:g, 2:10],
                    cf[0:g, 30:31], None, ALU.add)))
                guard(vop(lambda e, g=g, bk=bk: e.tensor_scalar(cf[0:g, 12:20], cf[0:g, 12:20],
                    0.0, None, ALU.is_ge)))
                guard(vop(lambda e, g=g, bk=bk: e.tensor_scalar(cf[0:g, 20:28], cf[0:g, 12:20],
                    -1.0, 1.0, ALU.mult, ALU.add)))
                v8 = vop(lambda e, g=g, bk=bk: e.tensor_mul(cf[0:g, 20:28], cf[0:g, 20:28],
                         cf[0:g, 2:10]))
                guard(v8)
                v9 = vop(lambda e, g=g, bk=bk: e.reduce_max(cf[0:g, 1:2], cf[0:g, 20:28],
                         axis=AX.X))
                guard(v9)
                guard(vop(lambda e, g=g, bk=bk: e.tensor_scalar(cf[0:g, 30:31], cf[0:g, 1:2],
                    -1.0, None, ALU.mult)))
                guard(vop(lambda e, g=g, bk=bk: e.tensor_scalar(cf[0:g, 20:28], cf[0:g, 20:28],
                    cf[0:g, 30:31], None, ALU.add)))
                guard(vop(lambda e, g=g, bk=bk: e.tensor_scalar(cf[0:g, 20:28], cf[0:g, 20:28],
                    0.0, None, ALU.is_ge)))
                guard(vop(lambda e, g=g, bk=bk: e.tensor_scalar(cf[0:g, 12:20], cf[0:g, 12:20],
                    cf[0:g, 0:1], None, ALU.mult)))
                guard(vop(lambda e, g=g, bk=bk: e.tensor_scalar(cf[0:g, 20:28], cf[0:g, 20:28],
                    cf[0:g, 1:2], None, ALU.mult)))
                guard(vop(lambda e, g=g, bk=bk: e.tensor_add(cf[0:g, 12:20], cf[0:g, 12:20],
                    cf[0:g, 20:28])))
                va = vop(lambda e, g=g, bk=bk: e.tensor_add(cf[0:g, 0:1], cf[0:g, 0:1],
                         cf[0:g, 1:2]))
                guard(va)
                vb_ = vop(lambda e, g=g, bk=bk: e.reciprocal(cf[0:g, 0:1], cf[0:g, 0:1]))
                guard(vb_)
                vqc = vop(lambda e, g=g, bk=bk: e.tensor_scalar(cf[0:g, 12:20], cf[0:g, 12:20],
                          cf[0:g, 0:1], None, ALU.mult))
                guard(vqc)
                vs1 = S.emit("scalar", lambda e, b=bk, g=g: e.activation(
                    out=cf[0:g, 28:29], in_=PSB[b][0:g, 8:9], func=AF.Exp,
                    scale=-1.0), waits=[(sT, lastg)], inc=sS)
                vs2 = vop(lambda e, g=g, bk=bk: e.tensor_scalar(cf[0:g, 28:29], cf[0:g, 28:29],
                          1.0, None, ALU.add), waits=[(sS, vs1)])
                guard(vs2)
                vs3 = vop(lambda e, g=g, bk=bk: e.reciprocal(cf[0:g, 28:29], cf[0:g, 28:29]))
                guard(vs3)
                vcp = vop(lambda e, g=g, bk=bk: e.tensor_copy(cf[0:g, 20:21], cf[0:g, 28:29]),
                          waits=[(sV, vqc)])
                bank_free[bk] = [(sV, vcp), (sS, vs1)]
                if dbg_dump and DBG and go == 0:
                    S.emit("gpsimd", lambda e, g=g, bk=bk: e.dma_start(
                        out=dbg_out["cf"], in_=cf), waits=[(sV, vcp)],
                        inc=sDG, inc_n=16)
                    S.emit("scalar", lambda e, b=bk, g=g: e.copy(
                        e_sc[:, 0:9], PSB[b][0:128, 0:9]),
                        waits=[(sS, vs1)], inc=sS)
                    S.emit("gpsimd", lambda e, g=g, bk=bk: e.dma_start(
                        out=dbg_out["lg"], in_=e_sc[:, 0:9]),
                        waits=[(sS, S.val(sS))], inc=sDG, inc_n=16)
                for grp0 in range(0, 9, 4):
                    nrow = min(4, 9 - grp0)
                    bk2, wfree2 = take_bank()
                    vtp = None
                    for rr in range(nrow):
                        vtp = S.emit("tensor", lambda e, b=bk2, g_=g, r_=grp0 + rr,
                                     c2=rr * 128: e.transpose(
                                     PSB[b][0:1, c2:c2 + g_],
                                     cf[0:g_, 12 + r_:13 + r_],
                                     identb[0:g_, 0:g_]),
                                     waits=([(sV, vcp)] +
                                            wfree2)
                                     if rr == 0 else [], inc=sT)
                    vev = S.emit("scalar", lambda e, b=bk2, go_=go, g_=g, W_=W,
                                 g0=grp0, nr=nrow: e.copy(
                                 cwT[0:1, g0 * W_: (g0 + nr) * W_].rearrange(
                                     "p (r w) -> p r w", r=nr)[:, :, go_:go_ + g_],
                                 PSB[b][0:1, 0:nr * 128].rearrange(
                                     "p (r w) -> p r w", r=nr)[:, :, 0:g_]),
                                 waits=[(sT, vtp)], inc=sS)
                    bank_free[bk2] = [(sS, vev)]
            ev_cw = (sS, S.val(sS))
            reps = []
            for row in range(9):
                o2 = 0
                while o2 < W:
                    w2 = min(512, W - o2)
                    bk, wfree = take_bank()
                    vrp = mm(PSB[bk][:, 0:w2], onesrow,
                             cwT[0:1, row * W + o2: row * W + o2 + w2],
                             start=True, stop=True,
                             waits=[ev_cw] + wfree)
                    ve = S.emit("scalar", lambda e, r_=row, o_=o2, w_=w2, b=bk:
                                e.copy(cwrep[:, r_ * W + o_: r_ * W + o_ + w_],
                                       PSB[b][:, 0:w_]),
                                waits=[(sT, vrp)], inc=sS)
                    bank_free[bk] = [(sS, ve)]
                    reps.append(ve)
                    o2 += w2
            cw_ready = (sS, reps[-1])
            if dbg_dump and DBG:
                S.emit("gpsimd", lambda e: e.dma_start(out=dbg_out["xln"], in_=xln),
                       waits=[ln_done], inc=sDG, inc_n=16)
                S.emit("gpsimd", lambda e: e.dma_start(out=dbg_out["cwt"], in_=cwT),
                       waits=[ev_cw], inc=sDG, inc_n=16)

            for ot in range(16):
                wb_, wr_, sl = load_wtile(w_sg[ot] if (cache_mode or NOCACHE) else None,
                                          None if (cache_mode or NOCACHE) else c_sg[ot], 8,
                                          c_sg[ot] if (cache_mode or NOCACHE) else None)
                for (mo, mw) in movs:
                    bg, fg = take_bank()
                    lg = None
                    for kt in range(8):
                        lg = mm(PSB[bg][:, 0:mw], wb_[:, kt * 128:(kt + 1) * 128],
                                xln[:, kt * W + mo: kt * W + mo + mw],
                                start=(kt == 0), stop=(kt == 7),
                                waits=([wr_, ln_done] + fg)
                                if kt == 0 else [])
                    vsl = S.emit("scalar", lambda e, b=bg, w_=mw, o_=ot * W + mo:
                                 e.activation(out=sact[:, o_:o_ + w_],
                                              in_=PSB[b][:, 0:w_], func=AF.Silu),
                                 waits=[(sT, lg)] +
                                       ([cw_ready] if ot >= 7 else []), inc=sS)
                    bank_free[bg] = [(sS, vsl)]
                free_wslot(sl)
            for ot in range(16):
                wb_, wr_, sl = load_wtile(w_su[ot] if (cache_mode or NOCACHE) else None,
                                          None if (cache_mode or NOCACHE) else c_su[ot], 8,
                                          c_su[ot] if (cache_mode or NOCACHE) else None)
                for (mo, mw) in movs:
                    bu, fu = take_bank()
                    lu = None
                    for kt in range(8):
                        lu = mm(PSB[bu][:, 0:mw], wb_[:, kt * 128:(kt + 1) * 128],
                                xln[:, kt * W + mo: kt * W + mo + mw],
                                start=(kt == 0), stop=(kt == 7),
                                waits=([wr_] + fu) if kt == 0 else [])
                    v1 = S.emit("vector", lambda e, b=bu, o_=ot * W + mo, w_=mw:
                                e.tensor_mul(sact[:, o_:o_ + w_],
                                             sact[:, o_:o_ + w_], PSB[b][:, 0:w_]),
                                waits=[(sT, lu), (sS, S.val(sS))], inc=sV)
                    if mw < 128:
                        S.emit("vector", lambda e: e.engine_nop(),
                               waits=[(sV, v1)], inc=sV)
                    S.emit("vector", lambda e, o_=ot * W + mo, mo_=mo, w_=mw:
                           e.tensor_mul(sact[:, o_:o_ + w_], sact[:, o_:o_ + w_],
                                        cwrep[:, 8 * W + mo_: 8 * W + mo_ + w_]),
                           waits=[cw_ready], inc=sV)
                    bank_free[bu] = [(sV, v1)]
                free_wslot(sl)

            for ex in range(8):
                for op in range(4):
                    wg_, wgr_, slg = load_wtile(
                        w_gu[ex][op] if (cache_mode or NOCACHE) else None,
                        None if (cache_mode or NOCACHE) else c_gu[ex][op], 8,
                        c_gu[ex][op] if (cache_mode or NOCACHE) else None)
                    wu_, wur_, slu = load_wtile(
                        w_gu[ex][op + 4] if (cache_mode or NOCACHE) else None,
                        None if (cache_mode or NOCACHE) else c_gu[ex][op + 4], 8,
                        c_gu[ex][op + 4] if (cache_mode or NOCACHE) else None)
                    for (mo, mw) in movs:
                        bg, fg = take_bank()
                        bu, fu = take_bank()
                        lg = lu = None
                        for kt in range(8):
                            lg = mm(PSB[bg][:, 0:mw],
                                    wg_[:, kt * 128:(kt + 1) * 128],
                                    xln[:, kt * W + mo: kt * W + mo + mw],
                                    start=(kt == 0), stop=(kt == 7),
                                    waits=([wgr_, ln_done] + fg)
                                    if kt == 0 else [])
                        for kt in range(8):
                            lu = mm(PSB[bu][:, 0:mw],
                                    wu_[:, kt * 128:(kt + 1) * 128],
                                    xln[:, kt * W + mo: kt * W + mo + mw],
                                    start=(kt == 0), stop=(kt == 7),
                                    waits=([wur_] + fu)
                                    if kt == 0 else [])
                        vsl = S.emit("scalar", lambda e, b=bg, w_=mw: e.activation(
                            out=sgs[:, 0:w_], in_=PSB[b][:, 0:w_], func=AF.Silu),
                            waits=[(sT, lg), (sV, S.val(sV))], inc=sS)
                        slot = ex * 4 + op
                        v1 = S.emit("vector", lambda e, b=bu, s_=slot * W + mo, w_=mw:
                                    e.tensor_mul(acts[:, s_:s_ + w_], sgs[:, 0:w_],
                                                 PSB[b][:, 0:w_]),
                                    waits=[(sS, vsl), (sT, lu)], inc=sV)
                        if mw < 128:
                            S.emit("vector", lambda e: e.engine_nop(),
                                   waits=[(sV, v1)], inc=sV)
                        S.emit("vector", lambda e, s_=slot * W + mo, mo_=mo, w_=mw,
                               ex_=ex: e.tensor_mul(
                               acts[:, s_:s_ + w_], acts[:, s_:s_ + w_],
                               cwrep[:, ex_ * W + mo_: ex_ * W + mo_ + w_]),
                               waits=[cw_ready], inc=sV)
                        bank_free[bg] = [(sV, v1)]
                        bank_free[bu] = [(sV, v1)]
                    free_wslot(slg)
                    free_wslot(slu)
            acts_ready = (sV, S.val(sV))
            if dbg_dump and DBG:
                S.emit("gpsimd", lambda e: e.dma_start(out=dbg_out["sact"], in_=sact),
                       waits=[acts_ready], inc=sDG, inc_n=16)
                S.emit("gpsimd", lambda e: e.dma_start(out=dbg_out["acts"], in_=acts),
                       waits=[acts_ready], inc=sDG, inc_n=16)

            for ot in range(8):
                wb_, wr_, sl = load_wtile(w_sd[ot] if (cache_mode or NOCACHE) else None,
                                          None if (cache_mode or NOCACHE) else c_sd[ot], 16,
                                          c_sd[ot] if (cache_mode or NOCACHE) else None)
                banks = []
                for (mo, mw) in movs:
                    bo, fo = take_bank()
                    banks.append((bo, mo, mw))
                    for kt in range(16):
                        mm(PSB[bo][:, 0:mw], wb_[:, kt * 128:(kt + 1) * 128],
                           sact[:, kt * W + mo: kt * W + mo + mw],
                           start=(kt == 0), stop=False,
                           waits=([wr_, acts_ready] + fo)
                           if kt == 0 else [])
                free_wslot(sl)
                lastd = None
                for ex in range(8):
                    wb2, wr2, sl2 = load_wtile(
                        w_dn[ex][ot] if (cache_mode or NOCACHE) else None,
                        None if (cache_mode or NOCACHE) else c_dn[ex][ot], 4,
                        c_dn[ex][ot] if (cache_mode or NOCACHE) else None)
                    for (bo, mo, mw) in banks:
                        for kt in range(4):
                            slot = ex * 4 + kt
                            lastd = mm(PSB[bo][:, 0:mw],
                                       wb2[:, kt * 128:(kt + 1) * 128],
                                       acts[:, slot * W + mo: slot * W + mo + mw],
                                       start=False, stop=(ex == 7 and kt == 3),
                                       waits=[wr2] if kt == 0 else [])
                    free_wslot(sl2)
                for (bo, mo, mw) in banks:
                    if final_chunk is None:
                        vep = S.emit("vector", lambda e, b=bo, o_=ot * W + mo, w_=mw:
                                     e.scalar_tensor_tensor(
                                     x_c[:, o_:o_ + w_], PSB[b][:, 0:w_], 0.5,
                                     x_c[:, o_:o_ + w_], ALU.mult, ALU.add),
                                     waits=[(sT, lastd)], inc=sV)
                        bank_free[bo] = [(sV, vep)]
                    else:
                        vep = S.emit("vector", lambda e, b=bo, o_=ot * W + mo,
                                     mo_=mo, w_=mw: e.scalar_tensor_tensor(
                                     of32[:, mo_:mo_ + w_], PSB[b][:, 0:w_], 0.5,
                                     x_c[:, o_:o_ + w_], ALU.mult, ALU.add),
                                     waits=[(sT, lastd), (sFin, S.val(sFin))],
                                     inc=sV)
                        bank_free[bo] = [(sV, vep)]
                        S.emit("gpsimd", lambda e, o_=ot, mo_=mo, w_=mw, c_=final_chunk:
                               e.dma_start(out=out_t[o_ * 128:(o_ + 1) * 128,
                                                     c_ * TCA + mo_: c_ * TCA + mo_ + w_],
                                           in_=of32[:, mo_:mo_ + w_]),
                               waits=[(sV, vep)], inc=sFin, inc_n=16)

        ag_d, ag_u, ag_b = [], [], []
        for c in range(NCA):
            cache_bar[0] = [(sDS, S.val(sDS))]
            K0, W = c * TCA, WA
            movs = [(0, 512), (512, 512), (1024, 8)]
            tgroups = [(i * 128, 128) for i in range(8)] + [(1024, 8)]
            own0 = HP
            xz_off = 0
            xz_movs = movs
            qw = W // 4
            for q in range(4):
                vld = S.emit("sync", lambda e, K_=K0 + q * qw, w_=qw: e.dma_start(
                    out=xf32[:, 0:8 * w_].rearrange("p (t w) -> p t w", t=8),
                    in_=x_in[:, K_:K_ + w_].rearrange("(t p) w -> p t w", p=128)),
                    waits=(barrier_waits() if q == 0 else [(sV, S.val(sV))]),
                    inc=sDW, inc_n=16)
                S.emit("vector", lambda e, o_=q * qw, w_=qw, W_=W: e.tensor_copy(
                    x_c[:, 0:8 * W_].rearrange("p (t w) -> p t w", t=8)[:, :, o_:o_ + w_],
                    xf32[:, 0:8 * w_].rearrange("p (t w) -> p t w", t=8)),
                    waits=[(sDW, vld)] + (prep_done if c == 0 and q == 0 else []),
                    inc=sV)
            emit_ffn(W, movs, tgroups, cache_mode=(c == 0), final_chunk=None, dbg_dump=(c == 0))
            S.emit("gpsimd", lambda e, c_=c, W_=W, o_=own0: e.dma_start(
                out=x1_sp[:, c_ * TCA:(c_ + 1) * TCA].rearrange(
                    "(t p) w -> p t w", p=128),
                in_=x_c[:, 0:8 * W_].rearrange("p (t w) -> p t w", t=8)[:, :, o_:o_ + TCA]),
                waits=[(sV, S.val(sV))], inc=sDG, inc_n=16)
            ln2 = emit_ln(W, movs)
            xz_gate = (sT, S.val(sT))
            for ot in range(16):
                wb_, wr_, sl = load_wtile(w_in[ot] if (c == 0 or NOCACHE) else None,
                                          None if (c == 0 or NOCACHE) else c_in[ot], 8,
                                          c_in[ot] if (c == 0 or NOCACHE) else None)
                for (mo, mw) in xz_movs:
                    bz, fz = take_bank()
                    lz = None
                    for kt in range(8):
                        lz = mm(PSB[bz][:, 0:mw], wb_[:, kt * 128:(kt + 1) * 128],
                                xln[:, kt * W + mo: kt * W + mo + mw],
                                start=(kt == 0), stop=(kt == 7),
                                waits=([wr_, ln2, xz_gate] + fz)
                                if kt == 0 else [])
                    vz = S.emit("scalar", lambda e, b=bz, d_=ot * WA + xz_off + mo,
                                w_=mw, o_=ot: e.activation(
                                out=xz[:, d_:d_ + w_], in_=PSB[b][:, 0:w_],
                                func=AF.Identity, bias=biasq[:, o_:o_ + 1]),
                                waits=[(sT, lz)], inc=sS)
                    bank_free[bz] = [(sS, vz)]
                free_wslot(sl)
            xz_done = (sS, S.val(sS))
            if c == 0:
                vmk = None
                for ot in range(16):
                    vmk = S.emit("vector", lambda e, o_=ot: e.tensor_mul(
                        xz[:, o_ * WA: o_ * WA + HP], xz[:, o_ * WA: o_ * WA + HP],
                        maskb), waits=[xz_done] if ot == 0 else [], inc=sV)
                xz_done = (sV, vmk)

            for ot in range(16):
                wb_, wr_, sl = load_wtile(w_gt[ot] if (c == 0 or NOCACHE) else None,
                                          None if (c == 0 or NOCACHE) else c_gt[ot], 8,
                                          c_gt[ot] if (c == 0 or NOCACHE) else None)
                for gi in range(2):
                    mo = own0 + gi * 512
                    bz, fz = take_bank()
                    lz = None
                    for kt in range(8):
                        lz = mm(PSB[bz][:, 0:512], wb_[:, kt * 128:(kt + 1) * 128],
                                xln[:, kt * W + mo: kt * W + mo + 512],
                                start=(kt == 0), stop=(kt == 7),
                                waits=([wr_] + fz) if kt == 0 else [])
                    vz = S.emit("scalar", lambda e, b=bz, o_=ot: e.activation(
                        out=sgs[:, 0:512], in_=PSB[b][:, 0:512], func=AF.Silu,
                        bias=biasq[:, 16 + o_:16 + o_ + 1]),
                        waits=[(sT, lz), (sDS, S.val(sDS))], inc=sS)
                    bank_free[bz] = [(sS, vz)]
                    S.emit("scalar", lambda e, c_=c, o_=ot, g_=gi: e.dma_start(
                        out=g_sp[o_ * 128:(o_ + 1) * 128,
                                 c_ * TCA + g_ * 512: c_ * TCA + g_ * 512 + 512],
                        in_=sgs[:, 0:512]), waits=[(sS, vz)], inc=sDS, inc_n=16)
                free_wslot(sl)

            for ot in range(16):
                wcv_, wcvr_, slcv = load_wtile(w_cv[ot] if (c == 0 or NOCACHE) else None,
                                               None if (c == 0 or NOCACHE) else c_cv[ot], 4,
                                               c_cv[ot] if (c == 0 or NOCACHE) else None)
                for gi in range(2):
                    mo = HP + gi * 512
                    bz, fz = take_bank()
                    lz = None
                    for k in range(4):
                        sft = 3 - k
                        lz = mm(PSB[bz][:, 0:512],
                                wcv_[:, k * 128:(k + 1) * 128],
                                xz[:, ot * WA + mo - sft: ot * WA + mo - sft + 512],
                                start=(k == 0), stop=(k == 3),
                                waits=([xz_done, wcvr_] + fz)
                                if k == 0 else [])
                    va = S.emit("scalar", lambda e, b=bz, o_=ot, g_=gi: e.activation(
                        out=actm[:, o_ * TCA + g_ * 512: o_ * TCA + g_ * 512 + 512],
                        in_=PSB[b][:, 0:512], func=AF.Silu,
                        bias=biasq[:, 48 + o_:48 + o_ + 1]),
                        waits=[(sT, lz)], inc=sS)
                    bank_free[bz] = [(sS, va)]
                    S.emit("gpsimd", lambda e, c_=c, o_=ot, g_=gi: e.dma_start(
                        out=act_sp[o_ * 128:(o_ + 1) * 128,
                                   c_ * TCA + g_ * 512: c_ * TCA + g_ * 512 + 512],
                        in_=actm[:, o_ * TCA + g_ * 512: o_ * TCA + g_ * 512 + 512]),
                        waits=[(sS, va)], inc=sDG, inc_n=16)
                free_wslot(slcv)
            act_done = (sS, S.val(sS))

            for ot in range(16):
                wb_, wr_, sl = load_wtile(w_dl[ot] if (c == 0 or NOCACHE) else None,
                                          None if (c == 0 or NOCACHE) else c_dl[ot], 16,
                                          c_dl[ot] if (c == 0 or NOCACHE) else None)
                for gi in range(2):
                    mo = gi * 512
                    bz, fz = take_bank()
                    lz = None
                    for kt in range(16):
                        lz = mm(PSB[bz][:, 0:512], wb_[:, kt * 128:(kt + 1) * 128],
                                actm[:, kt * TCA + mo: kt * TCA + mo + 512],
                                start=(kt == 0), stop=(kt == 15),
                                waits=([wr_, act_done] + fz)
                                if kt == 0 else [])
                    v1 = S.emit("scalar", lambda e, b=bz, o_=ot: e.activation(
                        out=e_sc, in_=PSB[b][:, 0:512], func=AF.Exp,
                        bias=biasq[:, 32 + o_:32 + o_ + 1]),
                        waits=[(sT, lz), (sDG, S.val(sDG)), (sV, S.val(sV))],
                        inc=sS)
                    v2 = S.emit("scalar", lambda e, m_=mo: e.activation(
                        out=dlt[:, m_:m_ + 512], in_=e_sc, func=AF.Ln, bias=1.0),
                        inc=sS)
                    bank_free[bz] = [(sS, v2)]
                S.emit("gpsimd", lambda e, c_=c, o_=ot: e.dma_start(
                    out=cc_di[c_][o_ * 128:(o_ + 1) * 128, :], in_=dlt),
                    waits=[(sS, S.val(sS))], inc=sDG, inc_n=16)
                vu = S.emit("vector", lambda e, o_=ot: e.tensor_mul(
                    ut, dlt, actm[:, o_ * TCA:(o_ + 1) * TCA]),
                    waits=[(sS, S.val(sS)), (sDG, S.val(sDG))], inc=sV)
                S.emit("gpsimd", lambda e, c_=c, o_=ot: e.dma_start(
                    out=cc_ui[c_][o_ * 128:(o_ + 1) * 128, :], in_=ut),
                    waits=[(sV, vu)], inc=sDG, inc_n=16)
                free_wslot(sl)

            for gi in range(2):
                mo = gi * 512
                bz, fz = take_bank()
                lz = None
                for kt in range(16):
                    lz = mm(PSB[bz][0:32, 0:512], wbc_b[:, kt * 32:(kt + 1) * 32],
                            actm[:, kt * TCA + mo: kt * TCA + mo + 512],
                            start=(kt == 0), stop=(kt == 15),
                            waits=([act_done, (sV, vbc0)] +
                                   fz) if kt == 0 else [])
                vz = S.emit("scalar", lambda e, b=bz, m_=mo: e.copy(
                    bct[:, m_:m_ + 512], PSB[b][0:32, 0:512]),
                    waits=[(sT, lz), (sDG, S.val(sDG))], inc=sS)
                bank_free[bz] = [(sS, vz)]
            S.emit("gpsimd", lambda e, c_=c: e.dma_start(
                out=cc_bi[c_], in_=bct), waits=[(sS, S.val(sS))], inc=sDG, inc_n=16)

            ag_d.append(S.emit("gpsimd", lambda e, c_=c: e.collective_compute(
                "AllGather", ALU.bypass, ins=[cc_di[c_]], outs=[cc_do[c_]],
                replica_groups=RG), waits=[(sDG, S.val(sDG))], inc=sG))
            ag_u.append(S.emit("gpsimd", lambda e, c_=c: e.collective_compute(
                "AllGather", ALU.bypass, ins=[cc_ui[c_]], outs=[cc_uo[c_]],
                replica_groups=RG), inc=sG))
            ag_b.append(S.emit("gpsimd", lambda e, c_=c: e.collective_compute(
                "AllGather", ALU.bypass, ins=[cc_bi[c_]], outs=[cc_bo[c_]],
                replica_groups=RG), inc=sG))

        rs_done = [None, None]
        dl_free = [None, None]
        u_free = [None, None]
        a_free = [None, None]
        y_free = None
        rep_guard = [None, None]
        mul_last = None
        free_ph = None
        free_p = None
        for s_ in range(NCB):
            j = s_ // 2
            lc = 2 * (s_ % 2)
            half = s_ % 2
            ag_wait = [(sG, ag_b[lc + 1])]
            if s_ == 0:
                ag_wait += barrier_waits()
            vb1 = S.emit("sync", lambda e, l_=lc, j_=j: e.dma_start(
                out=bc_g[:, 0:TCA], in_=cc_bo[l_][j_]),
                waits=ag_wait + [(sT, S.val(sT))], inc=sBC, inc_n=16)
            vb2 = S.emit("sync", lambda e, l_=lc, j_=j: e.dma_start(
                out=bc_g[:, TCA:TCB], in_=cc_bo[l_ + 1][j_]), inc=sBC, inc_n=16)
            rep_last = None
            first_rep = True
            for nn in range(4):
                for (dst, row) in ((brep, nn), (crep, 4 + nn)):
                    for sp in range(4):
                        bk, fk = take_bank()
                        vrp = mm(PSB[bk][:, 0:512],
                                 selb[:, row * 128:(row + 1) * 128],
                                 bc_g[:, sp * 512:(sp + 1) * 512],
                                 start=True, stop=True,
                                 waits=[(sBC, vb2)] + fk)
                        gw = ([(sV, S.val(sV)), (sP, S.val(sP))]
                              if first_rep else [])
                        first_rep = False
                        rep_last = S.emit("scalar", lambda e, d=dst, n_=nn, s2=sp,
                                          b=bk: e.copy(
                                          d[:, n_ * TCB + s2 * 512:
                                            n_ * TCB + s2 * 512 + 512],
                                          PSB[b][:, 0:512]),
                                          waits=[(sT, vrp)] + gw, inc=sS)
                        bank_free[bk] = [(sS, rep_last)]
            reps_ok = (sS, rep_last)
            if DBG and s_ == 0:
                S.emit("gpsimd", lambda e: e.dma_start(out=dbg_out["brep"], in_=brepb[0]),
                       waits=[reps_ok], inc=sDG, inc_n=16)
                S.emit("gpsimd", lambda e: e.dma_start(out=dbg_out["crep"], in_=crepb[0]),
                       waits=[reps_ok], inc=sDG, inc_n=16)

            for dt in range(16):
                sl = dt % 2
                sB = sB0 if sl == 0 else sB1
                vd1 = S.emit("sync", lambda e, l_=lc, j_=j, d_=dt, s2=sl: e.dma_start(
                    out=dl_g[:, s2 * TCB: s2 * TCB + TCA],
                    in_=cc_do[l_][j_][d_ * 128:(d_ + 1) * 128, :]),
                    waits=[dl_free[sl], (sG, ag_d[lc + 1])], inc=sB, inc_n=16)
                S.emit("sync", lambda e, l_=lc, j_=j, d_=dt, s2=sl: e.dma_start(
                    out=dl_g[:, s2 * TCB + TCA: (s2 + 1) * TCB],
                    in_=cc_do[l_ + 1][j_][d_ * 128:(d_ + 1) * 128, :]),
                    inc=sB, inc_n=16)
                S.emit("sync", lambda e, l_=lc, j_=j, d_=dt, s2=sl: e.dma_start(
                    out=u_g[:, s2 * TCB: s2 * TCB + TCA],
                    in_=cc_uo[l_][j_][d_ * 128:(d_ + 1) * 128, :]),
                    waits=[u_free[sl], (sG, ag_u[lc + 1])], inc=sB, inc_n=16)
                vu2 = S.emit("sync", lambda e, l_=lc, j_=j, d_=dt, s2=sl: e.dma_start(
                    out=u_g[:, s2 * TCB + TCA: (s2 + 1) * TCB],
                    in_=cc_uo[l_ + 1][j_][d_ * 128:(d_ + 1) * 128, :]),
                    inc=sB, inc_n=16)
                dma_ok = (sB, vu2)
                last_a = None
                last_b = None
                for nn in range(4):
                    idx = dt * 4 + nn
                    asl = idx % 2
                    va = S.emit("scalar", lambda e, s2=sl, a2=asl, i_=idx: e.activation(
                        out=a_t[:, a2 * TCB:(a2 + 1) * TCB],
                        in_=dl_g[:, s2 * TCB:(s2 + 1) * TCB], func=AF.Exp,
                        scale=a_sc[:, i_:i_ + 1]),
                        waits=[dma_ok, a_free[asl]], inc=sS)
                    last_a = va
                    vbm = S.emit("vector", lambda e, s2=sl, a2=asl, n_=nn,
                                 h2=half: e.tensor_mul(
                        b_t[:, a2 * TCB:(a2 + 1) * TCB],
                        u_g[:, s2 * TCB:(s2 + 1) * TCB],
                        brep[:, n_ * TCB:(n_ + 1) * TCB]),
                        waits=[dma_ok, reps_ok], inc=sV)
                    last_b = vbm
                    init_ap = 0.0 if s_ == 0 else h_last[:, idx:idx + 1]
                    vsc = S.emit("vector", lambda e, a2=asl, ini=init_ap:
                                 e.tensor_tensor_scan(
                                 out=h_t, data0=a_t[:, a2 * TCB:(a2 + 1) * TCB],
                                 data1=b_t[:, a2 * TCB:(a2 + 1) * TCB],
                                 initial=ini,
                                 op0=ALU.mult, op1=ALU.add),
                                 waits=[(sS, va)], inc=sV)
                    a_free[asl] = (sV, vsc)
                    if DBG and s_ == 0 and dt == 0 and nn == 0:
                        S.emit("gpsimd", lambda e, a2=asl: e.dma_start(
                            out=dbg_out["a0f"], in_=a_t[:, a2 * TCB:(a2 + 1) * TCB]),
                            waits=[(sV, vsc)], inc=sDG, inc_n=16)
                        S.emit("gpsimd", lambda e, a2=asl: e.dma_start(
                            out=dbg_out["b0"], in_=b_t[:, a2 * TCB:(a2 + 1) * TCB]),
                            waits=[(sV, vsc)], inc=sDG, inc_n=16)
                        vdmp = S.emit("gpsimd", lambda e: e.dma_start(
                            out=dbg_out["h0"], in_=h_t), waits=[(sV, vsc)],
                            inc=sDG, inc_n=16)
                        S.emit("vector", lambda e: e.engine_nop(),
                               waits=[(sDG, vdmp)], inc=sV)
                    S.emit("vector", lambda e: e.engine_nop(),
                           waits=[(sV, vsc)], inc=sV)
                    S.emit("vector", lambda e, i_=idx: e.tensor_copy(
                        h_last[:, i_:i_ + 1], h_t[:, TCB - 1:TCB]), inc=sV)
                    dst = ph_t if nn % 2 == 0 else p_t
                    S.emit("vector", lambda e, d=dst, n_=nn: e.tensor_mul(
                        d, h_t, crep[:, n_ * TCB:(n_ + 1) * TCB]), inc=sV)
                    if nn == 1:
                        S.emit("vector", lambda e: e.tensor_add(s1_t, ph_t, p_t),
                               inc=sV)
                    elif nn == 3:
                        S.emit("vector", lambda e: e.tensor_add(p_t, ph_t, p_t),
                               inc=sV)
                        vy = S.emit("vector", lambda e: e.tensor_add(y_t, s1_t, p_t),
                                    waits=[y_free], inc=sV)
                if DBG and s_ == 0 and dt == 0:
                    vdy = S.emit("gpsimd", lambda e: e.dma_start(
                        out=dbg_out["y0"], in_=y_t), waits=[(sV, vy)],
                        inc=sDG, inc_n=16)
                    S.emit("vector", lambda e: e.engine_nop(),
                           waits=[(sDG, vdy)], inc=sV)
                dl_free[sl] = (sS, last_a)
                u_free[sl] = (sV, last_b)
                y_free = (sDS, S.emit("scalar",
                          lambda e, h_=half, j_=j, d_=dt: e.dma_start(
                          out=cc_yp[h_][d_ // 4][j_, (d_ % 4) * 128:
                                                  (d_ % 4 + 1) * 128, :],
                          in_=y_t),
                          waits=[(sV, vy)], inc=sDS, inc_n=16))
                if s_ == 7 and dt % 4 == 3:
                    qs = ([(0, q) for q in range(4)] + [(1, 0)]) if dt == 3                         else [(1, dt // 4)]
                    for (hh, qq) in qs:
                        rs_done[hh] = S.emit("gpsimd",
                            lambda e, h2=hh, q2=qq: e.collective_compute(
                            "ReduceScatter", ALU.add, ins=[cc_yp[h2][q2]],
                            outs=[cc_yr[h2][q2]], replica_groups=RG),
                            waits=[(sDG, S.val(sDG)), (sDS, S.val(sDS))], inc=sG)
            rep_guard[half] = (sV, S.val(sV))

        for c in range(NCC):
            cache_bar[0] = [(sDS, S.val(sDS))]
            h_ = c // 2
            off = (c % 2) * TCC
            S.emit("sync", lambda e, c_=c: e.dma_start(
                out=actc.rearrange("p (t w) -> p t w", t=16),
                in_=act_sp[:, c_ * TCC:(c_ + 1) * TCC].rearrange(
                    "(t p) w -> p t w", p=128)),
                waits=barrier_waits(), inc=sDW, inc_n=16)
            S.emit("sync", lambda e, c_=c: e.dma_start(
                out=gc.rearrange("p (t w) -> p t w", t=16),
                in_=g_sp[:, c_ * TCC:(c_ + 1) * TCC].rearrange(
                    "(t p) w -> p t w", p=128)), inc=sDW, inc_n=16)
            vy_ = None
            for q in range(4):
                vy_ = S.emit("sync", lambda e, h2=h_, o_=off, q_=q: e.dma_start(
                    out=yc.rearrange("p (t w) -> p t w", t=16)[:, q_ * 4:
                                                              (q_ + 1) * 4, :],
                    in_=cc_yr[h2][q_].rearrange(
                        "(t p) w -> p t w", p=128)[:, :, o_:o_ + TCC]),
                    waits=[(sG, rs_done[h_])] if q == 0 else [],
                    inc=sDW, inc_n=16)
            vx_ = S.emit("sync", lambda e, c_=c: e.dma_start(
                out=x1c.rearrange("p (t w) -> p t w", t=8),
                in_=x1_sp[:, c_ * TCC:(c_ + 1) * TCC].rearrange(
                    "(t p) w -> p t w", p=128)), inc=sDW, inc_n=16)
            dok = (sDW, vx_)
            for dt in range(16):
                S.emit("vector", lambda e, d_=dt: e.scalar_tensor_tensor(
                    yc[:, d_ * TCC:(d_ + 1) * TCC], actc[:, d_ * TCC:(d_ + 1) * TCC],
                    dskq[:, d_:d_ + 1], yc[:, d_ * TCC:(d_ + 1) * TCC],
                    ALU.mult, ALU.add), waits=[dok] if dt == 0 else [], inc=sV)
                S.emit("vector", lambda e, d_=dt: e.tensor_mul(
                    ycb[:, d_ * TCC:(d_ + 1) * TCC], yc[:, d_ * TCC:(d_ + 1) * TCC],
                    gc[:, d_ * TCC:(d_ + 1) * TCC]), inc=sV)
            gat = (sV, S.val(sV))
            for ot in range(8):
                wb_, wr_, sl = load_wtile(w_ot[ot] if (c == 0 or NOCACHE) else None,
                                          None if (c == 0 or NOCACHE) else c_ot[ot], 16,
                                          c_ot[ot] if (c == 0 or NOCACHE) else None)
                for gi in range(2):
                    mo = gi * 512
                    bz, fz = take_bank()
                    lz = None
                    for kt in range(16):
                        lz = mm(PSB[bz][:, 0:512], wb_[:, kt * 128:(kt + 1) * 128],
                                ycb[:, kt * TCC + mo: kt * TCC + mo + 512],
                                start=(kt == 0), stop=(kt == 15),
                                waits=([wr_, gat] + fz)
                                if kt == 0 else [])
                    vep = S.emit("vector", lambda e, b=bz, o_=ot, m_=mo:
                                 e.scalar_tensor_tensor(
                                 x2st[:, m_:m_ + 512], PSB[b][:, 0:512],
                                 biasq[:, 64 + o_:64 + o_ + 1],
                                 x1c[:, o_ * TCC + m_: o_ * TCC + m_ + 512],
                                 ALU.add, ALU.add),
                                 waits=[(sT, lz), (sDS, S.val(sDS))], inc=sV)
                    bank_free[bz] = [(sV, vep)]
                    S.emit("scalar", lambda e, c_=c, o_=ot, m_=mo: e.dma_start(
                        out=x2_sp[o_ * 128:(o_ + 1) * 128,
                                  c_ * TCC + m_: c_ * TCC + m_ + 512],
                        in_=x2st[:, m_:m_ + 512]),
                        waits=[(sV, vep)], inc=sDS, inc_n=16)
                free_wslot(sl)
        x2_all = (sDS, S.val(sDS))

        for c in range(NCA):
            cache_bar[0] = [(sDS, S.val(sDS))]
            W = TCA
            movs = [(0, 512), (512, 512)]
            tgroups = [(i * 128, 128) for i in range(8)]
            vld = S.emit("sync", lambda e, c_=c: e.dma_start(
                out=x_c[:, 0:8 * W].rearrange("p (t w) -> p t w", t=8),
                in_=x2_sp[:, c_ * TCA:(c_ + 1) * TCA].rearrange(
                    "(t p) w -> p t w", p=128)),
                waits=[x2_all] + barrier_waits(), inc=sDW, inc_n=16)
            S.emit("vector", lambda e: e.engine_nop(), waits=[(sDW, vld)], inc=sV)
            emit_ffn(W, movs, tgroups, cache_mode=False, final_chunk=c)

        if DBG:
            S.emit("gpsimd", lambda e: e.dma_start(out=dbg_out["asc"], in_=a_sc),
                   waits=[(sFin, S.val(sFin))], inc=sDG, inc_n=16)
            S.emit("gpsimd", lambda e: e.dma_start(out=dbg_out["hl"], in_=h_last),
                   waits=[(sFin, S.val(sFin))], inc=sDG, inc_n=16)
            pairs = [("do0", cc_do[0].rearrange("s d w -> (s d) w")),
                     ("uo0", cc_uo[0].rearrange("s d w -> (s d) w")),
                     ("x1", x1_sp), ("act", act_sp), ("gact", g_sp),
                     ("delta", cc_di[0]), ("u", cc_ui[0]), ("bc", cc_bi[0]),
                     ("x2", x2_sp)]
            for nm, src in pairs:
                S.emit("gpsimd", lambda e, d=dbg_out[nm], s2=src: e.dma_start(
                    out=d, in_=s2), waits=[(sFin, S.val(sFin))], inc=sDG, inc_n=16)

        S.emit("gpsimd", None, waits=[(sFin, S.val(sFin)), (sDG, S.val(sDG)),
                                      (sDS, S.val(sDS)), (sDW, S.val(sDW))])

        @blk.sync
        def _(e):
            S.replay("sync", e)

        @blk.vector
        def _(e):
            S.replay("vector", e)

        @blk.scalar
        def _(e):
            S.replay("scalar", e)

        @blk.tensor
        def _(e):
            S.replay("tensor", e)

        @blk.gpsimd
        def _(e):
            S.replay("gpsimd", e)

    return nc


def _tile_w(w):
    K, O = w.shape
    return np.ascontiguousarray(
        w.reshape(K // 128, 128, O // 128, 128).transpose(2, 1, 0, 3)
        .reshape(O // 128, 128, (K // 128) * 128))


def _bias_cols(b, n):
    return np.ascontiguousarray(b.reshape(n, 128).T)


_NC_CACHE = {}


def kernel(**inp):
    x = inp["L0_attn_x"]
    B, L, _ = x.shape
    if "prog" not in _NC_CACHE:
        _NC_CACHE["prog"] = build_program()
    nc = _NC_CACHE["prog"]

    base = {
        "w_sg": _tile_w(np.ascontiguousarray(inp["L0_ffn_shared_gate_w"].T)),
        "w_su": _tile_w(np.ascontiguousarray(inp["L0_ffn_shared_up_w"].T)),
        "w_sd": _tile_w(np.ascontiguousarray(inp["L0_ffn_shared_down_w"].T)),
        "w_gu": np.stack([_tile_w(np.ascontiguousarray(
            inp["L0_ffn_gate_up_proj"][e].T)) for e in range(8)]),
        "w_dn": np.stack([_tile_w(np.ascontiguousarray(
            inp["L0_ffn_down_proj"][e].T)) for e in range(8)]),
        "w_r9": np.ascontiguousarray(
            np.concatenate([inp["L0_ffn_router_weight"].T,
                            inp["L0_ffn_shared_expert_gate_w"].T],
                           axis=1).reshape(8, 128, 9)),
        "w_in": _tile_w(inp["L0_attn_W_in"]),
        "w_gt": _tile_w(inp["L0_attn_W_gate"]),
        "w_dl": _tile_w(inp["L0_attn_W_delta"]),
        "w_bc": np.ascontiguousarray(
            np.concatenate([inp["L0_attn_W_B"], inp["L0_attn_W_C"]],
                           axis=1).reshape(16, 128, 32)),
        "w_ot": _tile_w(inp["L0_attn_W_out"]),
        "b_in": _bias_cols(inp["L0_attn_b_in"], 16),
        "b_gt": _bias_cols(inp["L0_attn_b_gate"], 16),
        "b_dl": _bias_cols(inp["L0_attn_b_delta"], 16),
        "b_cv": _bias_cols(inp["L0_attn_conv_b"], 16),
        "b_ot": _bias_cols(inp["L0_attn_b_out"], 8),
        "d_skip": _bias_cols(inp["L0_attn_D_skip"], 16),
        "ident": np.eye(128, dtype=np.float32),
    }
    cw = inp["L0_attn_conv_w"][:, 0, :]
    dg = np.zeros((16, 4, 128, 128), np.float32)
    ii = np.arange(128)
    for dt in range(16):
        for k in range(4):
            dg[dt, k, ii, ii] = cw[dt * 128:(dt + 1) * 128, k]
    base["w_cv"] = np.ascontiguousarray(
        dg.transpose(0, 2, 1, 3).reshape(16, 128, 512))
    alog = inp["L0_attn_A_log"]

    in_maps = []
    for core in range(8):
        g, r = core // 4, core % 4
        m = dict(base)
        xt = np.zeros((HP + TL, D), np.float32)
        lo = r * TL
        xt[HP:] = x[g, lo:lo + TL]
        if r > 0:
            xt[HP - 3:HP] = x[g, lo - 3:lo]
        m["x_fm"] = np.ascontiguousarray(xt.T)
        msk = np.ones((128, HP), np.float32)
        if r == 0:
            msk[:] = 0.0
        m["mask_h"] = msk
        a_sl = -np.exp(alog[:, 4 * r:4 * r + 4])
        m["a_own"] = np.ascontiguousarray(
            a_sl.reshape(16, 128, 4).transpose(1, 0, 2).reshape(128, 64))
        sel = np.zeros((32, 8, 128), np.float32)
        for nn in range(4):
            sel[4 * r + nn, nn, :] = 1.0
            sel[16 + 4 * r + nn, 4 + nn, :] = 1.0
        m["selbc"] = sel.reshape(32, 8 * 128)
        in_maps.append(m)

    res = run_bass_kernel_spmd(nc, in_maps, core_ids=list(range(8)))
    out = np.empty((B, L, D), np.float32)
    for core in range(8):
        g, r = core // 4, core % 4
        out[g, r * TL:(r + 1) * TL] = res.results[core]["out_fm"].T
    if DBG:
        kernel.dbg = [dict(res.results[c]) for c in range(8)]
    return out
